# version 10
# baseline (speedup 1.0000x reference)
import contextlib
import os
import numpy as np
import ml_dtypes
import concourse.bass as bass
import concourse.mybir as mybir
from concourse.bass_utils import run_bass_kernel_spmd

F32 = mybir.dt.float32
BF16 = mybir.dt.bfloat16
AF = mybir.ActivationFunctionType
ALU = mybir.AluOpType
AXX = mybir.AxisListType.X

B_, N_, D = 2, 8192, 1024
H, FD, HD = 16, 16, 64
T = 2048
PRE = 6144
NCH = PRE // 128
DK = D // 128
NPAIR = 120
RW = 49
E_UNITS = 18
DFF = 512
LN_EPS = 1e-5
ATT_EPS = 1e-12
GATE_EPS = 1e-10
NEG_BIG = -1e30


class Sem:
    def __init__(self, stack, nc, name):
        self.h = stack.enter_context(nc.semaphore(name))
        self.n = 0

    def inc(self, inst, k=1):
        assert inst is not None
        inst.then_inc(self.h, k)
        self.n += k


def build_program():
    nc = bass.Bass("TRN2", target_bir_lowering=False)

    def _w(eng, sem, n=None):
        v = sem.n if n is None else n
        assert v <= sem.n, f"future wait {v} > {sem.n}"
        if v > 0:
            eng.wait_ge(sem.h, v)

    xg_d = nc.dram_tensor("xgt", [128, 17 * 4096], F32, kind="ExternalInput")
    xo_d = nc.dram_tensor("xot", [128, DK * T], F32, kind="ExternalInput")
    mask_d = nc.dram_tensor("mask", [128, NCH], F32, kind="ExternalInput")
    scol_d = nc.dram_tensor("scol", [128, 1], F32, kind="ExternalInput")
    wq_d = nc.dram_tensor("wqT", [128, DK * 256], F32, kind="ExternalInput")
    wk_d = nc.dram_tensor("wkT", [128, DK * 256], F32, kind="ExternalInput")
    wv_d = nc.dram_tensor("wvT", [128, DK * 1024], F32, kind="ExternalInput")
    wke_d = nc.dram_tensor("wkeT", [128, DK * 256], F32, kind="ExternalInput")
    wve_d = nc.dram_tensor("wveT", [128, DK * 1024], F32, kind="ExternalInput")
    wo_d = nc.dram_tensor("woT", [128, DK * 1024], F32, kind="ExternalInput")
    gw_d = nc.dram_tensor("gwTm", [128, DK * 16], F32, kind="ExternalInput")
    ebr_d = nc.dram_tensor("ebr", [1, 16], F32, kind="ExternalInput")
    wsrc_d = nc.dram_tensor("wsrc", [54 * 128, 4096], F32, kind="ExternalInput")
    b1c_d = nc.dram_tensor("b1c", [128, 72], F32, kind="ExternalInput")
    b3c_d = nc.dram_tensor("b3c", [128, 72], F32, kind="ExternalInput")
    b2_d = nc.dram_tensor("b2m", [17, D], F32, kind="ExternalInput")
    out_d = nc.dram_tensor("out", [D, T], F32, kind="ExternalOutput")
    wsc = nc.dram_tensor("wsc", [E_UNITS * 3 * 128, 4096], BF16, kind="Internal")

    st = contextlib.ExitStack()
    with st:
        def sb(name, shape, dt):
            return st.enter_context(nc.sbuf_tensor(name, shape, dt))

        x_res = sb("x_res", [128, DK * T], F32)
        ones_rw = sb("ones_rw", [1, 256], BF16)
        onesmat = sb("onesmat", [128, 128], BF16)
        ident = sb("ident", [128, 128], BF16)
        tril = sb("tril", [128, 128], BF16)
        onescol = sb("onescol", [128, 1], F32)
        onescol_bf = sb("onescol_bf", [128, 1], BF16)
        eb16 = sb("eb16", [128, 16], F32)
        scol = sb("scol_s", [128, 1], F32)
        mcol = sb("mcol", [128, NCH], F32)
        mcol_bf = sb("mcol_bf", [128, NCH], BF16)
        ebrow = sb("ebrow", [1, 16], BF16)
        cvals = sb("cvals", [128, 2], F32)
        wq_sb = sb("wq_sb", [128, DK * 256], BF16)
        wk_sb = sb("wk_sb", [128, DK * 256], BF16)
        wv_sb = sb("wv_sb", [128, DK * 1024], BF16)
        wo_sb = sb("wo_sb", [128, DK * 1024], BF16)
        gw_sb = sb("gw_sb", [128, DK * 16], BF16)
        xhx = sb("xhx", [128, DK * 512], BF16)
        mb_s = sb("mb_s", [128, 512], F32)
        var_s = sb("var_s", [128, 512], F32)
        wu1 = [sb(f"wu1_{i}", [128, 4096], BF16) for i in range(2)]
        wu3 = [sb(f"wu3_{i}", [128, 4096], BF16) for i in range(2)]
        wu2 = [sb(f"wu2_{i}", [128, 4096], BF16) for i in range(2)]
        ktok = sb("ktok", [128, 256], BF16)
        qq = sb("qq", [128, H * 32], BF16)
        kk = sb("kk", [128, H * 32], BF16)
        qr_tok = sb("qr_tok", [128, 8 * 128], BF16)
        kr_tok = sb("kr_tok", [128, 8 * 128], BF16)
        vp = sb("vp", [128, H * 65], BF16)
        qr_feat = sb("qr_feat", [128, H * 128], BF16)
        kr_feat = sb("kr_feat", [128, H * 128], BF16)
        sbp = sb("sbp", [128, H * 64], BF16)
        sbpd = sb("sbpd", [128, 16], BF16)
        sbr = sb("sbr", [128, H * 65], BF16)
        yp = sb("yp", [128, 8 * 128], BF16)
        ztmp = sb("ztmp", [1, 1024], F32)
        zr = sb("zr", [1, 512], BF16)
        ftmp = sb("ftmp", [128, 2 * 512], BF16)
        zb_sb = sb("zb_sb", [128, 512], BF16)
        xb_g = wu1[1]
        qp_tok_t, qp_tok_o = wu3[1], 0
        kp_tok_t, kp_tok_o = wu3[1], 1920
        qtok_t, qtok_o = wu3[1], 3840
        pt_t, pt_o = wu2[1], 0
        qp_feat_t, qp_feat_o = wu2[1], 2048
        gwt = sb("gwt", [32, T], BF16)
        glog = sb("glog", [128, 256], F32)
        gt0 = sb("gt0", [128, 256], F32)
        gt1 = sb("gt1", [128, 256], F32)
        gt2 = sb("gt2", [128, 256], F32)
        gm = sb("gm", [128, 64], F32)
        b1c = sb("b1c_s", [128, 72], F32)
        b3c = sb("b3c_s", [128, 72], F32)

        h_t = kr_feat
        sa_t = qr_tok
        tb_t = kr_tok
        gb_t, gb_o = ktok, 0
        b2_sb = yp
        sb2_sb_row = 32
        gchb_t, gchb_o = qq, 0

        def sel_ap(u):
            if u < 16:
                return qr_feat[0:32, u * 128:(u + 1) * 128]
            return sbr[0:32, (u - 16) * 128:(u - 15) * 128]

        ps_s = [st.enter_context(nc.psum_tensor(f"ps_s{i}", [128, 512], F32))
                for i in range(4)]
        ps_t0 = st.enter_context(nc.psum_tensor("ps_t0", [128, 512], F32))
        ps_t1 = st.enter_context(nc.psum_tensor("ps_t1", [128, 512], F32))
        ps_t2 = st.enter_context(nc.psum_tensor("ps_t2", [128, 512], F32))
        ps_tr = st.enter_context(nc.psum_tensor("ps_tr", [128, 1024], BF16))

        def pairs_v_rg(hh):
            return ps_s[hh // 8], (hh % 8) * 64

        def rest_rg(p):
            return (ps_s[2], 65 * p) if p < 7 else (ps_s[3], 0)

        SEMS = ["sx", "swl", "scv", "ssc", "sfu", "sst",
                "pe_stat", "pe_proj", "pe_v", "pe_tr", "pe_g", "pe_y",
                "pe_zb", "pe_wout", "pe_state", "pe_gate", "pe_gb",
                "pe_ab", "pe_w2", "pe_misc",
                "dv_xb", "dv_row", "dv_xh", "dv_eqk", "dv_bld",
                "dv_evtr", "dv_cast", "dv_f", "dv_z", "dv_evy", "dv_wadd",
                "dv_gate", "dv_hev", "dv_oadd", "dv_init", "dv_gb", "dv_ms",
                "gp_sel", "ac_row", "ac_silu", "ac_f"]
        S = {nm: Sem(st, nc, nm) for nm in SEMS}

        blk = st.enter_context(nc.Block())

        @blk.sync
        def _(sy):
            sy.dma_start(mcol[:, :], mask_d.ap()).then_inc(S["sx"].h, 16)
            sy.dma_start(scol[:, :], scol_d.ap()).then_inc(S["sx"].h, 16)
            S["sx"].n += 32
            sy.dma_start(x_res[:, :], xo_d.ap()).then_inc(S["sx"].h, 16)
            S["sx"].n += 16

        @blk.gpsimd
        def _(g):
            g.memset(ones_rw[:, :], 1.0)
            g.memset(onesmat[:, :], 1.0)
            g.memset(onescol[:, :], 1.0)
            g.memset(onescol_bf[:, :], 1.0)
            g.memset(ident[:, :], 1.0)
            g.affine_select(ident[:, :], ident[:, :], [[1, 128]],
                            ALU.is_equal, 0.0, base=0, channel_multiplier=-1)
            g.memset(tril[:, :], 1.0)
            g.affine_select(tril[:, :], tril[:, :], [[1, 128]],
                            ALU.is_ge, 0.0, base=0, channel_multiplier=-1)
            g.memset(qr_tok[:, :], 0.0)
            g.memset(kr_tok[:, :], 0.0)
            g.memset(bass.AP(qr_tok, 48, [[8 * 128, 128], [64, 16], [1, 1]]), 1.0)
            g.memset(cvals[:, 0:1], LN_EPS)
            g.memset(cvals[:, 1:2], 0.125)
            g.memset(gwt[:, :], 1.0)
            g.affine_select(gwt[0:32, :], gwt[0:32, :], [[0, T]],
                            ALU.is_ge, 0.0, base=-16, channel_multiplier=1)
            g.affine_select(gwt[0:32, :], gwt[0:32, :], [[0, T]],
                            ALU.is_ge, 0.0, base=17, channel_multiplier=-1)
            inst = g.memset(vp[:, :], 0.0)
            S["dv_init"].inc(inst)
            g.dma_start(mcol_bf[:, :], mask_d.ap()).then_inc(S["swl"].h, 16)
            g.dma_start(ebrow[:, :], ebr_d.ap()).then_inc(S["swl"].h, 16)
            S["swl"].n += 32

            def wload(dst_sb, src_d, ncols):
                g.dma_start(dst_sb[:, :], src_d.ap()).then_inc(S["swl"].h, 16)
                S["swl"].n += 16

            wload(wq_sb, wq_d, 256)
            wload(wk_sb, wk_d, 256)
            wload(wv_sb, wv_d, 1024)
            wload(wo_sb, wo_d, 1024)
            wload(gw_sb, gw_d, 16)
            g.dma_start(b1c[:, :], b1c_d.ap()).then_inc(S["swl"].h, 16)
            g.dma_start(b3c[:, :], b3c_d.ap()).then_inc(S["swl"].h, 16)
            S["swl"].n += 32
            g.dma_start(wu2[0][:, 0:DK * 256], wke_d.ap()).then_inc(S["swl"].h, 16)
            src = bass.AP(wve_d, 0, [[DK * 1024, 128], [1, 4096]])
            g.dma_start(wu1[0][:, :], src).then_inc(S["swl"].h, 16)
            src = bass.AP(wve_d, 4096, [[DK * 1024, 128], [1, 4096]])
            g.dma_start(wu3[0][:, :], src).then_inc(S["swl"].h, 16)
            S["swl"].n += 48

        def wke_ap(dk):
            return wu2[0][:, dk * 256:(dk + 1) * 256]

        def wve_ap(dk, half):
            tt = wu1[0] if dk < 4 else wu3[0]
            return tt[:, (dk % 4) * 1024 + half * 512:(dk % 4) * 1024 + half * 512 + 512]

        @blk.vector
        def _(v):
            for i in range(4):
                v.memset(ps_s[i][:, :], 0.0)
            inst = v.memset(ps_t2[:, :], 0.0)
            S["dv_init"].inc(inst)

        @blk.tensor
        def _(t):
            _w(t, S["swl"])
            _w(t, S["dv_init"])
            inst = t.matmul(ps_t0[0:128, 0:16], ones_rw[0:1, 0:128],
                            ebrow[0:1, 0:16], start=True, stop=True)
            S["pe_misc"].inc(inst)

        @blk.vector
        def _(v):
            _w(v, S["pe_misc"])
            inst = v.tensor_copy(eb16[:, :], ps_t0[0:128, 0:16])
            S["dv_init"].inc(inst)

        NCONV = 54
        conv_stage = [wu1[0], wu3[0], wu2[0]]

        def _conv_src(i):
            return bass.AP(wsrc_d, i * 128 * 4096, [[4096, 128], [1, 4096]])

        conv_emitted = [0]

        def emit_conv(k):
            n0 = conv_emitted[0]
            n1 = min(NCONV, n0 + k)
            if n1 == n0:
                return
            conv_emitted[0] = n1

            @blk.gpsimd
            def _(g):
                for i in range(n0, n1):
                    buf = conv_stage[i % 3]
                    if i == 0:
                        g.wait_ge(S["pe_state"].h, 4)
                    if i >= 3:
                        g.wait_ge(S["ssc"].h, (i - 3 + 1) * 16)
                    g.dma_start(buf[:, :], _conv_src(i)).then_inc(
                        S["scv"].h, 16)
                    S["scv"].n += 16

            @blk.sync
            def _(sy):
                for i in range(n0, n1):
                    buf = conv_stage[i % 3]
                    sy.wait_ge(S["scv"].h, (i + 1) * 16)
                    dst = bass.AP(wsc, i * 128 * 4096, [[4096, 128], [1, 4096]])
                    sy.dma_start(dst, buf[:, :]).then_inc(S["ssc"].h, 16)
                    S["ssc"].n += 16

        def ln_group(tok0, from_res, first=False, gidx=0):
            @blk.gpsimd
            def _(g):
                if not from_res:
                    if S["pe_stat"].n > 0:
                        g.wait_ge(S["pe_stat"].h, S["pe_stat"].n)
                    if S["dv_xh"].n > 0:
                        g.wait_ge(S["dv_xh"].h, S["dv_xh"].n)
                    src = bass.AP(xg_d, gidx * 4096,
                                  [[17 * 4096, 128], [1, 4096]])
                    g.dma_start(xb_g[:, :], src).then_inc(S["sx"].h, 16)
                    S["sx"].n += 16

            @blk.vector
            def _(v):
                if from_res:
                    _w(v, S["pe_stat"])
                    src = bass.AP(x_res, tok0, [[DK * T, 128], [T, DK], [1, 512]])
                    v.tensor_copy(xb_g[:, :], src)
                else:
                    _w(v, S["sx"])
                _w(v, S["pe_v"])
                _w(v, S["pe_gate"])
                inst = v.tensor_tensor(xhx[:, :], xb_g[:, :], xb_g[:, :],
                                       ALU.mult)
                S["dv_xb"].inc(inst)

            @blk.tensor
            def _(t):
                _w(t, S["dv_xb"])
                if first:
                    _w(t, S["dv_init"])
                _w(t, S["dv_eqk"])
                _w(t, S["dv_row"])
                _w(t, S["dv_wadd"])
                _w(t, S["dv_bld"])
                _w(t, S["dv_evy"])
                for dk in range(DK):
                    t.matmul(ps_t0[0:128, 0:512], onesmat[:, :],
                             xb_g[:, dk * 512:(dk + 1) * 512],
                             start=(dk == 0), stop=(dk == DK - 1))
                for dk in range(DK):
                    inst = t.matmul(ps_t1[0:128, 0:512], onesmat[:, :],
                                    xhx[:, dk * 512:(dk + 1) * 512],
                                    start=(dk == 0), stop=(dk == DK - 1))
                S["pe_stat"].inc(inst)

            @blk.vector
            def _(v):
                _w(v, S["pe_stat"])
                v.tensor_scalar(mb_s[:, :], ps_t0[0:128, 0:512],
                                1.0 / D, None, ALU.mult)
                v.tensor_tensor(var_s[:, :], mb_s[:, :], mb_s[:, :], ALU.mult)
                inst = v.scalar_tensor_tensor(var_s[:, :], ps_t1[0:128, 0:512],
                                              1.0 / D, var_s[:, :],
                                              ALU.mult, ALU.subtract)
                S["dv_row"].inc(inst)

            @blk.scalar
            def _(s):
                _w(s, S["dv_row"])
                _w(s, S["dv_init"])
                inst = s.activation(var_s[:, :], var_s[:, :], AF.Sqrt,
                                    bias=cvals[:, 0:1], scale=1.0)
                S["ac_row"].inc(inst)

            @blk.vector
            def _(v):
                _w(v, S["ac_row"])
                v.reciprocal(var_s[:, :], var_s[:, :])
                mb_b = bass.AP(mb_s, 0, [[512, 128], [0, DK], [1, 512]])
                rb_b = bass.AP(var_s, 0, [[512, 128], [0, DK], [1, 512]])
                xb3 = bass.AP(xb_g, 0, [[DK * 512, 128], [512, DK], [1, 512]])
                if from_res:
                    _w(v, S["pe_wout"])
                    xh3a = xhat2_ap3(tok0)
                    t3 = bass.AP(xhx, 0, [[DK * 512, 128], [512, DK], [1, 512]])
                    v.tensor_tensor(t3, xb3, mb_b, ALU.subtract)
                    ta = bass.AP(xhx, 0, [[DK * 512, 128], [512, 4], [1, 512]])
                    tb = bass.AP(xhx, 4 * 512, [[DK * 512, 128], [512, 4], [1, 512]])
                    ra = bass.AP(var_s, 0, [[512, 128], [0, 4], [1, 512]])
                    v.tensor_tensor(xh3a[0], ta, ra, ALU.mult)
                    inst = v.tensor_tensor(xh3a[1], tb, ra, ALU.mult)
                else:
                    xh3 = bass.AP(xhx, 0, [[DK * 512, 128], [512, DK], [1, 512]])
                    v.tensor_tensor(xh3, xb3, mb_b, ALU.subtract)
                    inst = v.tensor_tensor(xh3, xh3, rb_b, ALU.mult)
                S["dv_xh"].inc(inst)

        def xhat2_ap(dk, c0, w):
            tt = wv_sb if dk < 4 else wo_sb
            o = (dk % 4) * T + c0
            return tt[:, o:o + w]

        def xhat2_ap3(tok0):
            a = bass.AP(wv_sb, tok0, [[4 * T, 128], [T, 4], [1, 512]])
            b = bass.AP(wo_sb, tok0, [[4 * T, 128], [T, 4], [1, 512]])
            return a, b

        def attn_chunk(kind, tokabs, cidx):
            is_own = kind == "own"
            if kind == "pre":
                jm, mct, mcp = tokabs // 128, mcol, NCH
                mc = mcol[:, jm:jm + 1]
                mcb = mcol_bf[:, jm:jm + 1]
            else:
                jm, mct, mcp = 0, onescol, 1
                mc = onescol[:, 0:1]
                mcb = onescol_bf[:, 0:1]

            def xh(dk):
                c = dk * 512 + cidx * 128
                return xhx[:, c:c + 128]

            def wk_ap(dk):
                if kind == "enc":
                    return wke_ap(dk)
                return wk_sb[:, dk * 256:(dk + 1) * 256]

            def wv_ap(dk, half):
                if kind == "enc":
                    return wve_ap(dk, half)
                return wv_sb[:, dk * 1024 + half * 512:dk * 1024 + half * 512 + 512]

            @blk.tensor
            def _(t):
                _w(t, S["dv_xh"])
                _w(t, S["dv_eqk"])
                _w(t, S["dv_wadd"])
                _w(t, S["dv_row"])
                for dk in range(DK):
                    t.matmul(ps_t0[0:128, 0:256], xh(dk), wk_ap(dk),
                             start=(dk == 0), stop=(dk == DK - 1))
                if is_own:
                    for dk in range(DK):
                        t.matmul(ps_t0[0:128, 256:512], xh(dk),
                                 wq_sb[:, dk * 256:(dk + 1) * 256],
                                 start=(dk == 0), stop=(dk == DK - 1))
                _w(t, S["dv_bld"])
                _w(t, S["dv_evy"])
                for dk in range(DK):
                    inst = t.matmul(ps_t1[0:128, 0:512], xh(dk), wv_ap(dk, 0),
                                    start=(dk == 0), stop=(dk == DK - 1))
                S["pe_proj"].inc(inst)

            @blk.vector
            def _(v):
                _w(v, S["pe_proj"])
                _w(v, S["pe_state"])
                _w(v, S["pe_y"])
                _w(v, S["sx"], 48)
                v.tensor_copy(ktok[:, :], ps_t0[0:128, 0:256])
                if is_own:
                    v.tensor_copy(bass.AP(qtok_t, qtok_o, [[4096, 128], [1, 256]]),
                                  ps_t0[0:128, 256:512])
                ksrc = bass.AP(ktok, 0, [[256, 128], [16, H], [1, 16]])
                v.tensor_copy(bass.AP(kk, 0, [[H * 32, 128], [32, H], [1, 16]]),
                              ksrc)
                v.tensor_copy(bass.AP(kk, 16, [[H * 32, 128], [32, H], [1, 16]]),
                              ksrc)
                if is_own:
                    qsrc = bass.AP(qtok_t, qtok_o, [[4096, 128], [16, H], [1, 16]])
                    v.tensor_copy(
                        bass.AP(qq, 0, [[H * 32, 128], [32, H], [1, 16]]), qsrc)
                    v.tensor_copy(
                        bass.AP(qq, 16, [[H * 32, 128], [32, H], [1, 16]]), qsrc)
                inst = v.tensor_copy(
                    bass.AP(vp, 64, [[H * 65, 128], [65, 8], [1, 1]]),
                    bass.AP(mct, jm, [[mcp, 128], [0, 8], [1, 1]]))
                S["dv_eqk"].inc(inst)

            @blk.scalar
            def _(s_):
                _w(s_, S["pe_proj"])
                _w(s_, S["pe_state"])
                _w(s_, S["pe_y"])
                dstA = bass.AP(vp, 0, [[H * 65, 128], [65, 8], [1, 64]])
                srcA = bass.AP(ps_t1, 0, [[512, 128], [64, 8], [1, 64]])
                inst = s_.mul(dstA, srcA, mc)
                S["ac_f"].inc(inst)

            @blk.tensor
            def _(t):
                _w(t, S["dv_eqk"])
                _w(t, S["ac_f"])
                for dk in range(DK):
                    inst = t.matmul(ps_t1[0:128, 0:512], xh(dk), wv_ap(dk, 1),
                                    start=(dk == 0), stop=(dk == DK - 1))
                S["pe_v"].inc(inst)

            @blk.scalar
            def _(s_):
                _w(s_, S["pe_v"])
                dstB = bass.AP(vp, 8 * 65, [[H * 65, 128], [65, 8], [1, 64]])
                srcB = bass.AP(ps_t1, 0, [[512, 128], [64, 8], [1, 64]])
                inst = s_.mul(dstB, srcB, mc)
                S["ac_f"].inc(inst)

            @blk.vector
            def _(v):
                _w(v, S["pe_v"])
                _w(v, S["pe_tr"])
                v.tensor_copy(
                    bass.AP(vp, 8 * 65 + 64, [[H * 65, 128], [65, 8], [1, 1]]),
                    bass.AP(mct, jm, [[mcp, 128], [0, 8], [1, 1]]))

                def build(pt_tt, pt_oo, rtile, dup, tok_ap):
                    for dd in range(1, 8):
                        i0 = bass.AP(dup, 0, [[H * 32, 128], [32, H], [1, 16]])
                        i1 = bass.AP(dup, dd, [[H * 32, 128], [32, H], [1, 16]])
                        oo = bass.AP(pt_tt, pt_oo + (dd - 1) * 16,
                                     [[4096 if pt_tt in (wu3[1],) else H * NPAIR, 128],
                                      [NPAIR, H], [1, 16]])
                        v.tensor_tensor(oo, i0, i1, ALU.mult)
                    i0 = bass.AP(dup, 0, [[H * 32, 128], [32, H], [1, 8]])
                    i1 = bass.AP(dup, 8, [[H * 32, 128], [32, H], [1, 8]])
                    oo = bass.AP(pt_tt, pt_oo + 112,
                                 [[4096 if pt_tt in (wu3[1],) else H * NPAIR, 128],
                                  [NPAIR, H], [1, 8]])
                    v.tensor_tensor(oo, i0, i1, ALU.mult)
                    src = bass.AP(tok_ap[0], tok_ap[1],
                                  [[tok_ap[2], 128], [32, 8], [16, 2], [1, 16]])
                    dl = bass.AP(rtile, 0,
                                 [[8 * 128, 128], [128, 8], [64, 2], [1, 16]])
                    v.tensor_copy(dl, src)
                    dd2 = bass.AP(rtile, 32,
                                  [[8 * 128, 128], [128, 8], [64, 2], [1, 16]])
                    return v.tensor_tensor(dd2, src, src, ALU.mult)

                _w(v, S["ac_f"])
                inst = build(kp_tok_t, kp_tok_o, kr_tok, kk, (ktok, 0, 256))
                dm = bass.AP(kr_tok, 48, [[8 * 128, 128], [128, 8], [64, 2], [1, 1]])
                sm = bass.AP(mct, jm, [[mcp, 128], [0, 8], [0, 2], [1, 1]])
                inst = v.tensor_copy(dm, sm)
                if is_own:
                    inst = build(qp_tok_t, qp_tok_o, qr_tok, qq,
                                 (qtok_t, qtok_o, 4096))
                S["dv_bld"].inc(inst)

            if is_own and OWNLVL >= 1:
                @blk.scalar
                def _(s_):
                    _w(s_, S["pe_state"])
                    _w(s_, S["pe_y"])
                    s_.mul(sbp[0:120, 0:512], ps_s[0][0:120, 0:512], 1.0 / 16.0)
                    s_.mul(sbp[0:120, 512:1024], ps_s[1][0:120, 0:512],
                           1.0 / 16.0)
                    inst = s_.mul(sbpd[0:120, 0:16], ps_s[2][0:120, 455:471],
                                  1.0 / 16.0)
                    S["ac_f"].inc(inst)

                @blk.vector
                def _(v):
                    _w(v, S["pe_state"])
                    _w(v, S["pe_y"])
                    ev_out = bass.AP(sbr, 0, [[H * 65, RW], [130, 7], [1, 65]])
                    ev_in = ps_s[2][0:RW, 0:455].rearrange(
                        "p (r c) -> p r c", r=7)
                    v.tensor_scalar(ev_out, ev_in, scol[0:RW, 0:1], None,
                                    ALU.mult)
                    od_out = bass.AP(sbr, 65, [[H * 65, RW], [130, 7], [1, 65]])
                    od_in = ps_s[2][64:64 + RW, 0:455].rearrange(
                        "p (r c) -> p r c", r=7)
                    v.tensor_scalar(od_out, od_in, scol[0:RW, 0:1], None,
                                    ALU.mult)
                    v.tensor_scalar(sbr[0:RW, 14 * 65:14 * 65 + 65],
                                    ps_s[3][0:RW, 0:65],
                                    scol[0:RW, 0:1], None, ALU.mult)
                    inst = v.tensor_scalar(sbr[0:RW, 15 * 65:15 * 65 + 65],
                                           ps_s[3][64:64 + RW, 0:65],
                                           scol[0:RW, 0:1], None, ALU.mult)
                    S["dv_cast"].inc(inst)

                pass
            if is_own and OWNLVL >= 2:
                def rto(hh):
                    return (hh // 2) * 128 + (hh % 2) * 64

                tr_specs = [
                    ([(qp_tok_t, qp_tok_o + hh * NPAIR, NPAIR) for hh in range(8)],
                     (qp_feat_t, qp_feat_o + 0), NPAIR),
                    ([(qp_tok_t, qp_tok_o + (8 + hh) * NPAIR, NPAIR)
                      for hh in range(8)],
                     (qp_feat_t, qp_feat_o + 1024), NPAIR),
                    ([(qr_tok, rto(hh), RW) for hh in range(8)],
                     (qr_feat, 0), RW),
                    ([(qr_tok, rto(8 + hh), RW) for hh in range(8)],
                     (qr_feat, 1024), RW),
                    ([(kr_tok, rto(hh), RW) for hh in range(8)],
                     (kr_feat, 0), RW),
                    ([(kr_tok, rto(8 + hh), RW) for hh in range(8)],
                     (kr_feat, 1024), RW),
                ]
                for wv_i in range(6):
                    srcs_w, (dt_t, dt_o), rows = tr_specs[wv_i]

                    @blk.tensor
                    def _(t, srcs_w=srcs_w, rows=rows):
                        _w(t, S["dv_bld"])
                        _w(t, S["dv_evtr"])
                        for i8, (tt, o, wdt) in enumerate(srcs_w):
                            src = bass.AP(tt, o, [[tt.shape[1], 128], [1, wdt]])
                            inst = t.transpose(
                                ps_tr[0:rows, i8 * 128:(i8 + 1) * 128],
                                src, ident[:, :])
                        S["pe_tr"].inc(inst)

                    @blk.scalar
                    def _(s_, dt_t=dt_t, dt_o=dt_o, rows=rows):
                        _w(s_, S["pe_tr"])
                        _w(s_, S["pe_g"])
                        _w(s_, S["pe_y"])
                        dst = bass.AP(dt_t, dt_o, [[dt_t.shape[1], rows], [1, 1024]])
                        inst = s_.copy(dst, ps_tr[0:rows, 0:1024])
                        S["dv_evtr"].inc(inst)

                pass
            if is_own and OWNLVL >= 3:
                for wv_i in range(4):
                    @blk.tensor
                    def _(t, wv_i=wv_i):
                        _w(t, S["dv_evtr"])
                        _w(t, S["dv_f"])
                        for hh in range(4 * wv_i, 4 * wv_i + 4):
                            inst = t.matmul(
                                ps_t0[0:128, (hh % 4) * 128:(hh % 4) * 128 + 128],
                                kr_feat[0:16, hh * 128:(hh + 1) * 128],
                                qr_feat[0:16, hh * 128:(hh + 1) * 128],
                                start=True, stop=True)
                        S["pe_g"].inc(inst)

                    @blk.scalar
                    def _(s, wv_i=wv_i):
                        _w(s, S["pe_g"])
                        _w(s, S["dv_f"], max(0, S["dv_f"].n - 1))
                        fts = mb_s if wv_i % 2 == 0 else var_s
                        inst = s.activation(fts[:, :], ps_t0[0:128, 0:512],
                                            AF.Identity, bias=1.0,
                                            scale=cvals[:, 1:2])
                        S["ac_f"].inc(inst)

                    @blk.vector
                    def _(v, wv_i=wv_i):
                        _w(v, S["ac_f"])
                        _w(v, S["pe_y"])
                        fts = mb_s if wv_i % 2 == 0 else var_s
                        par = (wv_i % 2) * 512
                        v.scalar_tensor_tensor(ftmp[:, par:par + 512],
                                               ps_t0[0:128, 0:512], 0.25,
                                               fts[:, :], ALU.mult, ALU.mult)
                        trb = bass.AP(tril, 0, [[128, 128], [0, 4], [1, 128]])
                        i0 = bass.AP(ftmp, par, [[1024, 128], [128, 4], [1, 128]])
                        oo = bass.AP(pt_t, pt_o + wv_i * 512,
                                     [[4096, 128], [128, 4], [1, 128]])
                        inst = v.scalar_tensor_tensor(oo, i0, 1.0, trb,
                                                      ALU.add, ALU.mult)
                        S["dv_f"].inc(inst)

                    if OWNLVL < 4:
                        continue

                    @blk.tensor
                    def _(t, wv_i=wv_i):
                        _w(t, S["dv_f"])
                        _w(t, S["dv_cast"])
                        _w(t, S["ac_f"])
                        _w(t, S["dv_bld"])
                        _w(t, S["dv_evy"])
                        for hh in range(4 * wv_i, 4 * wv_i + 4):
                            fo = (hh % 4) * 128
                            t.matmul(ps_t1[0:65, fo:fo + 128],
                                     vp[:, hh * 65:hh * 65 + 65],
                                     bass.AP(pt_t, pt_o + hh * 128,
                                             [[4096, 128], [1, 128]]),
                                     start=True, stop=False)
                            t.matmul(ps_t1[0:64, fo:fo + 128],
                                     sbp[0:120, hh * 64:(hh + 1) * 64],
                                     bass.AP(qp_feat_t, qp_feat_o + hh * 128,
                                             [[4096, 120], [1, 128]]),
                                     start=False, stop=False)
                            t.matmul(ps_t1[64:65, fo:fo + 128],
                                     sbpd[0:120, hh:hh + 1],
                                     bass.AP(qp_feat_t, qp_feat_o + hh * 128,
                                             [[4096, 120], [1, 128]]),
                                     start=False, stop=False)
                            inst = t.matmul(
                                ps_t1[0:65, fo:fo + 128],
                                sbr[0:RW, hh * 65:hh * 65 + 65],
                                qr_feat[0:RW, hh * 128:(hh + 1) * 128],
                                start=False, stop=True)
                        S["pe_y"].inc(inst)

                    if OWNLVL < 5:
                        continue

                    @blk.vector
                    def _(v, wv_i=wv_i):
                        _w(v, S["pe_y"])
                        _w(v, S["pe_zb"])
                        v.tensor_scalar(ztmp[0:1, 0:512], ps_t1[64:65, 0:512],
                                        ATT_EPS, None, ALU.add)
                        v.reciprocal(ztmp[0:1, 512:1024], ztmp[0:1, 0:512])
                        inst = v.tensor_copy(zr[0:1, 0:512], ztmp[0:1, 512:1024])
                        S["dv_z"].inc(inst)

                    @blk.tensor
                    def _(t, wv_i=wv_i):
                        _w(t, S["dv_z"])
                        _w(t, S["dv_evy"])
                        _w(t, S["dv_wadd"])
                        inst = t.matmul(ps_t2[0:128, 0:512], ones_rw[0:1, 0:128],
                                        zr[0:1, 0:512], start=True, stop=True)
                        S["pe_zb"].inc(inst)

                    @blk.scalar
                    def _(s_, wv_i=wv_i):
                        _w(s_, S["pe_zb"])
                        inst = s_.copy(zb_sb[:, :], ps_t2[0:128, 0:512])
                        S["ac_row"].inc(inst)

                    @blk.vector
                    def _(v, wv_i=wv_i):
                        _w(v, S["ac_row"])
                        _w(v, S["pe_zb"])
                        _w(v, S["pe_wout"])
                        inst = None
                        for hh in range(4 * wv_i, 4 * wv_i + 4):
                            p, rowb = hh // 2, (hh % 2) * 64
                            fo = (hh % 4) * 128
                            inst = v.tensor_tensor(
                                yp[rowb:rowb + 64, p * 128:(p + 1) * 128],
                                ps_t1[0:64, fo:fo + 128],
                                zb_sb[rowb:rowb + 64, fo:fo + 128], ALU.mult)
                        S["dv_evy"].inc(inst)

                pass
            if is_own and OWNLVL >= 6:
                @blk.tensor
                def _(t):
                    _w(t, S["dv_evy"])
                    _w(t, S["dv_f"])
                    for m in range(4):
                        for k in range(8):
                            inst = t.matmul(
                                ps_t0[0:128, m * 128:(m + 1) * 128],
                                wo_sb[:, k * 1024 + m * 128:k * 1024 + m * 128 + 128],
                                yp[:, k * 128:(k + 1) * 128],
                                start=(k == 0), stop=(k == 7))
                    for m in range(4, 8):
                        for k in range(8):
                            inst = t.matmul(
                                ps_t2[0:128, (m - 4) * 128:(m - 3) * 128],
                                wo_sb[:, k * 1024 + m * 128:k * 1024 + m * 128 + 128],
                                yp[:, k * 128:(k + 1) * 128],
                                start=(k == 0), stop=(k == 7))
                    S["pe_wout"].inc(inst)

                @blk.vector
                def _(v):
                    _w(v, S["pe_wout"])
                    _w(v, S["sx"], 48)
                    oc = tokabs - PRE
                    o1 = bass.AP(x_res, oc, [[DK * T, 128], [T, 4], [1, 128]])
                    v.tensor_tensor(o1, o1,
                                    bass.AP(ps_t0, 0, [[512, 128], [128, 4], [1, 128]]),
                                    ALU.add)
                    o2 = bass.AP(x_res, 4 * T + oc, [[DK * T, 128], [T, 4], [1, 128]])
                    inst = v.tensor_tensor(
                        o2, o2,
                        bass.AP(ps_t2, 0, [[512, 128], [128, 4], [1, 128]]),
                        ALU.add)
                    S["dv_wadd"].inc(inst)

            @blk.tensor
            def _(t):
                _w(t, S["dv_bld"])
                _w(t, S["dv_init"])
                if is_own:
                    _w(t, S["dv_cast"])
                    _w(t, S["ac_f"])
                inst = None
                for hh in range(H):
                    bank, off = pairs_v_rg(hh)
                    t.matmul(bank[0:NPAIR, off:off + 64],
                             bass.AP(kp_tok_t, kp_tok_o + hh * NPAIR,
                                     [[4096, 128], [1, NPAIR]]),
                             vp[:, hh * 65:hh * 65 + 64],
                             start=False, stop=True)
                    t.matmul(ps_s[2][0:NPAIR, 455 + hh:456 + hh],
                             bass.AP(kp_tok_t, kp_tok_o + hh * NPAIR,
                                     [[4096, 128], [1, NPAIR]]),
                             mcb, start=False, stop=True)
                for p in range(8):
                    bank, off = rest_rg(p)
                    for half in range(2):
                        hh = 2 * p + half
                        base = half * 64
                        inst = t.matmul(
                            bank[base:base + RW, off:off + 65],
                            kr_tok[:, p * 128 + base:p * 128 + base + RW],
                            vp[:, hh * 65:hh * 65 + 65],
                            start=False, stop=True)
                S["pe_state"].inc(inst)

        OWNLVL = int(os.environ.get("KMK_OWNLVL", "6"))
        PHASE = os.environ.get("KMK_PHASE", "full")
        NGRP = int(os.environ.get("KMK_GROUPS", "17"))
        DOCONV = os.environ.get("KMK_CONV", "1") == "1" and PHASE != "attn_noconv"
        groups = [("enc", 0)]
        groups += [("pre", 512 * g) for g in range(12)]
        groups += [("own", PRE + 512 * g) for g in range(4)]
        groups = groups[:NGRP]
        first = True
        for gi, (kind, tok0) in enumerate(groups):
            ln_group(tok0, from_res=False, first=first, gidx=gi)
            first = False
            for cidx in range(4):
                attn_chunk(kind, tok0 + cidx * 128, cidx)
            if kind != "enc" and DOCONV:
                emit_conv(4)
        if DOCONV:
            emit_conv(NCONV)
        if PHASE in ("attn", "attn_noconv"):
            @blk.sync
            def _(sy):
                _w(sy, S["dv_wadd"])
                if S["ssc"].n > 0:
                    _w(sy, S["ssc"])
                src = bass.AP(x_res, 0, [[DK * T, 128], [T, DK], [1, T]])
                dst = bass.AP(out_d, 0, [[T, 128], [128 * T, DK], [1, T]])
                sy.dma_start(dst, src).then_inc(S["sst"].h, 16)
                S["sst"].n += 16
                sy.wait_ge(S["sst"].h, S["sst"].n)

            @blk.tensor
            def _(t):
                _w(t, S["pe_state"])

            @blk.vector
            def _(v):
                _w(v, S["dv_bld"])
            return nc

        @blk.gpsimd
        def _(g):
            g.wait_ge(S["pe_state"].h, 68)
            g.wait_ge(S["pe_wout"].h, 16)
            g.dma_start(yp[0:17, 0:D], b2_d.ap()).then_inc(S["swl"].h, 16)
            S["swl"].n += 16
            g.memset(qr_feat[0:32, 0:2048], 1.0)
            g.affine_select(
                bass.AP(qr_feat, 0, [[H * 128, 32], [128, 16], [1, 128]]),
                bass.AP(qr_feat, 0, [[H * 128, 32], [128, 16], [1, 128]]),
                [[128, 16], [0, 128]], ALU.is_equal, 0.0,
                base=0, channel_multiplier=-128)
            g.memset(sbr[0:32, 0:256], 1.0)
            inst = g.affine_select(
                bass.AP(sbr, 0, [[H * 65, 32], [128, 2], [1, 128]]),
                bass.AP(sbr, 0, [[H * 65, 32], [128, 2], [1, 128]]),
                [[128, 2], [0, 128]], ALU.is_equal, 0.0,
                base=16 * 128, channel_multiplier=-128)
            S["gp_sel"].inc(inst)

        def gate_logits(ch):
            par = (ch % 2) * 16

            @blk.tensor
            def _(t):
                _w(t, S["dv_xh"])
                _w(t, S["dv_gate"])
                for dk in range(DK):
                    inst = t.matmul(ps_t0[0:128, par:par + 16],
                                    xhat2_ap(dk, ch * 128, 128),
                                    gw_sb[:, dk * 16:(dk + 1) * 16],
                                    start=(dk == 0), stop=(dk == DK - 1))
                S["pe_gate"].inc(inst)

            @blk.vector
            def _(v):
                _w(v, S["pe_gate"])
                inst = v.tensor_copy(glog[:, ch * 16:(ch + 1) * 16],
                                     ps_t0[0:128, par:par + 16])
                S["dv_gate"].inc(inst)

        for g4 in range(4):
            ln_group(g4 * 512, from_res=True)
            for ci in range(4):
                gate_logits(g4 * 4 + ci)

        @blk.scalar
        def _(s_):
            _w(s_, S["dv_gate"])
            inst = s_.activation(gt1[:, :], glog[:, :], AF.Sigmoid)
            S["ac_silu"].inc(inst)

        @blk.vector
        def _(v):
            _w(v, S["ac_silu"])
            ebb = bass.AP(eb16, 0, [[16, 128], [0, 16], [1, 16]])
            g3 = lambda tt: bass.AP(tt, 0, [[256, 128], [16, 16], [1, 16]])
            v.tensor_tensor(g3(gt0), g3(glog), ebb, ALU.add)
            v.drain()
            v.reduce_max(bass.AP(gm, 0, [[64, 128], [1, 16], [0, 1]]),
                         g3(gt0), axis=AXX)
            v.drain()
            m1b = bass.AP(gm, 0, [[64, 128], [1, 16], [0, 16]])
            v.tensor_tensor(g3(gt2), g3(gt0), m1b, ALU.is_ge)
            v.drain()
            v.scalar_tensor_tensor(gt2[:, :], gt2[:, :], NEG_BIG, gt0[:, :],
                                   ALU.mult, ALU.add)
            v.drain()
            v.reduce_max(bass.AP(gm, 16, [[64, 128], [1, 16], [0, 1]]),
                         g3(gt2), axis=AXX)
            v.drain()
            m2b = bass.AP(gm, 16, [[64, 128], [1, 16], [0, 16]])
            v.tensor_tensor(g3(gt2), g3(gt0), m2b, ALU.is_ge)
            v.drain()
            v.tensor_tensor(gt1[:, :], gt1[:, :], gt2[:, :], ALU.mult)
            v.drain()
            v.reduce_sum(bass.AP(gm, 32, [[64, 128], [1, 16], [0, 1]]),
                         g3(gt1), axis=AXX)
            v.drain()
            v.tensor_scalar(gm[:, 48:64], gm[:, 32:48], GATE_EPS, None, ALU.add)
            v.drain()
            v.reciprocal(gm[:, 32:48], gm[:, 48:64])
            v.drain()
            rb2 = bass.AP(gm, 32, [[64, 128], [1, 16], [0, 16]])
            inst = v.tensor_tensor(
                bass.AP(gchb_t, gchb_o, [[gchb_t.shape[1], 128], [16, 16], [1, 16]]),
                g3(gt1), rb2, ALU.mult)
            v.drain()
            S["dv_gate"].inc(inst)

        for ch in range(16):
            @blk.tensor
            def _(t, ch=ch):
                _w(t, S["dv_gate"])
                _w(t, S["dv_evtr"])
                inst = t.transpose(
                    ps_tr[0:16, (ch % 8) * 128:(ch % 8) * 128 + 128],
                    bass.AP(gchb_t, gchb_o + ch * 16,
                            [[gchb_t.shape[1], 128], [1, 16]]),
                    ident[:, :])
                S["pe_tr"].inc(inst)

            @blk.vector
            def _(v, ch=ch):
                _w(v, S["pe_tr"])
                inst = v.tensor_copy(gwt[0:16, ch * 128:(ch + 1) * 128],
                                     ps_tr[0:16, (ch % 8) * 128:(ch % 8) * 128 + 128])
                S["dv_evtr"].inc(inst)

        if PHASE == "ln2":
            @blk.vector
            def _(v):
                _w(v, S["dv_gate"])
                _w(v, S["dv_xh"])
                _w(v, S["gp_sel"])
                _w(v, S["dv_evtr"])
                inst = None
                if os.environ.get("KMK_DUMP", "xhat2") == "gwt":
                    inst = v.tensor_copy(x_res[0:32, 0:T], gwt[0:32, 0:T])
                    inst = v.tensor_copy(x_res[0:32, 3 * T:3 * T + 2048],
                                         qr_feat[0:32, 0:2048])
                    inst = v.tensor_copy(
                        x_res[0:128, 2 * T:2 * T + 256],
                        bass.AP(gchb_t, gchb_o, [[gchb_t.shape[1], 128], [1, 256]]))
                else:
                    for dk in range(DK):
                        inst = v.tensor_copy(x_res[:, dk * T:(dk + 1) * T],
                                             xhat2_ap(dk, 0, T))
                S["dv_oadd"].inc(inst)

            @blk.sync
            def _(sy):
                _w(sy, S["dv_oadd"])
                _w(sy, S["ssc"])
                src = bass.AP(x_res, 0, [[DK * T, 128], [T, DK], [1, T]])
                dst = bass.AP(out_d, 0, [[T, 128], [128 * T, DK], [1, T]])
                sy.dma_start(dst, src).then_inc(S["sst"].h, 16)
                S["sst"].n += 16
                sy.wait_ge(S["sst"].h, S["sst"].n)
            return nc

        for fblk in range(8):
            @blk.vector
            def _(v):
                _w(v, S["pe_w2"])
                _w(v, S["pe_state"])
                _w(v, S["pe_y"])
                inst = None
                for i in range(4):
                    inst = v.memset(ps_s[i][:, :], 0.0)
                S["dv_ms"].inc(inst)

            @blk.tensor
            def _(t):
                _w(t, S["dv_ms"])
                _w(t, S["dv_gate"])
                _w(t, S["swl"])
                inst = None
                for dk in range(DK):
                    bank, colo = ps_s[dk // 2], (dk % 2) * 256
                    inst = t.matmul(bank[0:128, colo:colo + 256],
                                     b2_sb[0:17, dk * 128:(dk + 1) * 128],
                                     gwt[0:17, fblk * 256:(fblk + 1) * 256],
                                     start=False, stop=True)
                S["pe_misc"].inc(inst)

            for u in range(E_UNITS):
                ui = fblk * E_UNITS + u
                par = ui % 2

                @blk.sync
                def _(sy):
                    if ui >= 2:
                        sy.wait_ge(S["pe_w2"].h, ui - 1)
                    else:
                        sy.wait_ge(S["ssc"].h, NCONV * 16)
                        _w(sy, S["dv_gate"])
                        _w(sy, S["dv_xb"])
                    if fblk == 0:
                        sy.wait_ge(S["ssc"].h, min(NCONV, 3 * (u + 1)) * 16)
                    for j, buf in enumerate([wu1[par], wu3[par], wu2[par]]):
                        src = bass.AP(wsc, (u * 3 + j) * 128 * 4096,
                                      [[4096, 128], [1, 4096]])
                        sy.dma_start(buf[:, :], src).then_inc(S["sfu"].h, 16)
                        S["sfu"].n += 16

                @blk.tensor
                def _(t):
                    _w(t, S["sfu"])
                    _w(t, S["dv_gb"])
                    _w(t, S["dv_xh"])
                    _w(t, S["dv_ms"])
                    _w(t, S["gp_sel"])
                    inst = t.matmul(ps_t0[0:128, 0:256], sel_ap(u),
                                    gwt[0:32, fblk * 256:(fblk + 1) * 256],
                                    start=True, stop=True)
                    S["pe_gb"].inc(inst)

                @blk.vector
                def _(v):
                    _w(v, S["pe_gb"])
                    _w(v, S["dv_hev"])
                    inst = v.tensor_copy(bass.AP(gb_t, gb_o, [[256, 128], [1, 256]]),
                                         ps_t0[0:128, 0:256])
                    S["dv_gb"].inc(inst)

                rot = [ps_t1, ps_t2, ps_t1, ps_t2]

                def emit_ab(ci):
                    @blk.tensor
                    def _(t):
                        if ci == 0:
                            _w(t, S["dv_gb"])
                        _w(t, S["dv_hev"], max(0, S["dv_hev"].n - 1))
                        bank = rot[ci]
                        for dk in range(DK):
                            t.matmul(
                                bank[0:128, 0:256],
                                wu1[par][:, dk * 512 + ci * 128:dk * 512 + ci * 128 + 128],
                                xhat2_ap(dk, fblk * 256, 256),
                                start=(dk == 0), stop=(dk == DK - 1))
                        for dk in range(DK):
                            inst = t.matmul(
                                bank[0:128, 256:512],
                                wu3[par][:, dk * 512 + ci * 128:dk * 512 + ci * 128 + 128],
                                xhat2_ap(dk, fblk * 256, 256),
                                start=(dk == 0), stop=(dk == DK - 1))
                        S["pe_ab"].inc(inst)

                    @blk.vector
                    def _(v):
                        v.wait_ge(S["ac_silu"].h, ac_base + ci + 1)
                        _w(v, S["pe_w2"])
                        v.scalar_tensor_tensor(
                            tb_t[:, ci * 256:(ci + 1) * 256],
                            rot[ci][0:128, 256:512],
                            b3c[:, u * 4 + ci:u * 4 + ci + 1],
                            bass.AP(gb_t, gb_o, [[256, 128], [1, 256]]),
                            ALU.add, ALU.mult)
                        inst = v.tensor_tensor(
                            h_t[:, ci * 256:(ci + 1) * 256],
                            tb_t[:, ci * 256:(ci + 1) * 256],
                            sa_t[:, ci * 256:(ci + 1) * 256], ALU.mult)
                        S["dv_hev"].inc(inst)

                def emit_w2(ci):
                    @blk.tensor
                    def _(t):
                        _w(t, S["dv_hev"],
                           S["dv_hev"].n - (1 if ci < 3 else 0))
                        inst = None
                        for dk in range(DK):
                            bank, colo = ps_s[dk // 2], (dk % 2) * 256
                            inst = t.matmul(
                                bank[0:128, colo:colo + 256],
                                wu2[par][:, ci * 1024 + dk * 128:ci * 1024 + dk * 128 + 128],
                                h_t[:, ci * 256:(ci + 1) * 256],
                                start=False, stop=True)
                        if ci == 3:
                            S["pe_w2"].inc(inst)
                        else:
                            S["pe_misc"].inc(inst)

                ab_base = S["pe_ab"].n
                hev_base = S["dv_hev"].n
                ac_base = S["ac_silu"].n

                @blk.scalar
                def _(s_):
                    for ci in range(4):
                        s_.wait_ge(S["pe_ab"].h, ab_base + ci + 1)
                        if hev_base + ci - 1 > 0:
                            s_.wait_ge(S["dv_hev"].h, hev_base + ci - 1)
                        inst = s_.activation(
                            sa_t[:, ci * 256:(ci + 1) * 256],
                            rot[ci][0:128, 0:256], AF.Silu,
                            bias=b1c[:, u * 4 + ci:u * 4 + ci + 1], scale=1.0)
                        S["ac_silu"].inc(inst)

                emit_ab(0)
                emit_ab(1)
                emit_w2(0)
                emit_ab(2)
                emit_w2(1)
                emit_ab(3)
                emit_w2(2)
                emit_w2(3)

            @blk.vector
            def _(v):
                _w(v, S["pe_w2"])
                _w(v, S["pe_misc"])
                inst = None
                for i in range(4):
                    oo = bass.AP(x_res, 2 * i * T + fblk * 256,
                                 [[DK * T, 128], [T, 2], [1, 256]])
                    inst = v.tensor_tensor(
                        oo, oo,
                        bass.AP(ps_s[i], 0, [[512, 128], [256, 2], [1, 256]]),
                        ALU.add)
                S["dv_oadd"].inc(inst)

        @blk.sync
        def _(sy):
            _w(sy, S["dv_oadd"])
            src = bass.AP(x_res, 0, [[DK * T, 128], [T, DK], [1, T]])
            dst = bass.AP(out_d, 0, [[T, 128], [128 * T, DK], [1, T]])
            sy.dma_start(dst, src).then_inc(S["sst"].h, 16)
            S["sst"].n += 16
            sy.wait_ge(S["sst"].h, S["sst"].n)

    return nc


_NC_CACHE = {}


def _get_nc():
    key = (os.environ.get("KMK_PHASE", "full"), os.environ.get("KMK_GROUPS", "17"),
           os.environ.get("KMK_CONV", "1"))
    if key not in _NC_CACHE:
        _NC_CACHE[key] = build_program()
    return _NC_CACHE[key]


def kernel(**inp):
    f32 = np.float32
    x = np.asarray(inp["L0_attn_x"], dtype=f32)
    wq = np.asarray(inp["L0_attn_Wq"], f32)
    wk = np.asarray(inp["L0_attn_Wk"], f32)
    wv = np.asarray(inp["L0_attn_Wv"], f32)
    wke = np.asarray(inp["L0_attn_Wk_enc"], f32)
    wve = np.asarray(inp["L0_attn_Wv_enc"], f32)
    wo = np.asarray(inp["L0_attn_Wout"], f32)
    gw = np.asarray(inp["L0_ffn_gate_w"], f32)
    eb = np.asarray(inp["L0_ffn_expert_bias"], f32)
    w1 = np.asarray(inp["L0_ffn_w1"], f32)
    b1 = np.asarray(inp["L0_ffn_b1"], f32)
    w2 = np.asarray(inp["L0_ffn_w2"], f32)
    b2 = np.asarray(inp["L0_ffn_b2"], f32)
    w3 = np.asarray(inp["L0_ffn_w3"], f32)
    b3 = np.asarray(inp["L0_ffn_b3"], f32)
    sw1 = np.asarray(inp["L0_ffn_sw1"], f32)
    sb1 = np.asarray(inp["L0_ffn_sb1"], f32)
    sw2 = np.asarray(inp["L0_ffn_sw2"], f32)
    sb2 = np.asarray(inp["L0_ffn_sb2"], f32)
    sw3 = np.asarray(inp["L0_ffn_sw3"], f32)
    sb3 = np.asarray(inp["L0_ffn_sb3"], f32)

    C = np.ascontiguousarray

    def tile_t(wt):
        ncols = wt.shape[1]
        return C(wt.reshape(DK, 128, ncols).transpose(1, 0, 2)
                 .reshape(128, DK * ncols))

    def tile_k(wt, kch):
        ncols = wt.shape[1]
        return wt.reshape(kch, 128, ncols).transpose(1, 0, 2).reshape(
            128, kch * ncols)

    wsrc = np.empty((54, 128, 4096), f32)
    i = 0
    for e in range(16):
        wsrc[i] = tile_k(w1[e].T, 8); i += 1
        wsrc[i] = tile_k(w3[e].T, 8); i += 1
        wsrc[i] = tile_k(w2[e].T, 4); i += 1
    for u_ in range(2):
        wsrc[i] = tile_k(sw1.T[:, u_ * 512:(u_ + 1) * 512], 8); i += 1
        wsrc[i] = tile_k(sw3.T[:, u_ * 512:(u_ + 1) * 512], 8); i += 1
        wsrc[i] = tile_k(sw2.T[u_ * 512:(u_ + 1) * 512], 4); i += 1
    shared = {
        "wqT": tile_t(C(wq.T)), "wkT": tile_t(C(wk.T)), "wvT": tile_t(C(wv.T)),
        "wkeT": tile_t(C(wke.T)), "wveT": tile_t(C(wve.T)),
        "woT": tile_t(C(wo.T)),
        "gwTm": tile_t(C(gw.T)), "ebr": C(eb[None, :]),
        "wsrc": C(wsrc.reshape(54 * 128, 4096)),
        "b2m": C(np.concatenate([b2, sb2[None, :]], axis=0)),
    }
    b1all = np.concatenate([b1, sb1.reshape(2, 512)], axis=0)
    b3all = np.concatenate([b3, sb3.reshape(2, 512)], axis=0)
    shared["b1c"] = C(b1all.reshape(E_UNITS, 4, 128).transpose(2, 0, 1)
                      .reshape(128, E_UNITS * 4))
    shared["b3c"] = C(b3all.reshape(E_UNITS, 4, 128).transpose(2, 0, 1)
                      .reshape(128, E_UNITS * 4))
    sc = np.zeros((128, 1), f32)
    sc[0:16] = 0.25
    sc[32:48] = 1.0 / 32.0
    sc[48] = 1.0
    sc[64:80] = 0.25
    sc[96:112] = 1.0 / 32.0
    sc[112] = 1.0
    shared["scol"] = sc

    g_tok0 = [0] + [512 * g for g in range(12)] + [PRE + 512 * g for g in range(4)]
    in_maps = []
    for core in range(8):
        b, c = core // 4, core % 4
        xb = x[b].T
        xTc = np.concatenate([xb[:, :PRE], xb[:, c * T:(c + 1) * T]], axis=1)
        xtt = tile_k(xTc, DK).reshape(128, DK, PRE + T)
        xgt = np.empty((128, 17, 4096), f32)
        for gi, t0 in enumerate(g_tok0):
            xgt[:, gi] = xtt[:, :, t0:t0 + 512].reshape(128, DK * 512)
        xot = C(xtt[:, :, PRE:].reshape(128, DK * T))
        m = np.zeros((128, NCH), f32)
        lim = c * T
        for j in range(NCH):
            base = j * 128
            if base + 128 <= lim:
                m[:, j] = 1.0
            elif base < lim:
                m[0:lim - base, j] = 1.0
        d = dict(shared)
        d["xgt"] = C(xgt.reshape(128, 17 * 4096))
        d["xot"] = xot
        d["mask"] = m
        in_maps.append(d)

    nc = _get_nc()
    res = run_bass_kernel_spmd(nc, in_maps, core_ids=list(range(8)))
    y = np.empty((B_, N_, D), f32)
    for core in range(8):
        b, c = core // 4, core % 4
        y[b, c * T:(c + 1) * T, :] = res.results[core]["out"].T
    return y


# version 11
# speedup vs baseline: 1.0859x; 1.0859x over previous
import contextlib
import os
import numpy as np
import ml_dtypes
import concourse.bass as bass
import concourse.mybir as mybir
from concourse.bass_utils import run_bass_kernel_spmd

F32 = mybir.dt.float32
BF16 = mybir.dt.bfloat16
AF = mybir.ActivationFunctionType
ALU = mybir.AluOpType
AXX = mybir.AxisListType.X

B_, N_, D = 2, 8192, 1024
H, FD, HD = 16, 16, 64
T = 2048
PRE = 6144
NCH = PRE // 128
DK = D // 128
NPAIR = 120
RW = 49
E_UNITS = 18
DFF = 512
LN_EPS = 1e-5
ATT_EPS = 1e-12
GATE_EPS = 1e-10
NEG_BIG = -1e30


class Sem:
    def __init__(self, stack, nc, name):
        self.h = stack.enter_context(nc.semaphore(name))
        self.n = 0

    def inc(self, inst, k=1):
        assert inst is not None
        inst.then_inc(self.h, k)
        self.n += k


def build_program():
    nc = bass.Bass("TRN2", target_bir_lowering=False)

    def _w(eng, sem, n=None):
        v = sem.n if n is None else n
        assert v <= sem.n, f"future wait {v} > {sem.n}"
        if v > 0:
            eng.wait_ge(sem.h, v)

    xg_d = nc.dram_tensor("xgt", [128, 17 * 4096], F32, kind="ExternalInput")
    xo_d = nc.dram_tensor("xot", [128, DK * T], F32, kind="ExternalInput")
    mask_d = nc.dram_tensor("mask", [128, NCH], F32, kind="ExternalInput")
    scol_d = nc.dram_tensor("scol", [128, 1], F32, kind="ExternalInput")
    wq_d = nc.dram_tensor("wqT", [128, DK * 256], F32, kind="ExternalInput")
    wk_d = nc.dram_tensor("wkT", [128, DK * 256], F32, kind="ExternalInput")
    wv_d = nc.dram_tensor("wvT", [128, DK * 1024], F32, kind="ExternalInput")
    wke_d = nc.dram_tensor("wkeT", [128, DK * 256], F32, kind="ExternalInput")
    wve_d = nc.dram_tensor("wveT", [128, DK * 1024], F32, kind="ExternalInput")
    wo_d = nc.dram_tensor("woT", [128, DK * 1024], F32, kind="ExternalInput")
    gw_d = nc.dram_tensor("gwTm", [128, DK * 16], F32, kind="ExternalInput")
    ebr_d = nc.dram_tensor("ebr", [1, 16], F32, kind="ExternalInput")
    wsrc_d = nc.dram_tensor("wsrc", [54 * 128, 4096], F32, kind="ExternalInput")
    b1c_d = nc.dram_tensor("b1c", [128, 72], F32, kind="ExternalInput")
    b3c_d = nc.dram_tensor("b3c", [128, 72], F32, kind="ExternalInput")
    b2_d = nc.dram_tensor("b2m", [17, D], F32, kind="ExternalInput")
    out_d = nc.dram_tensor("out", [D, T], F32, kind="ExternalOutput")
    wsc = nc.dram_tensor("wsc", [E_UNITS * 3 * 128, 4096], BF16, kind="Internal")

    st = contextlib.ExitStack()
    with st:
        def sb(name, shape, dt):
            return st.enter_context(nc.sbuf_tensor(name, shape, dt))

        x_res = sb("x_res", [128, DK * T], F32)
        ones_rw = sb("ones_rw", [1, 256], BF16)
        onesmat = sb("onesmat", [128, 128], BF16)
        ident = sb("ident", [128, 128], BF16)
        tril = sb("tril", [128, 128], BF16)
        onescol = sb("onescol", [128, 1], F32)
        onescol_bf = sb("onescol_bf", [128, 1], BF16)
        eb16 = sb("eb16", [128, 16], F32)
        scol = sb("scol_s", [128, 1], F32)
        mcol = sb("mcol", [128, NCH], F32)
        mcol_bf = sb("mcol_bf", [128, NCH], BF16)
        ebrow = sb("ebrow", [1, 16], BF16)
        cvals = sb("cvals", [128, 2], F32)
        wq_sb = sb("wq_sb", [128, DK * 256], BF16)
        wk_sb = sb("wk_sb", [128, DK * 256], BF16)
        wv_sb = sb("wv_sb", [128, DK * 1024], BF16)
        wo_sb = sb("wo_sb", [128, DK * 1024], BF16)
        gw_sb = sb("gw_sb", [128, DK * 16], BF16)
        xhx = sb("xhx", [128, DK * 512], BF16)
        mb_s = sb("mb_s", [128, 512], F32)
        var_s = sb("var_s", [128, 512], F32)
        wu1 = [sb(f"wu1_{i}", [128, 4096], BF16) for i in range(2)]
        wu3 = [sb(f"wu3_{i}", [128, 4096], BF16) for i in range(2)]
        wu2 = [sb(f"wu2_{i}", [128, 4096], BF16) for i in range(2)]
        ktok = sb("ktok", [128, 256], BF16)
        qq = sb("qq", [128, H * 32], BF16)
        kk = sb("kk", [128, H * 32], BF16)
        qr_tok = sb("qr_tok", [128, 8 * 128], BF16)
        kr_tok = sb("kr_tok", [128, 8 * 128], BF16)
        vp = sb("vp", [128, H * 65], BF16)
        qr_feat = sb("qr_feat", [128, H * 128], BF16)
        kr_feat = sb("kr_feat", [128, H * 128], BF16)
        sbp = sb("sbp", [128, H * 64], BF16)
        sbpd = sb("sbpd", [128, 16], BF16)
        sbr = sb("sbr", [128, H * 65], BF16)
        yp = sb("yp", [128, 8 * 128], BF16)
        ztmp = sb("ztmp", [1, 1024], F32)
        zr = sb("zr", [1, 512], BF16)
        ftmp = sb("ftmp", [128, 2 * 512], BF16)
        zb_sb = sb("zb_sb", [128, 512], BF16)
        xb_g = wu1[1]
        qp_tok_t, qp_tok_o = wu3[1], 0
        kp_tok_t, kp_tok_o = wu3[1], 1920
        qtok_t, qtok_o = wu3[1], 3840
        pt_t, pt_o = wu2[1], 0
        qp_feat_t, qp_feat_o = wu2[1], 2048
        gwt = sb("gwt", [32, T], BF16)
        glog = sb("glog", [128, 256], F32)
        gt0 = sb("gt0", [128, 256], F32)
        gt1 = sb("gt1", [128, 256], F32)
        gt2 = sb("gt2", [128, 256], F32)
        gm = sb("gm", [128, 64], F32)
        b1c = sb("b1c_s", [128, 72], F32)
        b3c = sb("b3c_s", [128, 72], F32)

        h_t = kr_feat
        sa_t = qr_tok
        tb_t = kr_tok
        gb_t, gb_o = ktok, 0
        b2_sb = yp
        sb2_sb_row = 32
        gchb_t, gchb_o = qq, 0

        def sel_ap(u):
            if u < 16:
                return qr_feat[0:32, u * 128:(u + 1) * 128]
            return sbr[0:32, (u - 16) * 128:(u - 15) * 128]

        ps_s = [st.enter_context(nc.psum_tensor(f"ps_s{i}", [128, 512], F32))
                for i in range(4)]
        ps_t0 = st.enter_context(nc.psum_tensor("ps_t0", [128, 512], F32))
        ps_t1 = st.enter_context(nc.psum_tensor("ps_t1", [128, 512], F32))
        ps_t2 = st.enter_context(nc.psum_tensor("ps_t2", [128, 512], F32))
        ps_tr = st.enter_context(nc.psum_tensor("ps_tr", [128, 1024], BF16))

        def pairs_v_rg(hh):
            return ps_s[hh // 8], (hh % 8) * 64

        def rest_rg(p):
            return (ps_s[2], 65 * p) if p < 7 else (ps_s[3], 0)

        SEMS = ["sx", "swl", "scv", "ssc", "sfu", "sst",
                "pe_stat", "pe_proj", "pe_v", "pe_tr", "pe_g", "pe_y",
                "pe_zb", "pe_wout", "pe_state", "pe_gate", "pe_gb",
                "pe_ab", "pe_w2", "pe_misc",
                "dv_xb", "dv_row", "dv_xh", "dv_eqk", "dv_bld",
                "dv_evtr", "dv_cast", "dv_f", "dv_z", "dv_evy", "dv_wadd",
                "dv_gate", "dv_hev", "dv_oadd", "dv_init", "dv_gb", "dv_ms",
                "gp_sel", "ac_row", "ac_silu", "ac_f"]
        S = {nm: Sem(st, nc, nm) for nm in SEMS}

        blk = st.enter_context(nc.Block())

        @blk.sync
        def _(sy):
            sy.dma_start(mcol[:, :], mask_d.ap()).then_inc(S["sx"].h, 16)
            sy.dma_start(scol[:, :], scol_d.ap()).then_inc(S["sx"].h, 16)
            S["sx"].n += 32
            sy.dma_start(x_res[:, :], xo_d.ap()).then_inc(S["sx"].h, 16)
            S["sx"].n += 16

        @blk.gpsimd
        def _(g):
            g.memset(ones_rw[:, :], 1.0)
            g.memset(onesmat[:, :], 1.0)
            g.memset(onescol[:, :], 1.0)
            g.memset(onescol_bf[:, :], 1.0)
            g.memset(ident[:, :], 1.0)
            g.affine_select(ident[:, :], ident[:, :], [[1, 128]],
                            ALU.is_equal, 0.0, base=0, channel_multiplier=-1)
            g.memset(tril[:, :], 1.0)
            g.affine_select(tril[:, :], tril[:, :], [[1, 128]],
                            ALU.is_ge, 0.0, base=0, channel_multiplier=-1)
            g.memset(qr_tok[:, :], 0.0)
            g.memset(kr_tok[:, :], 0.0)
            g.memset(bass.AP(qr_tok, 48, [[8 * 128, 128], [64, 16], [1, 1]]), 1.0)
            g.memset(cvals[:, 0:1], LN_EPS)
            g.memset(cvals[:, 1:2], 0.125)
            g.memset(gwt[:, :], 1.0)
            g.affine_select(gwt[0:32, :], gwt[0:32, :], [[0, T]],
                            ALU.is_ge, 0.0, base=-16, channel_multiplier=1)
            g.affine_select(gwt[0:32, :], gwt[0:32, :], [[0, T]],
                            ALU.is_ge, 0.0, base=17, channel_multiplier=-1)
            inst = g.memset(vp[:, :], 0.0)
            S["dv_init"].inc(inst)
            g.dma_start(mcol_bf[:, :], mask_d.ap()).then_inc(S["swl"].h, 16)
            g.dma_start(ebrow[:, :], ebr_d.ap()).then_inc(S["swl"].h, 16)
            S["swl"].n += 32

            def wload(dst_sb, src_d, ncols):
                g.dma_start(dst_sb[:, :], src_d.ap()).then_inc(S["swl"].h, 16)
                S["swl"].n += 16

            wload(wq_sb, wq_d, 256)
            wload(wk_sb, wk_d, 256)
            wload(wv_sb, wv_d, 1024)
            wload(wo_sb, wo_d, 1024)
            wload(gw_sb, gw_d, 16)
            g.dma_start(b1c[:, :], b1c_d.ap()).then_inc(S["swl"].h, 16)
            g.dma_start(b3c[:, :], b3c_d.ap()).then_inc(S["swl"].h, 16)
            S["swl"].n += 32
            g.dma_start(wu2[0][:, 0:DK * 256], wke_d.ap()).then_inc(S["swl"].h, 16)
            src = bass.AP(wve_d, 0, [[DK * 1024, 128], [1, 4096]])
            g.dma_start(wu1[0][:, :], src).then_inc(S["swl"].h, 16)
            src = bass.AP(wve_d, 4096, [[DK * 1024, 128], [1, 4096]])
            g.dma_start(wu3[0][:, :], src).then_inc(S["swl"].h, 16)
            S["swl"].n += 48

        def wke_ap(dk):
            return wu2[0][:, dk * 256:(dk + 1) * 256]

        def wve_ap(dk, half):
            tt = wu1[0] if dk < 4 else wu3[0]
            return tt[:, (dk % 4) * 1024 + half * 512:(dk % 4) * 1024 + half * 512 + 512]

        @blk.vector
        def _(v):
            for i in range(4):
                v.memset(ps_s[i][:, :], 0.0)
            inst = v.memset(ps_t2[:, :], 0.0)
            S["dv_init"].inc(inst)

        @blk.tensor
        def _(t):
            _w(t, S["swl"])
            _w(t, S["dv_init"])
            inst = t.matmul(ps_t0[0:128, 0:16], ones_rw[0:1, 0:128],
                            ebrow[0:1, 0:16], start=True, stop=True)
            S["pe_misc"].inc(inst)

        @blk.vector
        def _(v):
            _w(v, S["pe_misc"])
            inst = v.tensor_copy(eb16[:, :], ps_t0[0:128, 0:16])
            S["dv_init"].inc(inst)

        NCONV = 54
        conv_stage = [wu1[0], wu3[0], wu2[0]]

        def _conv_src(i):
            return bass.AP(wsrc_d, i * 128 * 4096, [[4096, 128], [1, 4096]])

        conv_emitted = [0]

        def emit_conv(k):
            n0 = conv_emitted[0]
            n1 = min(NCONV, n0 + k)
            if n1 == n0:
                return
            conv_emitted[0] = n1

            @blk.gpsimd
            def _(g):
                for i in range(n0, n1):
                    buf = conv_stage[i % 3]
                    if i == 0:
                        g.wait_ge(S["pe_state"].h, 4)
                    if i >= 3:
                        g.wait_ge(S["ssc"].h, (i - 3 + 1) * 16)
                    g.dma_start(buf[:, :], _conv_src(i)).then_inc(
                        S["scv"].h, 16)
                    S["scv"].n += 16

            @blk.sync
            def _(sy):
                for i in range(n0, n1):
                    buf = conv_stage[i % 3]
                    sy.wait_ge(S["scv"].h, (i + 1) * 16)
                    dst = bass.AP(wsc, i * 128 * 4096, [[4096, 128], [1, 4096]])
                    sy.dma_start(dst, buf[:, :]).then_inc(S["ssc"].h, 16)
                    S["ssc"].n += 16

        def ln_group(tok0, from_res, first=False, gidx=0):
            @blk.gpsimd
            def _(g):
                if not from_res:
                    if S["pe_stat"].n > 0:
                        g.wait_ge(S["pe_stat"].h, S["pe_stat"].n)
                    if S["dv_xh"].n > 0:
                        g.wait_ge(S["dv_xh"].h, S["dv_xh"].n)
                    src = bass.AP(xg_d, gidx * 4096,
                                  [[17 * 4096, 128], [1, 4096]])
                    g.dma_start(xb_g[:, :], src).then_inc(S["sx"].h, 16)
                    S["sx"].n += 16

            @blk.vector
            def _(v):
                if from_res:
                    _w(v, S["pe_stat"])
                    src = bass.AP(x_res, tok0, [[DK * T, 128], [T, DK], [1, 512]])
                    v.tensor_copy(xb_g[:, :], src)
                else:
                    _w(v, S["sx"])
                _w(v, S["pe_v"])
                _w(v, S["pe_gate"])
                inst = v.tensor_tensor(xhx[:, :], xb_g[:, :], xb_g[:, :],
                                       ALU.mult)
                S["dv_xb"].inc(inst)

            @blk.tensor
            def _(t):
                _w(t, S["dv_xb"])
                if first:
                    _w(t, S["dv_init"])
                _w(t, S["dv_eqk"])
                _w(t, S["dv_row"])
                _w(t, S["dv_wadd"])
                _w(t, S["dv_bld"])
                _w(t, S["dv_evy"])
                for dk in range(DK):
                    t.matmul(ps_t0[0:128, 0:512], onesmat[:, :],
                             xb_g[:, dk * 512:(dk + 1) * 512],
                             start=(dk == 0), stop=(dk == DK - 1))
                for dk in range(DK):
                    inst = t.matmul(ps_t1[0:128, 0:512], onesmat[:, :],
                                    xhx[:, dk * 512:(dk + 1) * 512],
                                    start=(dk == 0), stop=(dk == DK - 1))
                S["pe_stat"].inc(inst)

            @blk.vector
            def _(v):
                _w(v, S["pe_stat"])
                v.tensor_scalar(mb_s[:, :], ps_t0[0:128, 0:512],
                                1.0 / D, None, ALU.mult)
                v.tensor_tensor(var_s[:, :], mb_s[:, :], mb_s[:, :], ALU.mult)
                inst = v.scalar_tensor_tensor(var_s[:, :], ps_t1[0:128, 0:512],
                                              1.0 / D, var_s[:, :],
                                              ALU.mult, ALU.subtract)
                S["dv_row"].inc(inst)

            @blk.scalar
            def _(s):
                _w(s, S["dv_row"])
                _w(s, S["dv_init"])
                inst = s.activation(var_s[:, :], var_s[:, :], AF.Sqrt,
                                    bias=cvals[:, 0:1], scale=1.0)
                S["ac_row"].inc(inst)

            @blk.vector
            def _(v):
                _w(v, S["ac_row"])
                v.reciprocal(var_s[:, :], var_s[:, :])
                mb_b = bass.AP(mb_s, 0, [[512, 128], [0, DK], [1, 512]])
                rb_b = bass.AP(var_s, 0, [[512, 128], [0, DK], [1, 512]])
                xb3 = bass.AP(xb_g, 0, [[DK * 512, 128], [512, DK], [1, 512]])
                if from_res:
                    _w(v, S["pe_wout"])
                    xh3a = xhat2_ap3(tok0)
                    t3 = bass.AP(xhx, 0, [[DK * 512, 128], [512, DK], [1, 512]])
                    v.tensor_tensor(t3, xb3, mb_b, ALU.subtract)
                    ta = bass.AP(xhx, 0, [[DK * 512, 128], [512, 4], [1, 512]])
                    tb = bass.AP(xhx, 4 * 512, [[DK * 512, 128], [512, 4], [1, 512]])
                    ra = bass.AP(var_s, 0, [[512, 128], [0, 4], [1, 512]])
                    v.tensor_tensor(xh3a[0], ta, ra, ALU.mult)
                    inst = v.tensor_tensor(xh3a[1], tb, ra, ALU.mult)
                else:
                    xh3 = bass.AP(xhx, 0, [[DK * 512, 128], [512, DK], [1, 512]])
                    v.tensor_tensor(xh3, xb3, mb_b, ALU.subtract)
                    inst = v.tensor_tensor(xh3, xh3, rb_b, ALU.mult)
                S["dv_xh"].inc(inst)

        def xhat2_ap(dk, c0, w):
            tt = wv_sb if dk < 4 else wo_sb
            o = (dk % 4) * T + c0
            return tt[:, o:o + w]

        def xhat2_ap3(tok0):
            a = bass.AP(wv_sb, tok0, [[4 * T, 128], [T, 4], [1, 512]])
            b = bass.AP(wo_sb, tok0, [[4 * T, 128], [T, 4], [1, 512]])
            return a, b

        def attn_chunk(kind, tokabs, cidx):
            is_own = kind == "own"
            if kind == "pre":
                jm, mct, mcp = tokabs // 128, mcol, NCH
                mc = mcol[:, jm:jm + 1]
                mcb = mcol_bf[:, jm:jm + 1]
            else:
                jm, mct, mcp = 0, onescol, 1
                mc = onescol[:, 0:1]
                mcb = onescol_bf[:, 0:1]

            def xh(dk):
                c = dk * 512 + cidx * 128
                return xhx[:, c:c + 128]

            def wk_ap(dk):
                if kind == "enc":
                    return wke_ap(dk)
                return wk_sb[:, dk * 256:(dk + 1) * 256]

            def wv_ap(dk, half):
                if kind == "enc":
                    return wve_ap(dk, half)
                return wv_sb[:, dk * 1024 + half * 512:dk * 1024 + half * 512 + 512]

            @blk.tensor
            def _(t):
                _w(t, S["dv_xh"])
                _w(t, S["dv_eqk"])
                _w(t, S["dv_wadd"])
                _w(t, S["dv_row"])
                for dk in range(DK):
                    t.matmul(ps_t0[0:128, 0:256], xh(dk), wk_ap(dk),
                             start=(dk == 0), stop=(dk == DK - 1))
            if pending_state[0] is not None:
                pending_state[0]()
                pending_state[0] = None

            @blk.tensor
            def _(t):
                if is_own:
                    for dk in range(DK):
                        t.matmul(ps_t0[0:128, 256:512], xh(dk),
                                 wq_sb[:, dk * 256:(dk + 1) * 256],
                                 start=(dk == 0), stop=(dk == DK - 1))
                _w(t, S["dv_bld"])
                _w(t, S["dv_evy"])
                for dk in range(DK):
                    inst = t.matmul(ps_t1[0:128, 0:512], xh(dk), wv_ap(dk, 0),
                                    start=(dk == 0), stop=(dk == DK - 1))
                S["pe_proj"].inc(inst)

            @blk.vector
            def _(v):
                _w(v, S["pe_proj"])
                _w(v, S["pe_state"])
                _w(v, S["pe_y"])
                _w(v, S["sx"], 48)
                v.tensor_copy(ktok[:, :], ps_t0[0:128, 0:256])
                if is_own:
                    v.tensor_copy(bass.AP(qtok_t, qtok_o, [[4096, 128], [1, 256]]),
                                  ps_t0[0:128, 256:512])
                ksrc = bass.AP(ktok, 0, [[256, 128], [16, H], [1, 16]])
                v.tensor_copy(bass.AP(kk, 0, [[H * 32, 128], [32, H], [1, 16]]),
                              ksrc)
                v.tensor_copy(bass.AP(kk, 16, [[H * 32, 128], [32, H], [1, 16]]),
                              ksrc)
                if is_own:
                    qsrc = bass.AP(qtok_t, qtok_o, [[4096, 128], [16, H], [1, 16]])
                    v.tensor_copy(
                        bass.AP(qq, 0, [[H * 32, 128], [32, H], [1, 16]]), qsrc)
                    v.tensor_copy(
                        bass.AP(qq, 16, [[H * 32, 128], [32, H], [1, 16]]), qsrc)
                dstA = bass.AP(vp, 0, [[H * 65, 128], [65, 8], [1, 64]])
                srcA = bass.AP(ps_t1, 0, [[512, 128], [64, 8], [1, 64]])
                v.tensor_scalar(dstA, srcA, mc, None, ALU.mult)
                dstO = bass.AP(vp, 64, [[H * 65, 128], [65, 8], [1, 1]])
                srcO = bass.AP(mct, jm, [[mcp, 128], [0, 8], [1, 1]])
                inst = v.tensor_copy(dstO, srcO)
                S["dv_eqk"].inc(inst)

            @blk.tensor
            def _(t):
                _w(t, S["dv_eqk"])
                for dk in range(DK):
                    inst = t.matmul(ps_t1[0:128, 0:512], xh(dk), wv_ap(dk, 1),
                                    start=(dk == 0), stop=(dk == DK - 1))
                S["pe_v"].inc(inst)

            @blk.vector
            def _(v):
                _w(v, S["pe_v"])
                _w(v, S["pe_tr"])
                dstB = bass.AP(vp, 8 * 65, [[H * 65, 128], [65, 8], [1, 64]])
                srcB = bass.AP(ps_t1, 0, [[512, 128], [64, 8], [1, 64]])
                v.tensor_scalar(dstB, srcB, mc, None, ALU.mult)
                dstO = bass.AP(vp, 8 * 65 + 64, [[H * 65, 128], [65, 8], [1, 1]])
                srcO = bass.AP(mct, jm, [[mcp, 128], [0, 8], [1, 1]])
                v.tensor_copy(dstO, srcO)

                def build(pt_tt, pt_oo, rtile, dup, tok_ap):
                    for dd in range(1, 8):
                        i0 = bass.AP(dup, 0, [[H * 32, 128], [32, H], [1, 16]])
                        i1 = bass.AP(dup, dd, [[H * 32, 128], [32, H], [1, 16]])
                        oo = bass.AP(pt_tt, pt_oo + (dd - 1) * 16,
                                     [[4096 if pt_tt in (wu3[1],) else H * NPAIR, 128],
                                      [NPAIR, H], [1, 16]])
                        v.tensor_tensor(oo, i0, i1, ALU.mult)
                    i0 = bass.AP(dup, 0, [[H * 32, 128], [32, H], [1, 8]])
                    i1 = bass.AP(dup, 8, [[H * 32, 128], [32, H], [1, 8]])
                    oo = bass.AP(pt_tt, pt_oo + 112,
                                 [[4096 if pt_tt in (wu3[1],) else H * NPAIR, 128],
                                  [NPAIR, H], [1, 8]])
                    v.tensor_tensor(oo, i0, i1, ALU.mult)
                    src = bass.AP(tok_ap[0], tok_ap[1],
                                  [[tok_ap[2], 128], [32, 8], [16, 2], [1, 16]])
                    dl = bass.AP(rtile, 0,
                                 [[8 * 128, 128], [128, 8], [64, 2], [1, 16]])
                    v.tensor_copy(dl, src)
                    dd2 = bass.AP(rtile, 32,
                                  [[8 * 128, 128], [128, 8], [64, 2], [1, 16]])
                    return v.tensor_tensor(dd2, src, src, ALU.mult)

                inst = build(kp_tok_t, kp_tok_o, kr_tok, kk, (ktok, 0, 256))
                dm = bass.AP(kr_tok, 48, [[8 * 128, 128], [128, 8], [64, 2], [1, 1]])
                sm = bass.AP(mct, jm, [[mcp, 128], [0, 8], [0, 2], [1, 1]])
                inst = v.tensor_copy(dm, sm)
                if is_own:
                    inst = build(qp_tok_t, qp_tok_o, qr_tok, qq,
                                 (qtok_t, qtok_o, 4096))
                S["dv_bld"].inc(inst)

            if is_own and OWNLVL >= 1:
                @blk.vector
                def _(v):
                    _w(v, S["pe_state"])
                    _w(v, S["pe_y"])
                    v.tensor_scalar(sbp[0:120, 0:512], ps_s[0][0:120, 0:512],
                                    1.0 / 16.0, None, ALU.mult)
                    v.tensor_scalar(sbp[0:120, 512:1024], ps_s[1][0:120, 0:512],
                                    1.0 / 16.0, None, ALU.mult)
                    v.tensor_scalar(sbpd[0:120, 0:16], ps_s[2][0:120, 455:471],
                                    1.0 / 16.0, None, ALU.mult)
                    ev_out = bass.AP(sbr, 0, [[H * 65, RW], [130, 7], [1, 65]])
                    ev_in = ps_s[2][0:RW, 0:455].rearrange(
                        "p (r c) -> p r c", r=7)
                    v.tensor_scalar(ev_out, ev_in, scol[0:RW, 0:1], None,
                                    ALU.mult)
                    od_out = bass.AP(sbr, 65, [[H * 65, RW], [130, 7], [1, 65]])
                    od_in = ps_s[2][64:64 + RW, 0:455].rearrange(
                        "p (r c) -> p r c", r=7)
                    v.tensor_scalar(od_out, od_in, scol[0:RW, 0:1], None,
                                    ALU.mult)
                    v.tensor_scalar(sbr[0:RW, 14 * 65:14 * 65 + 65],
                                    ps_s[3][0:RW, 0:65],
                                    scol[0:RW, 0:1], None, ALU.mult)
                    inst = v.tensor_scalar(sbr[0:RW, 15 * 65:15 * 65 + 65],
                                           ps_s[3][64:64 + RW, 0:65],
                                           scol[0:RW, 0:1], None, ALU.mult)
                    S["dv_cast"].inc(inst)

                pass
            if is_own and OWNLVL >= 2:
                def rto(hh):
                    return (hh // 2) * 128 + (hh % 2) * 64

                tr_specs = [
                    ([(qp_tok_t, qp_tok_o + hh * NPAIR, NPAIR) for hh in range(8)],
                     (qp_feat_t, qp_feat_o + 0), NPAIR),
                    ([(qp_tok_t, qp_tok_o + (8 + hh) * NPAIR, NPAIR)
                      for hh in range(8)],
                     (qp_feat_t, qp_feat_o + 1024), NPAIR),
                    ([(qr_tok, rto(hh), RW) for hh in range(8)],
                     (qr_feat, 0), RW),
                    ([(qr_tok, rto(8 + hh), RW) for hh in range(8)],
                     (qr_feat, 1024), RW),
                    ([(kr_tok, rto(hh), RW) for hh in range(8)],
                     (kr_feat, 0), RW),
                    ([(kr_tok, rto(8 + hh), RW) for hh in range(8)],
                     (kr_feat, 1024), RW),
                ]
                for wv_i in range(6):
                    srcs_w, (dt_t, dt_o), rows = tr_specs[wv_i]

                    @blk.tensor
                    def _(t, srcs_w=srcs_w, rows=rows):
                        _w(t, S["dv_bld"])
                        _w(t, S["dv_evtr"])
                        for i8, (tt, o, wdt) in enumerate(srcs_w):
                            src = bass.AP(tt, o, [[tt.shape[1], 128], [1, wdt]])
                            inst = t.transpose(
                                ps_tr[0:rows, i8 * 128:(i8 + 1) * 128],
                                src, ident[:, :])
                        S["pe_tr"].inc(inst)

                    @blk.vector
                    def _(v, dt_t=dt_t, dt_o=dt_o, rows=rows):
                        _w(v, S["pe_tr"])
                        _w(v, S["pe_g"])
                        _w(v, S["pe_y"])
                        dst = bass.AP(dt_t, dt_o, [[dt_t.shape[1], rows], [1, 1024]])
                        inst = v.tensor_copy(dst, ps_tr[0:rows, 0:1024])
                        S["dv_evtr"].inc(inst)

                pass
            if is_own and OWNLVL >= 3:
                for wv_i in range(4):
                    @blk.tensor
                    def _(t, wv_i=wv_i):
                        _w(t, S["dv_evtr"])
                        _w(t, S["dv_f"])
                        for hh in range(4 * wv_i, 4 * wv_i + 4):
                            inst = t.matmul(
                                ps_t0[0:128, (hh % 4) * 128:(hh % 4) * 128 + 128],
                                kr_feat[0:16, hh * 128:(hh + 1) * 128],
                                qr_feat[0:16, hh * 128:(hh + 1) * 128],
                                start=True, stop=True)
                        S["pe_g"].inc(inst)

                    @blk.scalar
                    def _(s, wv_i=wv_i):
                        _w(s, S["pe_g"])
                        _w(s, S["dv_f"], max(0, S["dv_f"].n - 1))
                        fts = mb_s if wv_i % 2 == 0 else var_s
                        inst = s.activation(fts[:, :], ps_t0[0:128, 0:512],
                                            AF.Identity, bias=1.0,
                                            scale=cvals[:, 1:2])
                        S["ac_f"].inc(inst)

                    @blk.vector
                    def _(v, wv_i=wv_i):
                        _w(v, S["ac_f"])
                        _w(v, S["pe_y"])
                        fts = mb_s if wv_i % 2 == 0 else var_s
                        par = (wv_i % 2) * 512
                        v.scalar_tensor_tensor(ftmp[:, par:par + 512],
                                               ps_t0[0:128, 0:512], 0.25,
                                               fts[:, :], ALU.mult, ALU.mult)
                        trb = bass.AP(tril, 0, [[128, 128], [0, 4], [1, 128]])
                        i0 = bass.AP(ftmp, par, [[1024, 128], [128, 4], [1, 128]])
                        oo = bass.AP(pt_t, pt_o + wv_i * 512,
                                     [[4096, 128], [128, 4], [1, 128]])
                        inst = v.scalar_tensor_tensor(oo, i0, 1.0, trb,
                                                      ALU.add, ALU.mult)
                        S["dv_f"].inc(inst)

                    if OWNLVL < 4:
                        continue

                    @blk.tensor
                    def _(t, wv_i=wv_i):
                        _w(t, S["dv_f"])
                        _w(t, S["dv_cast"])
                        _w(t, S["dv_bld"])
                        _w(t, S["dv_evy"])
                        for hh in range(4 * wv_i, 4 * wv_i + 4):
                            fo = (hh % 4) * 128
                            t.matmul(ps_t1[0:65, fo:fo + 128],
                                     vp[:, hh * 65:hh * 65 + 65],
                                     bass.AP(pt_t, pt_o + hh * 128,
                                             [[4096, 128], [1, 128]]),
                                     start=True, stop=False)
                            t.matmul(ps_t1[0:64, fo:fo + 128],
                                     sbp[0:120, hh * 64:(hh + 1) * 64],
                                     bass.AP(qp_feat_t, qp_feat_o + hh * 128,
                                             [[4096, 120], [1, 128]]),
                                     start=False, stop=False)
                            t.matmul(ps_t1[64:65, fo:fo + 128],
                                     sbpd[0:120, hh:hh + 1],
                                     bass.AP(qp_feat_t, qp_feat_o + hh * 128,
                                             [[4096, 120], [1, 128]]),
                                     start=False, stop=False)
                            inst = t.matmul(
                                ps_t1[0:65, fo:fo + 128],
                                sbr[0:RW, hh * 65:hh * 65 + 65],
                                qr_feat[0:RW, hh * 128:(hh + 1) * 128],
                                start=False, stop=True)
                        S["pe_y"].inc(inst)

                    if OWNLVL < 5:
                        continue

                    @blk.vector
                    def _(v, wv_i=wv_i):
                        _w(v, S["pe_y"])
                        _w(v, S["pe_zb"])
                        v.tensor_scalar(ztmp[0:1, 0:512], ps_t1[64:65, 0:512],
                                        ATT_EPS, None, ALU.add)
                        v.reciprocal(ztmp[0:1, 512:1024], ztmp[0:1, 0:512])
                        inst = v.tensor_copy(zr[0:1, 0:512], ztmp[0:1, 512:1024])
                        S["dv_z"].inc(inst)

                    @blk.tensor
                    def _(t, wv_i=wv_i):
                        _w(t, S["dv_z"])
                        _w(t, S["dv_evy"])
                        _w(t, S["dv_wadd"])
                        inst = t.matmul(ps_t2[0:128, 0:512], ones_rw[0:1, 0:128],
                                        zr[0:1, 0:512], start=True, stop=True)
                        S["pe_zb"].inc(inst)

                    @blk.vector
                    def _(v, wv_i=wv_i):
                        _w(v, S["pe_zb"])
                        _w(v, S["pe_wout"])
                        v.tensor_copy(zb_sb[:, :], ps_t2[0:128, 0:512])
                        inst = None
                        for hh in range(4 * wv_i, 4 * wv_i + 4):
                            p, rowb = hh // 2, (hh % 2) * 64
                            fo = (hh % 4) * 128
                            inst = v.tensor_tensor(
                                yp[rowb:rowb + 64, p * 128:(p + 1) * 128],
                                ps_t1[0:64, fo:fo + 128],
                                zb_sb[rowb:rowb + 64, fo:fo + 128], ALU.mult)
                        S["dv_evy"].inc(inst)

                pass
            if is_own and OWNLVL >= 6:
                @blk.tensor
                def _(t):
                    _w(t, S["dv_evy"])
                    _w(t, S["dv_f"])
                    for m in range(4):
                        for k in range(8):
                            inst = t.matmul(
                                ps_t0[0:128, m * 128:(m + 1) * 128],
                                wo_sb[:, k * 1024 + m * 128:k * 1024 + m * 128 + 128],
                                yp[:, k * 128:(k + 1) * 128],
                                start=(k == 0), stop=(k == 7))
                    for m in range(4, 8):
                        for k in range(8):
                            inst = t.matmul(
                                ps_t2[0:128, (m - 4) * 128:(m - 3) * 128],
                                wo_sb[:, k * 1024 + m * 128:k * 1024 + m * 128 + 128],
                                yp[:, k * 128:(k + 1) * 128],
                                start=(k == 0), stop=(k == 7))
                    S["pe_wout"].inc(inst)

                @blk.vector
                def _(v):
                    _w(v, S["pe_wout"])
                    _w(v, S["sx"], 48)
                    oc = tokabs - PRE
                    o1 = bass.AP(x_res, oc, [[DK * T, 128], [T, 4], [1, 128]])
                    v.tensor_tensor(o1, o1,
                                    bass.AP(ps_t0, 0, [[512, 128], [128, 4], [1, 128]]),
                                    ALU.add)
                    o2 = bass.AP(x_res, 4 * T + oc, [[DK * T, 128], [T, 4], [1, 128]])
                    inst = v.tensor_tensor(
                        o2, o2,
                        bass.AP(ps_t2, 0, [[512, 128], [128, 4], [1, 128]]),
                        ALU.add)
                    S["dv_wadd"].inc(inst)

            def emit_state():
                @blk.tensor
                def _(t):
                    _w(t, S["dv_bld"])
                    _w(t, S["dv_init"])
                    if is_own:
                        _w(t, S["dv_cast"])
                    inst = None
                    for hh in range(H):
                        bank, off = pairs_v_rg(hh)
                        t.matmul(bank[0:NPAIR, off:off + 64],
                                 bass.AP(kp_tok_t, kp_tok_o + hh * NPAIR,
                                         [[4096, 128], [1, NPAIR]]),
                                 vp[:, hh * 65:hh * 65 + 64],
                                 start=False, stop=True)
                        t.matmul(ps_s[2][0:NPAIR, 455 + hh:456 + hh],
                                 bass.AP(kp_tok_t, kp_tok_o + hh * NPAIR,
                                         [[4096, 128], [1, NPAIR]]),
                                 mcb, start=False, stop=True)
                    for p in range(8):
                        bank, off = rest_rg(p)
                        for half in range(2):
                            hh = 2 * p + half
                            base = half * 64
                            inst = t.matmul(
                                bank[base:base + RW, off:off + 65],
                                kr_tok[:, p * 128 + base:p * 128 + base + RW],
                                vp[:, hh * 65:hh * 65 + 65],
                                start=False, stop=True)
                    S["pe_state"].inc(inst)
            return emit_state

        OWNLVL = int(os.environ.get("KMK_OWNLVL", "6"))
        PHASE = os.environ.get("KMK_PHASE", "full")
        NGRP = int(os.environ.get("KMK_GROUPS", "17"))
        DOCONV = os.environ.get("KMK_CONV", "1") == "1" and PHASE != "attn_noconv"
        groups = [("enc", 0)]
        groups += [("pre", 512 * g) for g in range(12)]
        groups += [("own", PRE + 512 * g) for g in range(4)]
        groups = groups[:NGRP]
        first = True
        pending_state = [None]
        for gi, (kind, tok0) in enumerate(groups):
            ln_group(tok0, from_res=False, first=first, gidx=gi)
            first = False
            for cidx in range(4):
                pending_state[0] = attn_chunk(kind, tok0 + cidx * 128, cidx)
            if kind != "enc" and DOCONV:
                emit_conv(4)
        if pending_state[0] is not None:
            pending_state[0]()
            pending_state[0] = None
        if DOCONV:
            emit_conv(NCONV)
        if PHASE in ("attn", "attn_noconv"):
            @blk.sync
            def _(sy):
                _w(sy, S["dv_wadd"])
                if S["ssc"].n > 0:
                    _w(sy, S["ssc"])
                src = bass.AP(x_res, 0, [[DK * T, 128], [T, DK], [1, T]])
                dst = bass.AP(out_d, 0, [[T, 128], [128 * T, DK], [1, T]])
                sy.dma_start(dst, src).then_inc(S["sst"].h, 16)
                S["sst"].n += 16
                sy.wait_ge(S["sst"].h, S["sst"].n)

            @blk.tensor
            def _(t):
                _w(t, S["pe_state"])

            @blk.vector
            def _(v):
                _w(v, S["dv_bld"])
            return nc

        @blk.gpsimd
        def _(g):
            g.wait_ge(S["pe_state"].h, 68)
            g.wait_ge(S["pe_wout"].h, 16)
            g.dma_start(yp[0:17, 0:D], b2_d.ap()).then_inc(S["swl"].h, 16)
            S["swl"].n += 16
            g.memset(qr_feat[0:32, 0:2048], 1.0)
            g.affine_select(
                bass.AP(qr_feat, 0, [[H * 128, 32], [128, 16], [1, 128]]),
                bass.AP(qr_feat, 0, [[H * 128, 32], [128, 16], [1, 128]]),
                [[128, 16], [0, 128]], ALU.is_equal, 0.0,
                base=0, channel_multiplier=-128)
            g.memset(sbr[0:32, 0:256], 1.0)
            inst = g.affine_select(
                bass.AP(sbr, 0, [[H * 65, 32], [128, 2], [1, 128]]),
                bass.AP(sbr, 0, [[H * 65, 32], [128, 2], [1, 128]]),
                [[128, 2], [0, 128]], ALU.is_equal, 0.0,
                base=16 * 128, channel_multiplier=-128)
            S["gp_sel"].inc(inst)

        def gate_logits(ch):
            par = (ch % 2) * 16

            @blk.tensor
            def _(t):
                _w(t, S["dv_xh"])
                _w(t, S["dv_gate"])
                for dk in range(DK):
                    inst = t.matmul(ps_t0[0:128, par:par + 16],
                                    xhat2_ap(dk, ch * 128, 128),
                                    gw_sb[:, dk * 16:(dk + 1) * 16],
                                    start=(dk == 0), stop=(dk == DK - 1))
                S["pe_gate"].inc(inst)

            @blk.vector
            def _(v):
                _w(v, S["pe_gate"])
                inst = v.tensor_copy(glog[:, ch * 16:(ch + 1) * 16],
                                     ps_t0[0:128, par:par + 16])
                S["dv_gate"].inc(inst)

        for g4 in range(4):
            ln_group(g4 * 512, from_res=True)
            for ci in range(4):
                gate_logits(g4 * 4 + ci)

        @blk.scalar
        def _(s_):
            _w(s_, S["dv_gate"])
            inst = s_.activation(gt1[:, :], glog[:, :], AF.Sigmoid)
            S["ac_silu"].inc(inst)

        @blk.vector
        def _(v):
            _w(v, S["ac_silu"])
            ebb = bass.AP(eb16, 0, [[16, 128], [0, 16], [1, 16]])
            g3 = lambda tt: bass.AP(tt, 0, [[256, 128], [16, 16], [1, 16]])
            v.tensor_tensor(g3(gt0), g3(glog), ebb, ALU.add)
            v.drain()
            v.reduce_max(bass.AP(gm, 0, [[64, 128], [1, 16], [0, 1]]),
                         g3(gt0), axis=AXX)
            v.drain()
            m1b = bass.AP(gm, 0, [[64, 128], [1, 16], [0, 16]])
            v.tensor_tensor(g3(gt2), g3(gt0), m1b, ALU.is_ge)
            v.drain()
            v.scalar_tensor_tensor(gt2[:, :], gt2[:, :], NEG_BIG, gt0[:, :],
                                   ALU.mult, ALU.add)
            v.drain()
            v.reduce_max(bass.AP(gm, 16, [[64, 128], [1, 16], [0, 1]]),
                         g3(gt2), axis=AXX)
            v.drain()
            m2b = bass.AP(gm, 16, [[64, 128], [1, 16], [0, 16]])
            v.tensor_tensor(g3(gt2), g3(gt0), m2b, ALU.is_ge)
            v.drain()
            v.tensor_tensor(gt1[:, :], gt1[:, :], gt2[:, :], ALU.mult)
            v.drain()
            v.reduce_sum(bass.AP(gm, 32, [[64, 128], [1, 16], [0, 1]]),
                         g3(gt1), axis=AXX)
            v.drain()
            v.tensor_scalar(gm[:, 48:64], gm[:, 32:48], GATE_EPS, None, ALU.add)
            v.drain()
            v.reciprocal(gm[:, 32:48], gm[:, 48:64])
            v.drain()
            rb2 = bass.AP(gm, 32, [[64, 128], [1, 16], [0, 16]])
            inst = v.tensor_tensor(
                bass.AP(gchb_t, gchb_o, [[gchb_t.shape[1], 128], [16, 16], [1, 16]]),
                g3(gt1), rb2, ALU.mult)
            v.drain()
            S["dv_gate"].inc(inst)

        for ch in range(16):
            @blk.tensor
            def _(t, ch=ch):
                _w(t, S["dv_gate"])
                _w(t, S["dv_evtr"])
                inst = t.transpose(
                    ps_tr[0:16, (ch % 8) * 128:(ch % 8) * 128 + 128],
                    bass.AP(gchb_t, gchb_o + ch * 16,
                            [[gchb_t.shape[1], 128], [1, 16]]),
                    ident[:, :])
                S["pe_tr"].inc(inst)

            @blk.vector
            def _(v, ch=ch):
                _w(v, S["pe_tr"])
                inst = v.tensor_copy(gwt[0:16, ch * 128:(ch + 1) * 128],
                                     ps_tr[0:16, (ch % 8) * 128:(ch % 8) * 128 + 128])
                S["dv_evtr"].inc(inst)

        if PHASE == "ln2":
            @blk.vector
            def _(v):
                _w(v, S["dv_gate"])
                _w(v, S["dv_xh"])
                _w(v, S["gp_sel"])
                _w(v, S["dv_evtr"])
                inst = None
                if os.environ.get("KMK_DUMP", "xhat2") == "gwt":
                    inst = v.tensor_copy(x_res[0:32, 0:T], gwt[0:32, 0:T])
                    inst = v.tensor_copy(x_res[0:32, 3 * T:3 * T + 2048],
                                         qr_feat[0:32, 0:2048])
                    inst = v.tensor_copy(
                        x_res[0:128, 2 * T:2 * T + 256],
                        bass.AP(gchb_t, gchb_o, [[gchb_t.shape[1], 128], [1, 256]]))
                else:
                    for dk in range(DK):
                        inst = v.tensor_copy(x_res[:, dk * T:(dk + 1) * T],
                                             xhat2_ap(dk, 0, T))
                S["dv_oadd"].inc(inst)

            @blk.sync
            def _(sy):
                _w(sy, S["dv_oadd"])
                _w(sy, S["ssc"])
                src = bass.AP(x_res, 0, [[DK * T, 128], [T, DK], [1, T]])
                dst = bass.AP(out_d, 0, [[T, 128], [128 * T, DK], [1, T]])
                sy.dma_start(dst, src).then_inc(S["sst"].h, 16)
                S["sst"].n += 16
                sy.wait_ge(S["sst"].h, S["sst"].n)
            return nc

        for fblk in range(8):
            @blk.vector
            def _(v):
                _w(v, S["pe_w2"])
                _w(v, S["pe_state"])
                _w(v, S["pe_y"])
                inst = None
                for i in range(4):
                    inst = v.memset(ps_s[i][:, :], 0.0)
                S["dv_ms"].inc(inst)

            @blk.tensor
            def _(t):
                _w(t, S["dv_ms"])
                _w(t, S["dv_gate"])
                _w(t, S["swl"])
                inst = None
                for dk in range(DK):
                    bank, colo = ps_s[dk // 2], (dk % 2) * 256
                    inst = t.matmul(bank[0:128, colo:colo + 256],
                                     b2_sb[0:17, dk * 128:(dk + 1) * 128],
                                     gwt[0:17, fblk * 256:(fblk + 1) * 256],
                                     start=False, stop=True)
                S["pe_misc"].inc(inst)

            for u in range(E_UNITS):
                ui = fblk * E_UNITS + u
                par = ui % 2

                @blk.sync
                def _(sy):
                    if ui >= 2:
                        sy.wait_ge(S["pe_w2"].h, ui - 1)
                    else:
                        sy.wait_ge(S["ssc"].h, NCONV * 16)
                        _w(sy, S["dv_gate"])
                        _w(sy, S["dv_xb"])
                    if fblk == 0:
                        sy.wait_ge(S["ssc"].h, min(NCONV, 3 * (u + 1)) * 16)
                    for j, buf in enumerate([wu1[par], wu3[par], wu2[par]]):
                        src = bass.AP(wsc, (u * 3 + j) * 128 * 4096,
                                      [[4096, 128], [1, 4096]])
                        sy.dma_start(buf[:, :], src).then_inc(S["sfu"].h, 16)
                        S["sfu"].n += 16

                @blk.tensor
                def _(t):
                    _w(t, S["sfu"])
                    _w(t, S["dv_gb"])
                    _w(t, S["dv_xh"])
                    _w(t, S["dv_ms"])
                    _w(t, S["gp_sel"])
                    inst = t.matmul(ps_t0[0:128, 0:256], sel_ap(u),
                                    gwt[0:32, fblk * 256:(fblk + 1) * 256],
                                    start=True, stop=True)
                    S["pe_gb"].inc(inst)

                @blk.vector
                def _(v):
                    _w(v, S["pe_gb"])
                    _w(v, S["dv_hev"])
                    inst = v.tensor_copy(bass.AP(gb_t, gb_o, [[256, 128], [1, 256]]),
                                         ps_t0[0:128, 0:256])
                    S["dv_gb"].inc(inst)

                rot = [ps_t1, ps_t2, ps_t1, ps_t2]

                def emit_ab(ci):
                    @blk.tensor
                    def _(t):
                        if ci == 0:
                            _w(t, S["dv_gb"])
                        _w(t, S["dv_hev"], max(0, S["dv_hev"].n - 1))
                        bank = rot[ci]
                        for dk in range(DK):
                            t.matmul(
                                bank[0:128, 0:256],
                                wu1[par][:, dk * 512 + ci * 128:dk * 512 + ci * 128 + 128],
                                xhat2_ap(dk, fblk * 256, 256),
                                start=(dk == 0), stop=(dk == DK - 1))
                        for dk in range(DK):
                            inst = t.matmul(
                                bank[0:128, 256:512],
                                wu3[par][:, dk * 512 + ci * 128:dk * 512 + ci * 128 + 128],
                                xhat2_ap(dk, fblk * 256, 256),
                                start=(dk == 0), stop=(dk == DK - 1))
                        S["pe_ab"].inc(inst)

                    @blk.vector
                    def _(v):
                        v.wait_ge(S["ac_silu"].h, ac_base + ci + 1)
                        _w(v, S["pe_w2"])
                        v.scalar_tensor_tensor(
                            tb_t[:, ci * 256:(ci + 1) * 256],
                            rot[ci][0:128, 256:512],
                            b3c[:, u * 4 + ci:u * 4 + ci + 1],
                            bass.AP(gb_t, gb_o, [[256, 128], [1, 256]]),
                            ALU.add, ALU.mult)
                        inst = v.tensor_tensor(
                            h_t[:, ci * 256:(ci + 1) * 256],
                            tb_t[:, ci * 256:(ci + 1) * 256],
                            sa_t[:, ci * 256:(ci + 1) * 256], ALU.mult)
                        S["dv_hev"].inc(inst)

                def emit_w2(ci):
                    @blk.tensor
                    def _(t):
                        _w(t, S["dv_hev"],
                           S["dv_hev"].n - (1 if ci < 3 else 0))
                        inst = None
                        for dk in range(DK):
                            bank, colo = ps_s[dk // 2], (dk % 2) * 256
                            inst = t.matmul(
                                bank[0:128, colo:colo + 256],
                                wu2[par][:, ci * 1024 + dk * 128:ci * 1024 + dk * 128 + 128],
                                h_t[:, ci * 256:(ci + 1) * 256],
                                start=False, stop=True)
                        if ci == 3:
                            S["pe_w2"].inc(inst)
                        else:
                            S["pe_misc"].inc(inst)

                ab_base = S["pe_ab"].n
                hev_base = S["dv_hev"].n
                ac_base = S["ac_silu"].n

                @blk.scalar
                def _(s_):
                    for ci in range(4):
                        s_.wait_ge(S["pe_ab"].h, ab_base + ci + 1)
                        if hev_base + ci - 1 > 0:
                            s_.wait_ge(S["dv_hev"].h, hev_base + ci - 1)
                        inst = s_.activation(
                            sa_t[:, ci * 256:(ci + 1) * 256],
                            rot[ci][0:128, 0:256], AF.Silu,
                            bias=b1c[:, u * 4 + ci:u * 4 + ci + 1], scale=1.0)
                        S["ac_silu"].inc(inst)

                emit_ab(0)
                emit_ab(1)
                emit_w2(0)
                emit_ab(2)
                emit_w2(1)
                emit_ab(3)
                emit_w2(2)
                emit_w2(3)

            @blk.vector
            def _(v):
                _w(v, S["pe_w2"])
                _w(v, S["pe_misc"])
                inst = None
                for i in range(4):
                    oo = bass.AP(x_res, 2 * i * T + fblk * 256,
                                 [[DK * T, 128], [T, 2], [1, 256]])
                    inst = v.tensor_tensor(
                        oo, oo,
                        bass.AP(ps_s[i], 0, [[512, 128], [256, 2], [1, 256]]),
                        ALU.add)
                S["dv_oadd"].inc(inst)

        @blk.sync
        def _(sy):
            _w(sy, S["dv_oadd"])
            src = bass.AP(x_res, 0, [[DK * T, 128], [T, DK], [1, T]])
            dst = bass.AP(out_d, 0, [[T, 128], [128 * T, DK], [1, T]])
            sy.dma_start(dst, src).then_inc(S["sst"].h, 16)
            S["sst"].n += 16
            sy.wait_ge(S["sst"].h, S["sst"].n)

    return nc


_NC_CACHE = {}


def _get_nc():
    key = (os.environ.get("KMK_PHASE", "full"), os.environ.get("KMK_GROUPS", "17"),
           os.environ.get("KMK_CONV", "1"))
    if key not in _NC_CACHE:
        _NC_CACHE[key] = build_program()
    return _NC_CACHE[key]


def kernel(**inp):
    f32 = np.float32
    x = np.asarray(inp["L0_attn_x"], dtype=f32)
    wq = np.asarray(inp["L0_attn_Wq"], f32)
    wk = np.asarray(inp["L0_attn_Wk"], f32)
    wv = np.asarray(inp["L0_attn_Wv"], f32)
    wke = np.asarray(inp["L0_attn_Wk_enc"], f32)
    wve = np.asarray(inp["L0_attn_Wv_enc"], f32)
    wo = np.asarray(inp["L0_attn_Wout"], f32)
    gw = np.asarray(inp["L0_ffn_gate_w"], f32)
    eb = np.asarray(inp["L0_ffn_expert_bias"], f32)
    w1 = np.asarray(inp["L0_ffn_w1"], f32)
    b1 = np.asarray(inp["L0_ffn_b1"], f32)
    w2 = np.asarray(inp["L0_ffn_w2"], f32)
    b2 = np.asarray(inp["L0_ffn_b2"], f32)
    w3 = np.asarray(inp["L0_ffn_w3"], f32)
    b3 = np.asarray(inp["L0_ffn_b3"], f32)
    sw1 = np.asarray(inp["L0_ffn_sw1"], f32)
    sb1 = np.asarray(inp["L0_ffn_sb1"], f32)
    sw2 = np.asarray(inp["L0_ffn_sw2"], f32)
    sb2 = np.asarray(inp["L0_ffn_sb2"], f32)
    sw3 = np.asarray(inp["L0_ffn_sw3"], f32)
    sb3 = np.asarray(inp["L0_ffn_sb3"], f32)

    C = np.ascontiguousarray

    def tile_t(wt):
        ncols = wt.shape[1]
        return C(wt.reshape(DK, 128, ncols).transpose(1, 0, 2)
                 .reshape(128, DK * ncols))

    def tile_k(wt, kch):
        ncols = wt.shape[1]
        return wt.reshape(kch, 128, ncols).transpose(1, 0, 2).reshape(
            128, kch * ncols)

    wsrc = np.empty((54, 128, 4096), f32)
    i = 0
    for e in range(16):
        wsrc[i] = tile_k(w1[e].T, 8); i += 1
        wsrc[i] = tile_k(w3[e].T, 8); i += 1
        wsrc[i] = tile_k(w2[e].T, 4); i += 1
    for u_ in range(2):
        wsrc[i] = tile_k(sw1.T[:, u_ * 512:(u_ + 1) * 512], 8); i += 1
        wsrc[i] = tile_k(sw3.T[:, u_ * 512:(u_ + 1) * 512], 8); i += 1
        wsrc[i] = tile_k(sw2.T[u_ * 512:(u_ + 1) * 512], 4); i += 1
    shared = {
        "wqT": tile_t(C(wq.T)), "wkT": tile_t(C(wk.T)), "wvT": tile_t(C(wv.T)),
        "wkeT": tile_t(C(wke.T)), "wveT": tile_t(C(wve.T)),
        "woT": tile_t(C(wo.T)),
        "gwTm": tile_t(C(gw.T)), "ebr": C(eb[None, :]),
        "wsrc": C(wsrc.reshape(54 * 128, 4096)),
        "b2m": C(np.concatenate([b2, sb2[None, :]], axis=0)),
    }
    b1all = np.concatenate([b1, sb1.reshape(2, 512)], axis=0)
    b3all = np.concatenate([b3, sb3.reshape(2, 512)], axis=0)
    shared["b1c"] = C(b1all.reshape(E_UNITS, 4, 128).transpose(2, 0, 1)
                      .reshape(128, E_UNITS * 4))
    shared["b3c"] = C(b3all.reshape(E_UNITS, 4, 128).transpose(2, 0, 1)
                      .reshape(128, E_UNITS * 4))
    sc = np.zeros((128, 1), f32)
    sc[0:16] = 0.25
    sc[32:48] = 1.0 / 32.0
    sc[48] = 1.0
    sc[64:80] = 0.25
    sc[96:112] = 1.0 / 32.0
    sc[112] = 1.0
    shared["scol"] = sc

    g_tok0 = [0] + [512 * g for g in range(12)] + [PRE + 512 * g for g in range(4)]
    in_maps = []
    for core in range(8):
        b, c = core // 4, core % 4
        xb = x[b].T
        xTc = np.concatenate([xb[:, :PRE], xb[:, c * T:(c + 1) * T]], axis=1)
        xtt = tile_k(xTc, DK).reshape(128, DK, PRE + T)
        xgt = np.empty((128, 17, 4096), f32)
        for gi, t0 in enumerate(g_tok0):
            xgt[:, gi] = xtt[:, :, t0:t0 + 512].reshape(128, DK * 512)
        xot = C(xtt[:, :, PRE:].reshape(128, DK * T))
        m = np.zeros((128, NCH), f32)
        lim = c * T
        for j in range(NCH):
            base = j * 128
            if base + 128 <= lim:
                m[:, j] = 1.0
            elif base < lim:
                m[0:lim - base, j] = 1.0
        d = dict(shared)
        d["xgt"] = C(xgt.reshape(128, 17 * 4096))
        d["xot"] = xot
        d["mask"] = m
        in_maps.append(d)

    nc = _get_nc()
    res = run_bass_kernel_spmd(nc, in_maps, core_ids=list(range(8)))
    y = np.empty((B_, N_, D), f32)
    for core in range(8):
        b, c = core // 4, core % 4
        y[b, c * T:(c + 1) * T, :] = res.results[core]["out"].T
    return y


# version 12
# speedup vs baseline: 1.1306x; 1.0412x over previous
import contextlib
import os
import numpy as np
import ml_dtypes
import concourse.bass as bass
import concourse.mybir as mybir
from concourse.bass_utils import run_bass_kernel_spmd

F32 = mybir.dt.float32
BF16 = mybir.dt.bfloat16
AF = mybir.ActivationFunctionType
ALU = mybir.AluOpType
AXX = mybir.AxisListType.X

B_, N_, D = 2, 8192, 1024
H, FD, HD = 16, 16, 64
T = 2048
PRE = 6144
NCH = PRE // 128
DK = D // 128
NPAIR = 120
RW = 49
E_UNITS = 18
DFF = 512
LN_EPS = 1e-5
ATT_EPS = 1e-12
GATE_EPS = 1e-10
NEG_BIG = -1e30


class Sem:
    def __init__(self, stack, nc, name):
        self.h = stack.enter_context(nc.semaphore(name))
        self.n = 0

    def inc(self, inst, k=1):
        assert inst is not None
        inst.then_inc(self.h, k)
        self.n += k


def build_program():
    nc = bass.Bass("TRN2", target_bir_lowering=False)

    def _w(eng, sem, n=None):
        v = sem.n if n is None else n
        assert v <= sem.n, f"future wait {v} > {sem.n}"
        if v > 0:
            eng.wait_ge(sem.h, v)

    xg_d = nc.dram_tensor("xgt", [128, 17 * 4096], F32, kind="ExternalInput")
    xo_d = nc.dram_tensor("xot", [128, DK * T], F32, kind="ExternalInput")
    mask_d = nc.dram_tensor("mask", [128, NCH], F32, kind="ExternalInput")
    scol_d = nc.dram_tensor("scol", [128, 1], F32, kind="ExternalInput")
    wq_d = nc.dram_tensor("wqT", [128, DK * 256], F32, kind="ExternalInput")
    wk_d = nc.dram_tensor("wkT", [128, DK * 256], F32, kind="ExternalInput")
    wv_d = nc.dram_tensor("wvT", [128, DK * 1024], F32, kind="ExternalInput")
    wke_d = nc.dram_tensor("wkeT", [128, DK * 256], F32, kind="ExternalInput")
    wve_d = nc.dram_tensor("wveT", [128, DK * 1024], F32, kind="ExternalInput")
    wo_d = nc.dram_tensor("woT", [128, DK * 1024], F32, kind="ExternalInput")
    gw_d = nc.dram_tensor("gwTm", [128, DK * 16], F32, kind="ExternalInput")
    ebr_d = nc.dram_tensor("ebr", [1, 16], F32, kind="ExternalInput")
    wsrc_d = nc.dram_tensor("wsrc", [54 * 128, 4096], F32, kind="ExternalInput")
    b1c_d = nc.dram_tensor("b1c", [128, 72], F32, kind="ExternalInput")
    b3c_d = nc.dram_tensor("b3c", [128, 72], F32, kind="ExternalInput")
    b2_d = nc.dram_tensor("b2m", [17, D], F32, kind="ExternalInput")
    out_d = nc.dram_tensor("out", [D, T], F32, kind="ExternalOutput")
    wsc = nc.dram_tensor("wsc", [E_UNITS * 3 * 128, 4096], BF16, kind="Internal")

    st = contextlib.ExitStack()
    with st:
        def sb(name, shape, dt):
            return st.enter_context(nc.sbuf_tensor(name, shape, dt))

        x_res = sb("x_res", [128, DK * T], F32)
        ones_rw = sb("ones_rw", [1, 256], BF16)
        onesmat = sb("onesmat", [128, 128], BF16)
        ident = sb("ident", [128, 128], BF16)
        tril = sb("tril", [128, 128], BF16)
        onescol = sb("onescol", [128, 1], F32)
        onescol_bf = sb("onescol_bf", [128, 1], BF16)
        eb16 = sb("eb16", [128, 16], F32)
        scol = sb("scol_s", [128, 1], F32)
        mcol = sb("mcol", [128, NCH], F32)
        mcol_bf = sb("mcol_bf", [128, NCH], BF16)
        ebrow = sb("ebrow", [1, 16], BF16)
        cvals = sb("cvals", [128, 2], F32)
        wq_sb = sb("wq_sb", [128, DK * 256], BF16)
        wk_sb = sb("wk_sb", [128, DK * 256], BF16)
        wv_sb = sb("wv_sb", [128, DK * 1024], BF16)
        wo_sb = sb("wo_sb", [128, DK * 1024], BF16)
        gw_sb = sb("gw_sb", [128, DK * 16], BF16)
        xhx = sb("xhx", [128, DK * 512], BF16)
        mb_s = sb("mb_s", [128, 512], F32)
        var_s = sb("var_s", [128, 512], F32)
        wu1 = [sb(f"wu1_{i}", [128, 4096], BF16) for i in range(2)]
        wu3 = [sb(f"wu3_{i}", [128, 4096], BF16) for i in range(2)]
        wu2 = [sb(f"wu2_{i}", [128, 4096], BF16) for i in range(2)]
        ktok = sb("ktok", [128, 256], BF16)
        qq = sb("qq", [128, H * 32], BF16)
        kk = sb("kk", [128, H * 32], BF16)
        qr_tok = sb("qr_tok", [128, 8 * 128], BF16)
        kr_tok = sb("kr_tok", [128, 8 * 128], BF16)
        vp = sb("vp", [128, H * 65], BF16)
        qr_feat = sb("qr_feat", [128, H * 128], BF16)
        kr_feat = sb("kr_feat", [128, H * 128], BF16)
        sbp = sb("sbp", [128, H * 64], BF16)
        sbpd = sb("sbpd", [128, 16], BF16)
        sbr = sb("sbr", [128, H * 65], BF16)
        yp = sb("yp", [128, 8 * 128], BF16)
        ztmp = sb("ztmp", [1, 1024], F32)
        zr = sb("zr", [1, 512], BF16)
        ftmp = sb("ftmp", [128, 2 * 512], BF16)
        zb_sb = sb("zb_sb", [128, 512], BF16)
        xb_g = wu1[1]
        qp_tok_t, qp_tok_o = wu3[1], 0
        kp_tok_t, kp_tok_o = wu3[1], 1920
        qtok_t, qtok_o = wu3[1], 3840
        pt_t, pt_o = wu2[1], 0
        qp_feat_t, qp_feat_o = wu2[1], 2048
        gwt = sb("gwt", [32, T], BF16)
        glog = sb("glog", [128, 256], F32)
        gt0 = sb("gt0", [128, 256], F32)
        gt1 = sb("gt1", [128, 256], F32)
        gt2 = sb("gt2", [128, 256], F32)
        gm = sb("gm", [128, 64], F32)
        b1c = sb("b1c_s", [128, 72], F32)
        b3c = sb("b3c_s", [128, 72], F32)

        h_t = kr_feat
        sa_t = qr_tok
        tb_t = kr_tok
        gb_t, gb_o = ktok, 0
        b2_sb = yp
        sb2_sb_row = 32
        gchb_t, gchb_o = qq, 0

        def sel_ap(u):
            if u < 16:
                return qr_feat[0:32, u * 128:(u + 1) * 128]
            return sbr[0:32, (u - 16) * 128:(u - 15) * 128]

        ps_s = [st.enter_context(nc.psum_tensor(f"ps_s{i}", [128, 512], F32))
                for i in range(4)]
        ps_t0 = st.enter_context(nc.psum_tensor("ps_t0", [128, 512], F32))
        ps_t1 = st.enter_context(nc.psum_tensor("ps_t1", [128, 512], F32))
        ps_t2 = st.enter_context(nc.psum_tensor("ps_t2", [128, 512], F32))
        ps_tr = st.enter_context(nc.psum_tensor("ps_tr", [128, 1024], BF16))

        def pairs_v_rg(hh):
            return ps_s[hh // 8], (hh % 8) * 64

        def rest_rg(p):
            return (ps_s[2], 65 * p) if p < 7 else (ps_s[3], 0)

        SEMS = ["sx", "swl", "scv", "ssc", "sfu", "sst",
                "pe_stat", "pe_proj", "pe_v", "pe_tr", "pe_g", "pe_y",
                "pe_zb", "pe_wout", "pe_state", "pe_gate", "pe_gb",
                "pe_ab", "pe_w2", "pe_misc",
                "dv_xb", "dv_row", "dv_xh", "dv_eqk", "dv_bld",
                "dv_evtr", "dv_cast", "dv_f", "dv_z", "dv_evy", "dv_wadd",
                "dv_gate", "dv_hev", "dv_oadd", "dv_init", "dv_gb", "dv_ms",
                "gp_sel", "ac_row", "ac_silu", "ac_f"]
        S = {nm: Sem(st, nc, nm) for nm in SEMS}

        blk = st.enter_context(nc.Block())

        @blk.sync
        def _(sy):
            sy.dma_start(mcol[:, :], mask_d.ap()).then_inc(S["sx"].h, 16)
            sy.dma_start(scol[:, :], scol_d.ap()).then_inc(S["sx"].h, 16)
            S["sx"].n += 32
            sy.dma_start(x_res[:, :], xo_d.ap()).then_inc(S["sx"].h, 16)
            S["sx"].n += 16

        @blk.gpsimd
        def _(g):
            g.memset(ones_rw[:, :], 1.0)
            g.memset(onesmat[:, :], 1.0)
            g.memset(onescol[:, :], 1.0)
            g.memset(onescol_bf[:, :], 1.0)
            g.memset(ident[:, :], 1.0)
            g.affine_select(ident[:, :], ident[:, :], [[1, 128]],
                            ALU.is_equal, 0.0, base=0, channel_multiplier=-1)
            g.memset(tril[:, :], 1.0)
            g.affine_select(tril[:, :], tril[:, :], [[1, 128]],
                            ALU.is_ge, 0.0, base=0, channel_multiplier=-1)
            g.memset(qr_tok[:, :], 0.0)
            g.memset(kr_tok[:, :], 0.0)
            g.memset(bass.AP(qr_tok, 48, [[8 * 128, 128], [64, 16], [1, 1]]), 1.0)
            g.memset(cvals[:, 0:1], LN_EPS)
            g.memset(cvals[:, 1:2], 0.125)
            g.memset(gwt[:, :], 1.0)
            g.affine_select(gwt[0:32, :], gwt[0:32, :], [[0, T]],
                            ALU.is_ge, 0.0, base=-16, channel_multiplier=1)
            g.affine_select(gwt[0:32, :], gwt[0:32, :], [[0, T]],
                            ALU.is_ge, 0.0, base=17, channel_multiplier=-1)
            inst = g.memset(vp[:, :], 0.0)
            S["dv_init"].inc(inst)
            g.dma_start(mcol_bf[:, :], mask_d.ap()).then_inc(S["swl"].h, 16)
            g.dma_start(ebrow[:, :], ebr_d.ap()).then_inc(S["swl"].h, 16)
            S["swl"].n += 32

            def wload(dst_sb, src_d, ncols):
                g.dma_start(dst_sb[:, :], src_d.ap()).then_inc(S["swl"].h, 16)
                S["swl"].n += 16

            wload(wq_sb, wq_d, 256)
            wload(wk_sb, wk_d, 256)
            wload(wv_sb, wv_d, 1024)
            wload(wo_sb, wo_d, 1024)
            wload(gw_sb, gw_d, 16)
            g.dma_start(b1c[:, :], b1c_d.ap()).then_inc(S["swl"].h, 16)
            g.dma_start(b3c[:, :], b3c_d.ap()).then_inc(S["swl"].h, 16)
            S["swl"].n += 32
            g.dma_start(wu2[0][:, 0:DK * 256], wke_d.ap()).then_inc(S["swl"].h, 16)
            src = bass.AP(wve_d, 0, [[DK * 1024, 128], [1, 4096]])
            g.dma_start(wu1[0][:, :], src).then_inc(S["swl"].h, 16)
            src = bass.AP(wve_d, 4096, [[DK * 1024, 128], [1, 4096]])
            g.dma_start(wu3[0][:, :], src).then_inc(S["swl"].h, 16)
            S["swl"].n += 48

        def wke_ap(dk):
            return wu2[0][:, dk * 256:(dk + 1) * 256]

        def wve_ap(dk, half):
            tt = wu1[0] if dk < 4 else wu3[0]
            return tt[:, (dk % 4) * 1024 + half * 512:(dk % 4) * 1024 + half * 512 + 512]

        @blk.vector
        def _(v):
            for i in range(4):
                v.memset(ps_s[i][:, :], 0.0)
            inst = v.memset(ps_t2[:, :], 0.0)
            S["dv_init"].inc(inst)

        @blk.tensor
        def _(t):
            _w(t, S["swl"])
            _w(t, S["dv_init"])
            inst = t.matmul(ps_t0[0:128, 0:16], ones_rw[0:1, 0:128],
                            ebrow[0:1, 0:16], start=True, stop=True)
            S["pe_misc"].inc(inst)

        @blk.vector
        def _(v):
            _w(v, S["pe_misc"])
            inst = v.tensor_copy(eb16[:, :], ps_t0[0:128, 0:16])
            S["dv_init"].inc(inst)

        NCONV = 54
        conv_stage = [wu1[0], wu3[0], wu2[0]]

        def _conv_src(i):
            return bass.AP(wsrc_d, i * 128 * 4096, [[4096, 128], [1, 4096]])

        conv_emitted = [0]

        def emit_conv(k):
            n0 = conv_emitted[0]
            n1 = min(NCONV, n0 + k)
            if n1 == n0:
                return
            conv_emitted[0] = n1

            @blk.gpsimd
            def _(g):
                for i in range(n0, n1):
                    buf = conv_stage[i % 3]
                    if i == 0:
                        g.wait_ge(S["pe_state"].h, 4)
                    if i >= 3:
                        g.wait_ge(S["ssc"].h, (i - 3 + 1) * 16)
                    g.dma_start(buf[:, :], _conv_src(i)).then_inc(
                        S["scv"].h, 16)
                    S["scv"].n += 16

            @blk.sync
            def _(sy):
                for i in range(n0, n1):
                    buf = conv_stage[i % 3]
                    sy.wait_ge(S["scv"].h, (i + 1) * 16)
                    dst = bass.AP(wsc, i * 128 * 4096, [[4096, 128], [1, 4096]])
                    sy.dma_start(dst, buf[:, :]).then_inc(S["ssc"].h, 16)
                    S["ssc"].n += 16

        def ln_group(tok0, from_res, first=False, gidx=0):
            @blk.gpsimd
            def _(g):
                if not from_res:
                    if S["pe_stat"].n > 0:
                        g.wait_ge(S["pe_stat"].h, S["pe_stat"].n)
                    if S["dv_xh"].n > 0:
                        g.wait_ge(S["dv_xh"].h, S["dv_xh"].n)
                    src = bass.AP(xg_d, gidx * 4096,
                                  [[17 * 4096, 128], [1, 4096]])
                    g.dma_start(xb_g[:, :], src).then_inc(S["sx"].h, 16)
                    S["sx"].n += 16

            @blk.vector
            def _(v):
                if from_res:
                    _w(v, S["pe_stat"])
                    src = bass.AP(x_res, tok0, [[DK * T, 128], [T, DK], [1, 512]])
                    v.tensor_copy(xb_g[:, :], src)
                else:
                    _w(v, S["sx"])
                _w(v, S["pe_v"])
                _w(v, S["pe_gate"])
                inst = v.tensor_tensor(xhx[:, :], xb_g[:, :], xb_g[:, :],
                                       ALU.mult)
                S["dv_xb"].inc(inst)

            @blk.tensor
            def _(t):
                _w(t, S["dv_xb"])
                if first:
                    _w(t, S["dv_init"])
                _w(t, S["dv_eqk"])
                _w(t, S["dv_row"])
                _w(t, S["dv_wadd"])
                _w(t, S["dv_bld"])
                _w(t, S["dv_evy"])
                for dk in range(DK):
                    t.matmul(ps_t0[0:128, 0:512], onesmat[:, :],
                             xb_g[:, dk * 512:(dk + 1) * 512],
                             start=(dk == 0), stop=(dk == DK - 1))
                for dk in range(DK):
                    inst = t.matmul(ps_t1[0:128, 0:512], onesmat[:, :],
                                    xhx[:, dk * 512:(dk + 1) * 512],
                                    start=(dk == 0), stop=(dk == DK - 1))
                S["pe_stat"].inc(inst)

            @blk.vector
            def _(v):
                _w(v, S["pe_stat"])
                v.tensor_scalar(mb_s[:, :], ps_t0[0:128, 0:512],
                                1.0 / D, None, ALU.mult)
                v.tensor_tensor(var_s[:, :], mb_s[:, :], mb_s[:, :], ALU.mult)
                inst = v.scalar_tensor_tensor(var_s[:, :], ps_t1[0:128, 0:512],
                                              1.0 / D, var_s[:, :],
                                              ALU.mult, ALU.subtract)
                S["dv_row"].inc(inst)

            @blk.scalar
            def _(s):
                _w(s, S["dv_row"])
                _w(s, S["dv_init"])
                inst = s.activation(var_s[:, :], var_s[:, :], AF.Sqrt,
                                    bias=cvals[:, 0:1], scale=1.0)
                S["ac_row"].inc(inst)

            @blk.vector
            def _(v):
                _w(v, S["ac_row"])
                v.reciprocal(var_s[:, :], var_s[:, :])
                mb_b = bass.AP(mb_s, 0, [[512, 128], [0, DK], [1, 512]])
                rb_b = bass.AP(var_s, 0, [[512, 128], [0, DK], [1, 512]])
                xb3 = bass.AP(xb_g, 0, [[DK * 512, 128], [512, DK], [1, 512]])
                if from_res:
                    _w(v, S["pe_wout"])
                    xh3a = xhat2_ap3(tok0)
                    t3 = bass.AP(xhx, 0, [[DK * 512, 128], [512, DK], [1, 512]])
                    v.tensor_tensor(t3, xb3, mb_b, ALU.subtract)
                    ta = bass.AP(xhx, 0, [[DK * 512, 128], [512, 4], [1, 512]])
                    tb = bass.AP(xhx, 4 * 512, [[DK * 512, 128], [512, 4], [1, 512]])
                    ra = bass.AP(var_s, 0, [[512, 128], [0, 4], [1, 512]])
                    v.tensor_tensor(xh3a[0], ta, ra, ALU.mult)
                    inst = v.tensor_tensor(xh3a[1], tb, ra, ALU.mult)
                else:
                    xh3 = bass.AP(xhx, 0, [[DK * 512, 128], [512, DK], [1, 512]])
                    v.tensor_tensor(xh3, xb3, mb_b, ALU.subtract)
                    inst = v.tensor_tensor(xh3, xh3, rb_b, ALU.mult)
                S["dv_xh"].inc(inst)

        def xhat2_ap(dk, c0, w):
            tt = wv_sb if dk < 4 else wo_sb
            o = (dk % 4) * T + c0
            return tt[:, o:o + w]

        def xhat2_ap3(tok0):
            a = bass.AP(wv_sb, tok0, [[4 * T, 128], [T, 4], [1, 512]])
            b = bass.AP(wo_sb, tok0, [[4 * T, 128], [T, 4], [1, 512]])
            return a, b

        def attn_chunk(kind, tokabs, cidx):
            is_own = kind == "own"
            if kind == "pre":
                jm, mct, mcp = tokabs // 128, mcol, NCH
                mc = mcol[:, jm:jm + 1]
                mcb = mcol_bf[:, jm:jm + 1]
            else:
                jm, mct, mcp = 0, onescol, 1
                mc = onescol[:, 0:1]
                mcb = onescol_bf[:, 0:1]

            def xh(dk):
                c = dk * 512 + cidx * 128
                return xhx[:, c:c + 128]

            def wk_ap(dk):
                if kind == "enc":
                    return wke_ap(dk)
                return wk_sb[:, dk * 256:(dk + 1) * 256]

            def wv_ap(dk, half):
                if kind == "enc":
                    return wve_ap(dk, half)
                return wv_sb[:, dk * 1024 + half * 512:dk * 1024 + half * 512 + 512]

            @blk.tensor
            def _(t):
                _w(t, S["dv_xh"])
                _w(t, S["dv_eqk"])
                _w(t, S["dv_wadd"])
                _w(t, S["dv_row"])
                for dk in range(DK):
                    t.matmul(ps_t0[0:128, 0:256], xh(dk), wk_ap(dk),
                             start=(dk == 0), stop=(dk == DK - 1))
                if is_own:
                    for dk in range(DK):
                        t.matmul(ps_t0[0:128, 256:512], xh(dk),
                                 wq_sb[:, dk * 256:(dk + 1) * 256],
                                 start=(dk == 0), stop=(dk == DK - 1))
                _w(t, S["dv_bld"])
                _w(t, S["dv_evy"])
                for dk in range(DK):
                    inst = t.matmul(ps_t1[0:128, 0:512], xh(dk), wv_ap(dk, 0),
                                    start=(dk == 0), stop=(dk == DK - 1))
                S["pe_proj"].inc(inst)

            @blk.vector
            def _(v):
                _w(v, S["pe_proj"])
                _w(v, S["pe_state"])
                _w(v, S["pe_y"])
                _w(v, S["sx"], 48)
                v.tensor_copy(ktok[:, :], ps_t0[0:128, 0:256])
                if is_own:
                    v.tensor_copy(bass.AP(qtok_t, qtok_o, [[4096, 128], [1, 256]]),
                                  ps_t0[0:128, 256:512])
                ksrc = bass.AP(ktok, 0, [[256, 128], [16, H], [1, 16]])
                v.tensor_copy(bass.AP(kk, 0, [[H * 32, 128], [32, H], [1, 16]]),
                              ksrc)
                v.tensor_copy(bass.AP(kk, 16, [[H * 32, 128], [32, H], [1, 16]]),
                              ksrc)
                if is_own:
                    qsrc = bass.AP(qtok_t, qtok_o, [[4096, 128], [16, H], [1, 16]])
                    v.tensor_copy(
                        bass.AP(qq, 0, [[H * 32, 128], [32, H], [1, 16]]), qsrc)
                    v.tensor_copy(
                        bass.AP(qq, 16, [[H * 32, 128], [32, H], [1, 16]]), qsrc)
                dstA = bass.AP(vp, 0, [[H * 65, 128], [65, 8], [1, 64]])
                srcA = bass.AP(ps_t1, 0, [[512, 128], [64, 8], [1, 64]])
                v.tensor_scalar(dstA, srcA, mc, None, ALU.mult)
                dstO = bass.AP(vp, 64, [[H * 65, 128], [65, 8], [1, 1]])
                srcO = bass.AP(mct, jm, [[mcp, 128], [0, 8], [1, 1]])
                inst = v.tensor_copy(dstO, srcO)
                S["dv_eqk"].inc(inst)

            @blk.tensor
            def _(t):
                _w(t, S["dv_eqk"])
                for dk in range(DK):
                    inst = t.matmul(ps_t1[0:128, 0:512], xh(dk), wv_ap(dk, 1),
                                    start=(dk == 0), stop=(dk == DK - 1))
                S["pe_v"].inc(inst)

            @blk.vector
            def _(v):
                _w(v, S["pe_v"])
                _w(v, S["pe_tr"])
                dstB = bass.AP(vp, 8 * 65, [[H * 65, 128], [65, 8], [1, 64]])
                srcB = bass.AP(ps_t1, 0, [[512, 128], [64, 8], [1, 64]])
                v.tensor_scalar(dstB, srcB, mc, None, ALU.mult)
                dstO = bass.AP(vp, 8 * 65 + 64, [[H * 65, 128], [65, 8], [1, 1]])
                srcO = bass.AP(mct, jm, [[mcp, 128], [0, 8], [1, 1]])
                v.tensor_copy(dstO, srcO)

                def build(pt_tt, pt_oo, rtile, dup, tok_ap):
                    for dd in range(1, 8):
                        i0 = bass.AP(dup, 0, [[H * 32, 128], [32, H], [1, 16]])
                        i1 = bass.AP(dup, dd, [[H * 32, 128], [32, H], [1, 16]])
                        oo = bass.AP(pt_tt, pt_oo + (dd - 1) * 16,
                                     [[4096 if pt_tt in (wu3[1],) else H * NPAIR, 128],
                                      [NPAIR, H], [1, 16]])
                        v.tensor_tensor(oo, i0, i1, ALU.mult)
                    i0 = bass.AP(dup, 0, [[H * 32, 128], [32, H], [1, 8]])
                    i1 = bass.AP(dup, 8, [[H * 32, 128], [32, H], [1, 8]])
                    oo = bass.AP(pt_tt, pt_oo + 112,
                                 [[4096 if pt_tt in (wu3[1],) else H * NPAIR, 128],
                                  [NPAIR, H], [1, 8]])
                    v.tensor_tensor(oo, i0, i1, ALU.mult)
                    src = bass.AP(tok_ap[0], tok_ap[1],
                                  [[tok_ap[2], 128], [32, 8], [16, 2], [1, 16]])
                    dl = bass.AP(rtile, 0,
                                 [[8 * 128, 128], [128, 8], [64, 2], [1, 16]])
                    v.tensor_copy(dl, src)
                    dd2 = bass.AP(rtile, 32,
                                  [[8 * 128, 128], [128, 8], [64, 2], [1, 16]])
                    return v.tensor_tensor(dd2, src, src, ALU.mult)

                inst = build(kp_tok_t, kp_tok_o, kr_tok, kk, (ktok, 0, 256))
                dm = bass.AP(kr_tok, 48, [[8 * 128, 128], [128, 8], [64, 2], [1, 1]])
                sm = bass.AP(mct, jm, [[mcp, 128], [0, 8], [0, 2], [1, 1]])
                inst = v.tensor_copy(dm, sm)
                if is_own:
                    inst = build(qp_tok_t, qp_tok_o, qr_tok, qq,
                                 (qtok_t, qtok_o, 4096))
                S["dv_bld"].inc(inst)

            if is_own and OWNLVL >= 1:
                @blk.vector
                def _(v):
                    _w(v, S["pe_state"])
                    _w(v, S["pe_y"])
                    v.tensor_scalar(sbp[0:120, 0:512], ps_s[0][0:120, 0:512],
                                    1.0 / 16.0, None, ALU.mult)
                    v.tensor_scalar(sbp[0:120, 512:1024], ps_s[1][0:120, 0:512],
                                    1.0 / 16.0, None, ALU.mult)
                    v.tensor_scalar(sbpd[0:120, 0:16], ps_s[2][0:120, 455:471],
                                    1.0 / 16.0, None, ALU.mult)
                    ev_out = bass.AP(sbr, 0, [[H * 65, RW], [130, 7], [1, 65]])
                    ev_in = ps_s[2][0:RW, 0:455].rearrange(
                        "p (r c) -> p r c", r=7)
                    v.tensor_scalar(ev_out, ev_in, scol[0:RW, 0:1], None,
                                    ALU.mult)
                    od_out = bass.AP(sbr, 65, [[H * 65, RW], [130, 7], [1, 65]])
                    od_in = ps_s[2][64:64 + RW, 0:455].rearrange(
                        "p (r c) -> p r c", r=7)
                    v.tensor_scalar(od_out, od_in, scol[0:RW, 0:1], None,
                                    ALU.mult)
                    v.tensor_scalar(sbr[0:RW, 14 * 65:14 * 65 + 65],
                                    ps_s[3][0:RW, 0:65],
                                    scol[0:RW, 0:1], None, ALU.mult)
                    inst = v.tensor_scalar(sbr[0:RW, 15 * 65:15 * 65 + 65],
                                           ps_s[3][64:64 + RW, 0:65],
                                           scol[0:RW, 0:1], None, ALU.mult)
                    S["dv_cast"].inc(inst)

                pass
            if is_own and OWNLVL >= 2:
                def rto(hh):
                    return (hh // 2) * 128 + (hh % 2) * 64

                tr_specs = [
                    ([(qp_tok_t, qp_tok_o + hh * NPAIR, NPAIR) for hh in range(8)],
                     (qp_feat_t, qp_feat_o + 0), NPAIR),
                    ([(qp_tok_t, qp_tok_o + (8 + hh) * NPAIR, NPAIR)
                      for hh in range(8)],
                     (qp_feat_t, qp_feat_o + 1024), NPAIR),
                    ([(qr_tok, rto(hh), RW) for hh in range(8)],
                     (qr_feat, 0), RW),
                    ([(qr_tok, rto(8 + hh), RW) for hh in range(8)],
                     (qr_feat, 1024), RW),
                    ([(kr_tok, rto(hh), RW) for hh in range(8)],
                     (kr_feat, 0), RW),
                    ([(kr_tok, rto(8 + hh), RW) for hh in range(8)],
                     (kr_feat, 1024), RW),
                ]
                for wv_i in range(6):
                    srcs_w, (dt_t, dt_o), rows = tr_specs[wv_i]

                    @blk.tensor
                    def _(t, srcs_w=srcs_w, rows=rows):
                        _w(t, S["dv_bld"])
                        _w(t, S["dv_evtr"])
                        for i8, (tt, o, wdt) in enumerate(srcs_w):
                            src = bass.AP(tt, o, [[tt.shape[1], 128], [1, wdt]])
                            inst = t.transpose(
                                ps_tr[0:rows, i8 * 128:(i8 + 1) * 128],
                                src, ident[:, :])
                        S["pe_tr"].inc(inst)

                    @blk.vector
                    def _(v, dt_t=dt_t, dt_o=dt_o, rows=rows):
                        _w(v, S["pe_tr"])
                        _w(v, S["pe_g"])
                        _w(v, S["pe_y"])
                        dst = bass.AP(dt_t, dt_o, [[dt_t.shape[1], rows], [1, 1024]])
                        inst = v.tensor_copy(dst, ps_tr[0:rows, 0:1024])
                        S["dv_evtr"].inc(inst)

                pass
            if is_own and OWNLVL >= 3:
                for wv_i in range(4):
                    @blk.tensor
                    def _(t, wv_i=wv_i):
                        _w(t, S["dv_evtr"])
                        _w(t, S["dv_f"])
                        for hh in range(4 * wv_i, 4 * wv_i + 4):
                            inst = t.matmul(
                                ps_t0[0:128, (hh % 4) * 128:(hh % 4) * 128 + 128],
                                kr_feat[0:16, hh * 128:(hh + 1) * 128],
                                qr_feat[0:16, hh * 128:(hh + 1) * 128],
                                start=True, stop=True)
                        S["pe_g"].inc(inst)

                    @blk.scalar
                    def _(s, wv_i=wv_i):
                        _w(s, S["pe_g"])
                        _w(s, S["dv_f"], max(0, S["dv_f"].n - 1))
                        fts = mb_s if wv_i % 2 == 0 else var_s
                        inst = s.activation(fts[:, :], ps_t0[0:128, 0:512],
                                            AF.Identity, bias=1.0,
                                            scale=cvals[:, 1:2])
                        S["ac_f"].inc(inst)

                    @blk.vector
                    def _(v, wv_i=wv_i):
                        _w(v, S["ac_f"])
                        _w(v, S["pe_y"])
                        fts = mb_s if wv_i % 2 == 0 else var_s
                        par = (wv_i % 2) * 512
                        v.scalar_tensor_tensor(ftmp[:, par:par + 512],
                                               ps_t0[0:128, 0:512], 0.25,
                                               fts[:, :], ALU.mult, ALU.mult)
                        trb = bass.AP(tril, 0, [[128, 128], [0, 4], [1, 128]])
                        i0 = bass.AP(ftmp, par, [[1024, 128], [128, 4], [1, 128]])
                        oo = bass.AP(pt_t, pt_o + wv_i * 512,
                                     [[4096, 128], [128, 4], [1, 128]])
                        inst = v.scalar_tensor_tensor(oo, i0, 1.0, trb,
                                                      ALU.add, ALU.mult)
                        S["dv_f"].inc(inst)

                    if OWNLVL < 4:
                        continue

                    @blk.tensor
                    def _(t, wv_i=wv_i):
                        _w(t, S["dv_f"])
                        _w(t, S["dv_cast"])
                        _w(t, S["dv_bld"])
                        _w(t, S["dv_evy"])
                        for hh in range(4 * wv_i, 4 * wv_i + 4):
                            fo = (hh % 4) * 128
                            t.matmul(ps_t1[0:65, fo:fo + 128],
                                     vp[:, hh * 65:hh * 65 + 65],
                                     bass.AP(pt_t, pt_o + hh * 128,
                                             [[4096, 128], [1, 128]]),
                                     start=True, stop=False)
                            t.matmul(ps_t1[0:64, fo:fo + 128],
                                     sbp[0:120, hh * 64:(hh + 1) * 64],
                                     bass.AP(qp_feat_t, qp_feat_o + hh * 128,
                                             [[4096, 120], [1, 128]]),
                                     start=False, stop=False)
                            t.matmul(ps_t1[64:65, fo:fo + 128],
                                     sbpd[0:120, hh:hh + 1],
                                     bass.AP(qp_feat_t, qp_feat_o + hh * 128,
                                             [[4096, 120], [1, 128]]),
                                     start=False, stop=False)
                            inst = t.matmul(
                                ps_t1[0:65, fo:fo + 128],
                                sbr[0:RW, hh * 65:hh * 65 + 65],
                                qr_feat[0:RW, hh * 128:(hh + 1) * 128],
                                start=False, stop=True)
                        S["pe_y"].inc(inst)

                    if OWNLVL < 5:
                        continue

                    @blk.vector
                    def _(v, wv_i=wv_i):
                        _w(v, S["pe_y"])
                        _w(v, S["pe_zb"])
                        v.tensor_scalar(ztmp[0:1, 0:512], ps_t1[64:65, 0:512],
                                        ATT_EPS, None, ALU.add)
                        v.reciprocal(ztmp[0:1, 512:1024], ztmp[0:1, 0:512])
                        inst = v.tensor_copy(zr[0:1, 0:512], ztmp[0:1, 512:1024])
                        S["dv_z"].inc(inst)

                    @blk.tensor
                    def _(t, wv_i=wv_i):
                        _w(t, S["dv_z"])
                        _w(t, S["dv_evy"])
                        _w(t, S["dv_wadd"])
                        inst = t.matmul(ps_t2[0:128, 0:512], ones_rw[0:1, 0:128],
                                        zr[0:1, 0:512], start=True, stop=True)
                        S["pe_zb"].inc(inst)

                    @blk.vector
                    def _(v, wv_i=wv_i):
                        _w(v, S["pe_zb"])
                        _w(v, S["pe_wout"])
                        v.tensor_copy(zb_sb[:, :], ps_t2[0:128, 0:512])
                        inst = None
                        for hh in range(4 * wv_i, 4 * wv_i + 4):
                            p, rowb = hh // 2, (hh % 2) * 64
                            fo = (hh % 4) * 128
                            inst = v.tensor_tensor(
                                yp[rowb:rowb + 64, p * 128:(p + 1) * 128],
                                ps_t1[0:64, fo:fo + 128],
                                zb_sb[rowb:rowb + 64, fo:fo + 128], ALU.mult)
                        S["dv_evy"].inc(inst)

                pass
            if is_own and OWNLVL >= 6:
                @blk.tensor
                def _(t):
                    _w(t, S["dv_evy"])
                    _w(t, S["dv_f"])
                    for m in range(4):
                        for k in range(8):
                            inst = t.matmul(
                                ps_t0[0:128, m * 128:(m + 1) * 128],
                                wo_sb[:, k * 1024 + m * 128:k * 1024 + m * 128 + 128],
                                yp[:, k * 128:(k + 1) * 128],
                                start=(k == 0), stop=(k == 7))
                    for m in range(4, 8):
                        for k in range(8):
                            inst = t.matmul(
                                ps_t2[0:128, (m - 4) * 128:(m - 3) * 128],
                                wo_sb[:, k * 1024 + m * 128:k * 1024 + m * 128 + 128],
                                yp[:, k * 128:(k + 1) * 128],
                                start=(k == 0), stop=(k == 7))
                    S["pe_wout"].inc(inst)

                @blk.vector
                def _(v):
                    _w(v, S["pe_wout"])
                    _w(v, S["sx"], 48)
                    oc = tokabs - PRE
                    o1 = bass.AP(x_res, oc, [[DK * T, 128], [T, 4], [1, 128]])
                    v.tensor_tensor(o1, o1,
                                    bass.AP(ps_t0, 0, [[512, 128], [128, 4], [1, 128]]),
                                    ALU.add)
                    o2 = bass.AP(x_res, 4 * T + oc, [[DK * T, 128], [T, 4], [1, 128]])
                    inst = v.tensor_tensor(
                        o2, o2,
                        bass.AP(ps_t2, 0, [[512, 128], [128, 4], [1, 128]]),
                        ALU.add)
                    S["dv_wadd"].inc(inst)

            @blk.tensor
            def _(t):
                _w(t, S["dv_bld"])
                _w(t, S["dv_init"])
                if is_own:
                    _w(t, S["dv_cast"])
                inst = None
                for hh in range(H):
                    bank, off = pairs_v_rg(hh)
                    t.matmul(bank[0:NPAIR, off:off + 64],
                             bass.AP(kp_tok_t, kp_tok_o + hh * NPAIR,
                                     [[4096, 128], [1, NPAIR]]),
                             vp[:, hh * 65:hh * 65 + 64],
                             start=False, stop=True)
                    t.matmul(ps_s[2][0:NPAIR, 455 + hh:456 + hh],
                             bass.AP(kp_tok_t, kp_tok_o + hh * NPAIR,
                                     [[4096, 128], [1, NPAIR]]),
                             mcb, start=False, stop=True)
                for p in range(8):
                    bank, off = rest_rg(p)
                    for half in range(2):
                        hh = 2 * p + half
                        base = half * 64
                        inst = t.matmul(
                            bank[base:base + RW, off:off + 65],
                            kr_tok[:, p * 128 + base:p * 128 + base + RW],
                            vp[:, hh * 65:hh * 65 + 65],
                            start=False, stop=True)
                S["pe_state"].inc(inst)

        OWNLVL = int(os.environ.get("KMK_OWNLVL", "6"))
        PHASE = os.environ.get("KMK_PHASE", "full")
        NGRP = int(os.environ.get("KMK_GROUPS", "17"))
        DOCONV = os.environ.get("KMK_CONV", "1") == "1" and PHASE != "attn_noconv"
        groups = [("enc", 0)]
        groups += [("pre", 512 * g) for g in range(12)]
        groups += [("own", PRE + 512 * g) for g in range(4)]
        groups = groups[:NGRP]
        first = True
        for gi, (kind, tok0) in enumerate(groups):
            ln_group(tok0, from_res=False, first=first, gidx=gi)
            first = False
            for cidx in range(4):
                attn_chunk(kind, tok0 + cidx * 128, cidx)
            if kind != "enc" and DOCONV:
                emit_conv(4)
        if DOCONV:
            emit_conv(NCONV)
        if PHASE in ("attn", "attn_noconv"):
            @blk.sync
            def _(sy):
                _w(sy, S["dv_wadd"])
                if S["ssc"].n > 0:
                    _w(sy, S["ssc"])
                src = bass.AP(x_res, 0, [[DK * T, 128], [T, DK], [1, T]])
                dst = bass.AP(out_d, 0, [[T, 128], [128 * T, DK], [1, T]])
                sy.dma_start(dst, src).then_inc(S["sst"].h, 16)
                S["sst"].n += 16
                sy.wait_ge(S["sst"].h, S["sst"].n)

            @blk.tensor
            def _(t):
                _w(t, S["pe_state"])

            @blk.vector
            def _(v):
                _w(v, S["dv_bld"])
            return nc

        @blk.gpsimd
        def _(g):
            g.wait_ge(S["pe_state"].h, 68)
            g.wait_ge(S["pe_wout"].h, 16)
            g.dma_start(yp[0:17, 0:D], b2_d.ap()).then_inc(S["swl"].h, 16)
            S["swl"].n += 16
            g.memset(qr_feat[0:32, 0:2048], 1.0)
            g.affine_select(
                bass.AP(qr_feat, 0, [[H * 128, 32], [128, 16], [1, 128]]),
                bass.AP(qr_feat, 0, [[H * 128, 32], [128, 16], [1, 128]]),
                [[128, 16], [0, 128]], ALU.is_equal, 0.0,
                base=0, channel_multiplier=-128)
            g.memset(sbr[0:32, 0:256], 1.0)
            inst = g.affine_select(
                bass.AP(sbr, 0, [[H * 65, 32], [128, 2], [1, 128]]),
                bass.AP(sbr, 0, [[H * 65, 32], [128, 2], [1, 128]]),
                [[128, 2], [0, 128]], ALU.is_equal, 0.0,
                base=16 * 128, channel_multiplier=-128)
            S["gp_sel"].inc(inst)

        def gate_logits(ch):
            par = (ch % 2) * 16

            @blk.tensor
            def _(t):
                _w(t, S["dv_xh"])
                _w(t, S["dv_gate"])
                for dk in range(DK):
                    inst = t.matmul(ps_t0[0:128, par:par + 16],
                                    xhat2_ap(dk, ch * 128, 128),
                                    gw_sb[:, dk * 16:(dk + 1) * 16],
                                    start=(dk == 0), stop=(dk == DK - 1))
                S["pe_gate"].inc(inst)

            @blk.vector
            def _(v):
                _w(v, S["pe_gate"])
                inst = v.tensor_copy(glog[:, ch * 16:(ch + 1) * 16],
                                     ps_t0[0:128, par:par + 16])
                S["dv_gate"].inc(inst)

        for g4 in range(4):
            ln_group(g4 * 512, from_res=True)
            for ci in range(4):
                gate_logits(g4 * 4 + ci)

        @blk.scalar
        def _(s_):
            _w(s_, S["dv_gate"])
            inst = s_.activation(gt1[:, :], glog[:, :], AF.Sigmoid)
            S["ac_silu"].inc(inst)

        @blk.vector
        def _(v):
            _w(v, S["ac_silu"])
            ebb = bass.AP(eb16, 0, [[16, 128], [0, 16], [1, 16]])
            g3 = lambda tt: bass.AP(tt, 0, [[256, 128], [16, 16], [1, 16]])
            v.tensor_tensor(g3(gt0), g3(glog), ebb, ALU.add)
            v.drain()
            v.reduce_max(bass.AP(gm, 0, [[64, 128], [1, 16], [0, 1]]),
                         g3(gt0), axis=AXX)
            v.drain()
            m1b = bass.AP(gm, 0, [[64, 128], [1, 16], [0, 16]])
            v.tensor_tensor(g3(gt2), g3(gt0), m1b, ALU.is_ge)
            v.drain()
            v.scalar_tensor_tensor(gt2[:, :], gt2[:, :], NEG_BIG, gt0[:, :],
                                   ALU.mult, ALU.add)
            v.drain()
            v.reduce_max(bass.AP(gm, 16, [[64, 128], [1, 16], [0, 1]]),
                         g3(gt2), axis=AXX)
            v.drain()
            m2b = bass.AP(gm, 16, [[64, 128], [1, 16], [0, 16]])
            v.tensor_tensor(g3(gt2), g3(gt0), m2b, ALU.is_ge)
            v.drain()
            v.tensor_tensor(gt1[:, :], gt1[:, :], gt2[:, :], ALU.mult)
            v.drain()
            v.reduce_sum(bass.AP(gm, 32, [[64, 128], [1, 16], [0, 1]]),
                         g3(gt1), axis=AXX)
            v.drain()
            v.tensor_scalar(gm[:, 48:64], gm[:, 32:48], GATE_EPS, None, ALU.add)
            v.drain()
            v.reciprocal(gm[:, 32:48], gm[:, 48:64])
            v.drain()
            rb2 = bass.AP(gm, 32, [[64, 128], [1, 16], [0, 16]])
            inst = v.tensor_tensor(
                bass.AP(gchb_t, gchb_o, [[gchb_t.shape[1], 128], [16, 16], [1, 16]]),
                g3(gt1), rb2, ALU.mult)
            v.drain()
            S["dv_gate"].inc(inst)

        for ch in range(16):
            @blk.tensor
            def _(t, ch=ch):
                _w(t, S["dv_gate"])
                _w(t, S["dv_evtr"])
                inst = t.transpose(
                    ps_tr[0:16, (ch % 8) * 128:(ch % 8) * 128 + 128],
                    bass.AP(gchb_t, gchb_o + ch * 16,
                            [[gchb_t.shape[1], 128], [1, 16]]),
                    ident[:, :])
                S["pe_tr"].inc(inst)

            @blk.vector
            def _(v, ch=ch):
                _w(v, S["pe_tr"])
                inst = v.tensor_copy(gwt[0:16, ch * 128:(ch + 1) * 128],
                                     ps_tr[0:16, (ch % 8) * 128:(ch % 8) * 128 + 128])
                S["dv_evtr"].inc(inst)

        if PHASE == "ln2":
            @blk.vector
            def _(v):
                _w(v, S["dv_gate"])
                _w(v, S["dv_xh"])
                _w(v, S["gp_sel"])
                _w(v, S["dv_evtr"])
                inst = None
                if os.environ.get("KMK_DUMP", "xhat2") == "gwt":
                    inst = v.tensor_copy(x_res[0:32, 0:T], gwt[0:32, 0:T])
                    inst = v.tensor_copy(x_res[0:32, 3 * T:3 * T + 2048],
                                         qr_feat[0:32, 0:2048])
                    inst = v.tensor_copy(
                        x_res[0:128, 2 * T:2 * T + 256],
                        bass.AP(gchb_t, gchb_o, [[gchb_t.shape[1], 128], [1, 256]]))
                else:
                    for dk in range(DK):
                        inst = v.tensor_copy(x_res[:, dk * T:(dk + 1) * T],
                                             xhat2_ap(dk, 0, T))
                S["dv_oadd"].inc(inst)

            @blk.sync
            def _(sy):
                _w(sy, S["dv_oadd"])
                _w(sy, S["ssc"])
                src = bass.AP(x_res, 0, [[DK * T, 128], [T, DK], [1, T]])
                dst = bass.AP(out_d, 0, [[T, 128], [128 * T, DK], [1, T]])
                sy.dma_start(dst, src).then_inc(S["sst"].h, 16)
                S["sst"].n += 16
                sy.wait_ge(S["sst"].h, S["sst"].n)
            return nc

        for fblk in range(8):
            @blk.vector
            def _(v):
                _w(v, S["pe_w2"])
                _w(v, S["pe_state"])
                _w(v, S["pe_y"])
                inst = None
                for i in range(4):
                    inst = v.memset(ps_s[i][:, :], 0.0)
                S["dv_ms"].inc(inst)

            @blk.tensor
            def _(t):
                _w(t, S["dv_ms"])
                _w(t, S["dv_gate"])
                _w(t, S["swl"])
                inst = None
                for dk in range(DK):
                    bank, colo = ps_s[dk // 2], (dk % 2) * 256
                    inst = t.matmul(bank[0:128, colo:colo + 256],
                                     b2_sb[0:17, dk * 128:(dk + 1) * 128],
                                     gwt[0:17, fblk * 256:(fblk + 1) * 256],
                                     start=False, stop=True)
                S["pe_misc"].inc(inst)

            for u in range(E_UNITS):
                ui = fblk * E_UNITS + u
                par = ui % 2

                @blk.sync
                def _(sy):
                    if ui >= 2:
                        sy.wait_ge(S["pe_w2"].h, ui - 1)
                    else:
                        sy.wait_ge(S["ssc"].h, NCONV * 16)
                        _w(sy, S["dv_gate"])
                        _w(sy, S["dv_xb"])
                    if fblk == 0:
                        sy.wait_ge(S["ssc"].h, min(NCONV, 3 * (u + 1)) * 16)
                    for j, buf in enumerate([wu1[par], wu3[par], wu2[par]]):
                        src = bass.AP(wsc, (u * 3 + j) * 128 * 4096,
                                      [[4096, 128], [1, 4096]])
                        sy.dma_start(buf[:, :], src).then_inc(S["sfu"].h, 16)
                        S["sfu"].n += 16

                @blk.tensor
                def _(t):
                    _w(t, S["sfu"])
                    _w(t, S["dv_gb"])
                    _w(t, S["dv_xh"])
                    _w(t, S["dv_ms"])
                    _w(t, S["gp_sel"])
                    inst = t.matmul(ps_t0[0:128, 0:256], sel_ap(u),
                                    gwt[0:32, fblk * 256:(fblk + 1) * 256],
                                    start=True, stop=True)
                    S["pe_gb"].inc(inst)

                @blk.vector
                def _(v):
                    _w(v, S["pe_gb"])
                    _w(v, S["dv_hev"])
                    inst = v.tensor_copy(bass.AP(gb_t, gb_o, [[256, 128], [1, 256]]),
                                         ps_t0[0:128, 0:256])
                    S["dv_gb"].inc(inst)

                rot = [ps_t1, ps_t2, ps_t1, ps_t2]

                def emit_ab(ci):
                    @blk.tensor
                    def _(t):
                        _w(t, S["dv_hev"], max(0, S["dv_hev"].n - 1))
                        bank = rot[ci]
                        for dk in range(DK):
                            t.matmul(
                                bank[0:128, 0:256],
                                wu1[par][:, dk * 512 + ci * 128:dk * 512 + ci * 128 + 128],
                                xhat2_ap(dk, fblk * 256, 256),
                                start=(dk == 0), stop=(dk == DK - 1))
                        for dk in range(DK):
                            inst = t.matmul(
                                bank[0:128, 256:512],
                                wu3[par][:, dk * 512 + ci * 128:dk * 512 + ci * 128 + 128],
                                xhat2_ap(dk, fblk * 256, 256),
                                start=(dk == 0), stop=(dk == DK - 1))
                        S["pe_ab"].inc(inst)

                    @blk.vector
                    def _(v):
                        v.wait_ge(S["ac_silu"].h, ac_base + ci + 1)
                        _w(v, S["pe_w2"])
                        v.scalar_tensor_tensor(
                            tb_t[:, ci * 256:(ci + 1) * 256],
                            rot[ci][0:128, 256:512],
                            b3c[:, u * 4 + ci:u * 4 + ci + 1],
                            bass.AP(gb_t, gb_o, [[256, 128], [1, 256]]),
                            ALU.add, ALU.mult)
                        inst = v.tensor_tensor(
                            h_t[:, ci * 256:(ci + 1) * 256],
                            tb_t[:, ci * 256:(ci + 1) * 256],
                            sa_t[:, ci * 256:(ci + 1) * 256], ALU.mult)
                        S["dv_hev"].inc(inst)

                def emit_w2(ci):
                    @blk.tensor
                    def _(t):
                        _w(t, S["dv_hev"],
                           S["dv_hev"].n - (1 if ci < 3 else 0))
                        inst = None
                        for dk in range(DK):
                            bank, colo = ps_s[dk // 2], (dk % 2) * 256
                            inst = t.matmul(
                                bank[0:128, colo:colo + 256],
                                wu2[par][:, ci * 1024 + dk * 128:ci * 1024 + dk * 128 + 128],
                                h_t[:, ci * 256:(ci + 1) * 256],
                                start=False, stop=True)
                        if ci == 3:
                            S["pe_w2"].inc(inst)
                        else:
                            S["pe_misc"].inc(inst)

                ab_base = S["pe_ab"].n
                hev_base = S["dv_hev"].n
                ac_base = S["ac_silu"].n

                @blk.scalar
                def _(s_):
                    for ci in range(4):
                        s_.wait_ge(S["pe_ab"].h, ab_base + ci + 1)
                        if hev_base + ci - 1 > 0:
                            s_.wait_ge(S["dv_hev"].h, hev_base + ci - 1)
                        inst = s_.activation(
                            sa_t[:, ci * 256:(ci + 1) * 256],
                            rot[ci][0:128, 0:256], AF.Silu,
                            bias=b1c[:, u * 4 + ci:u * 4 + ci + 1], scale=1.0)
                        S["ac_silu"].inc(inst)

                emit_ab(0)
                emit_ab(1)
                emit_w2(0)
                emit_ab(2)
                emit_w2(1)
                emit_ab(3)
                emit_w2(2)
                emit_w2(3)

            @blk.vector
            def _(v):
                _w(v, S["pe_w2"])
                _w(v, S["pe_misc"])
                inst = None
                for i in range(4):
                    oo = bass.AP(x_res, 2 * i * T + fblk * 256,
                                 [[DK * T, 128], [T, 2], [1, 256]])
                    inst = v.tensor_tensor(
                        oo, oo,
                        bass.AP(ps_s[i], 0, [[512, 128], [256, 2], [1, 256]]),
                        ALU.add)
                S["dv_oadd"].inc(inst)

        @blk.sync
        def _(sy):
            _w(sy, S["dv_oadd"])
            src = bass.AP(x_res, 0, [[DK * T, 128], [T, DK], [1, T]])
            dst = bass.AP(out_d, 0, [[T, 128], [128 * T, DK], [1, T]])
            sy.dma_start(dst, src).then_inc(S["sst"].h, 16)
            S["sst"].n += 16
            sy.wait_ge(S["sst"].h, S["sst"].n)

    return nc


_NC_CACHE = {}


def _get_nc():
    key = (os.environ.get("KMK_PHASE", "full"), os.environ.get("KMK_GROUPS", "17"),
           os.environ.get("KMK_CONV", "1"))
    if key not in _NC_CACHE:
        _NC_CACHE[key] = build_program()
    return _NC_CACHE[key]


def kernel(**inp):
    f32 = np.float32
    x = np.asarray(inp["L0_attn_x"], dtype=f32)
    wq = np.asarray(inp["L0_attn_Wq"], f32)
    wk = np.asarray(inp["L0_attn_Wk"], f32)
    wv = np.asarray(inp["L0_attn_Wv"], f32)
    wke = np.asarray(inp["L0_attn_Wk_enc"], f32)
    wve = np.asarray(inp["L0_attn_Wv_enc"], f32)
    wo = np.asarray(inp["L0_attn_Wout"], f32)
    gw = np.asarray(inp["L0_ffn_gate_w"], f32)
    eb = np.asarray(inp["L0_ffn_expert_bias"], f32)
    w1 = np.asarray(inp["L0_ffn_w1"], f32)
    b1 = np.asarray(inp["L0_ffn_b1"], f32)
    w2 = np.asarray(inp["L0_ffn_w2"], f32)
    b2 = np.asarray(inp["L0_ffn_b2"], f32)
    w3 = np.asarray(inp["L0_ffn_w3"], f32)
    b3 = np.asarray(inp["L0_ffn_b3"], f32)
    sw1 = np.asarray(inp["L0_ffn_sw1"], f32)
    sb1 = np.asarray(inp["L0_ffn_sb1"], f32)
    sw2 = np.asarray(inp["L0_ffn_sw2"], f32)
    sb2 = np.asarray(inp["L0_ffn_sb2"], f32)
    sw3 = np.asarray(inp["L0_ffn_sw3"], f32)
    sb3 = np.asarray(inp["L0_ffn_sb3"], f32)

    C = np.ascontiguousarray

    def tile_t(wt):
        ncols = wt.shape[1]
        return C(wt.reshape(DK, 128, ncols).transpose(1, 0, 2)
                 .reshape(128, DK * ncols))

    def tile_k(wt, kch):
        ncols = wt.shape[1]
        return wt.reshape(kch, 128, ncols).transpose(1, 0, 2).reshape(
            128, kch * ncols)

    wsrc = np.empty((54, 128, 4096), f32)
    i = 0
    for e in range(16):
        wsrc[i] = tile_k(w1[e].T, 8); i += 1
        wsrc[i] = tile_k(w3[e].T, 8); i += 1
        wsrc[i] = tile_k(w2[e].T, 4); i += 1
    for u_ in range(2):
        wsrc[i] = tile_k(sw1.T[:, u_ * 512:(u_ + 1) * 512], 8); i += 1
        wsrc[i] = tile_k(sw3.T[:, u_ * 512:(u_ + 1) * 512], 8); i += 1
        wsrc[i] = tile_k(sw2.T[u_ * 512:(u_ + 1) * 512], 4); i += 1
    shared = {
        "wqT": tile_t(C(wq.T)), "wkT": tile_t(C(wk.T)), "wvT": tile_t(C(wv.T)),
        "wkeT": tile_t(C(wke.T)), "wveT": tile_t(C(wve.T)),
        "woT": tile_t(C(wo.T)),
        "gwTm": tile_t(C(gw.T)), "ebr": C(eb[None, :]),
        "wsrc": C(wsrc.reshape(54 * 128, 4096)),
        "b2m": C(np.concatenate([b2, sb2[None, :]], axis=0)),
    }
    b1all = np.concatenate([b1, sb1.reshape(2, 512)], axis=0)
    b3all = np.concatenate([b3, sb3.reshape(2, 512)], axis=0)
    shared["b1c"] = C(b1all.reshape(E_UNITS, 4, 128).transpose(2, 0, 1)
                      .reshape(128, E_UNITS * 4))
    shared["b3c"] = C(b3all.reshape(E_UNITS, 4, 128).transpose(2, 0, 1)
                      .reshape(128, E_UNITS * 4))
    sc = np.zeros((128, 1), f32)
    sc[0:16] = 0.25
    sc[32:48] = 1.0 / 32.0
    sc[48] = 1.0
    sc[64:80] = 0.25
    sc[96:112] = 1.0 / 32.0
    sc[112] = 1.0
    shared["scol"] = sc

    g_tok0 = [0] + [512 * g for g in range(12)] + [PRE + 512 * g for g in range(4)]
    in_maps = []
    for core in range(8):
        b, c = core // 4, core % 4
        xb = x[b].T
        xTc = np.concatenate([xb[:, :PRE], xb[:, c * T:(c + 1) * T]], axis=1)
        xtt = tile_k(xTc, DK).reshape(128, DK, PRE + T)
        xgt = np.empty((128, 17, 4096), f32)
        for gi, t0 in enumerate(g_tok0):
            xgt[:, gi] = xtt[:, :, t0:t0 + 512].reshape(128, DK * 512)
        xot = C(xtt[:, :, PRE:].reshape(128, DK * T))
        m = np.zeros((128, NCH), f32)
        lim = c * T
        for j in range(NCH):
            base = j * 128
            if base + 128 <= lim:
                m[:, j] = 1.0
            elif base < lim:
                m[0:lim - base, j] = 1.0
        d = dict(shared)
        d["xgt"] = C(xgt.reshape(128, 17 * 4096))
        d["xot"] = xot
        d["mask"] = m
        in_maps.append(d)

    nc = _get_nc()
    res = run_bass_kernel_spmd(nc, in_maps, core_ids=list(range(8)))
    y = np.empty((B_, N_, D), f32)
    for core in range(8):
        b, c = core // 4, core % 4
        y[b, c * T:(c + 1) * T, :] = res.results[core]["out"].T
    return y


# version 13
# speedup vs baseline: 1.1449x; 1.0126x over previous
import contextlib
import os
import numpy as np
import ml_dtypes
import concourse.bass as bass
import concourse.mybir as mybir
from concourse.bass_utils import run_bass_kernel_spmd

F32 = mybir.dt.float32
BF16 = mybir.dt.bfloat16
AF = mybir.ActivationFunctionType
ALU = mybir.AluOpType
AXX = mybir.AxisListType.X

B_, N_, D = 2, 8192, 1024
H, FD, HD = 16, 16, 64
T = 2048
PRE = 6144
NCH = PRE // 128
DK = D // 128
NPAIR = 120
RW = 49
E_UNITS = 18
DFF = 512
LN_EPS = 1e-5
ATT_EPS = 1e-12
GATE_EPS = 1e-10
NEG_BIG = -1e30


class Sem:
    def __init__(self, stack, nc, name):
        self.h = stack.enter_context(nc.semaphore(name))
        self.n = 0

    def inc(self, inst, k=1):
        assert inst is not None
        inst.then_inc(self.h, k)
        self.n += k


def build_program():
    nc = bass.Bass("TRN2", target_bir_lowering=False)

    def _w(eng, sem, n=None):
        v = sem.n if n is None else n
        assert v <= sem.n, f"future wait {v} > {sem.n}"
        if v > 0:
            eng.wait_ge(sem.h, v)

    xg_d = nc.dram_tensor("xgt", [128, 17 * 4096], F32, kind="ExternalInput")
    xo_d = nc.dram_tensor("xot", [128, DK * T], F32, kind="ExternalInput")
    mask_d = nc.dram_tensor("mask", [128, NCH], F32, kind="ExternalInput")
    scol_d = nc.dram_tensor("scol", [128, 1], F32, kind="ExternalInput")
    wq_d = nc.dram_tensor("wqT", [128, DK * 256], F32, kind="ExternalInput")
    wk_d = nc.dram_tensor("wkT", [128, DK * 256], F32, kind="ExternalInput")
    wv_d = nc.dram_tensor("wvT", [128, DK * 1024], F32, kind="ExternalInput")
    wke_d = nc.dram_tensor("wkeT", [128, DK * 256], F32, kind="ExternalInput")
    wve_d = nc.dram_tensor("wveT", [128, DK * 1024], F32, kind="ExternalInput")
    wo_d = nc.dram_tensor("woT", [128, DK * 1024], F32, kind="ExternalInput")
    gw_d = nc.dram_tensor("gwTm", [128, DK * 16], F32, kind="ExternalInput")
    ebr_d = nc.dram_tensor("ebr", [1, 16], F32, kind="ExternalInput")
    wsrc_d = nc.dram_tensor("wsrc", [54 * 128, 4096], F32, kind="ExternalInput")
    b1c_d = nc.dram_tensor("b1c", [128, 72], F32, kind="ExternalInput")
    b3c_d = nc.dram_tensor("b3c", [128, 72], F32, kind="ExternalInput")
    b2_d = nc.dram_tensor("b2m", [17, D], F32, kind="ExternalInput")
    out_d = nc.dram_tensor("out", [D, T], F32, kind="ExternalOutput")
    wsc = nc.dram_tensor("wsc", [E_UNITS * 3 * 128, 4096], BF16, kind="Internal")

    st = contextlib.ExitStack()
    with st:
        def sb(name, shape, dt):
            return st.enter_context(nc.sbuf_tensor(name, shape, dt))

        x_res = sb("x_res", [128, DK * T], F32)
        ones_rw = sb("ones_rw", [1, 256], BF16)
        onesmat = sb("onesmat", [128, 128], BF16)
        ident = sb("ident", [128, 128], BF16)
        tril = sb("tril", [128, 128], BF16)
        onescol = sb("onescol", [128, 1], F32)
        onescol_bf = sb("onescol_bf", [128, 1], BF16)
        eb16 = sb("eb16", [128, 16], F32)
        scol = sb("scol_s", [128, 1], F32)
        mcol = sb("mcol", [128, NCH], F32)
        mcol_bf = sb("mcol_bf", [128, NCH], BF16)
        ebrow = sb("ebrow", [1, 16], BF16)
        cvals = sb("cvals", [128, 2], F32)
        wq_sb = sb("wq_sb", [128, DK * 256], BF16)
        wk_sb = sb("wk_sb", [128, DK * 256], BF16)
        wv_sb = sb("wv_sb", [128, DK * 1024], BF16)
        wo_sb = sb("wo_sb", [128, DK * 1024], BF16)
        gw_sb = sb("gw_sb", [128, DK * 16], BF16)
        xhx = sb("xhx", [128, DK * 512], BF16)
        mb_s = sb("mb_s", [128, 512], F32)
        var_s = sb("var_s", [128, 512], F32)
        wu1 = [sb(f"wu1_{i}", [128, 4096], BF16) for i in range(2)]
        wu3 = [sb(f"wu3_{i}", [128, 4096], BF16) for i in range(2)]
        wu2 = [sb(f"wu2_{i}", [128, 4096], BF16) for i in range(2)]
        ktok = sb("ktok", [128, 256], BF16)
        qq = sb("qq", [128, H * 32], BF16)
        kk = sb("kk", [128, H * 32], BF16)
        qr_tok = sb("qr_tok", [128, 8 * 128], BF16)
        kr_tok = sb("kr_tok", [128, 8 * 128], BF16)
        vp = sb("vp", [128, H * 65], BF16)
        qr_feat = sb("qr_feat", [128, H * 128], BF16)
        kr_feat = sb("kr_feat", [128, H * 128], BF16)
        sbp = sb("sbp", [128, H * 64], BF16)
        sbpd = sb("sbpd", [128, 16], BF16)
        sbr = sb("sbr", [128, H * 65], BF16)
        yp = sb("yp", [128, 8 * 128], BF16)
        ztmp = sb("ztmp", [1, 1024], F32)
        zr = sb("zr", [1, 512], BF16)
        ftmp = sb("ftmp", [128, 2 * 512], BF16)
        zb_sb = sb("zb_sb", [128, 512], BF16)
        xb_g = wu1[1]
        qp_tok_t, qp_tok_o = wu3[1], 0
        kp_tok_t, kp_tok_o = wu3[1], 1920
        qtok_t, qtok_o = wu3[1], 3840
        pt_t, pt_o = wu2[1], 0
        qp_feat_t, qp_feat_o = wu2[1], 2048
        gwt = sb("gwt", [32, T], BF16)
        glog = sb("glog", [128, 256], F32)
        gt0 = sb("gt0", [128, 256], F32)
        gt1 = sb("gt1", [128, 256], F32)
        gt2 = sb("gt2", [128, 256], F32)
        gm = sb("gm", [128, 64], F32)
        b1c = sb("b1c_s", [128, 72], F32)
        b3c = sb("b3c_s", [128, 72], F32)

        h_t = kr_feat
        sa_t = qr_tok
        tb_t = kr_tok
        gb_t, gb_o = ktok, 0
        b2_sb = yp
        sb2_sb_row = 32
        gchb_t, gchb_o = qq, 0

        def sel_ap(u):
            if u < 16:
                return qr_feat[0:32, u * 128:(u + 1) * 128]
            return sbr[0:32, (u - 16) * 128:(u - 15) * 128]

        ps_s = [st.enter_context(nc.psum_tensor(f"ps_s{i}", [128, 512], F32))
                for i in range(4)]
        ps_t0 = st.enter_context(nc.psum_tensor("ps_t0", [128, 512], F32))
        ps_t1 = st.enter_context(nc.psum_tensor("ps_t1", [128, 512], F32))
        ps_t2 = st.enter_context(nc.psum_tensor("ps_t2", [128, 512], F32))
        ps_tr = st.enter_context(nc.psum_tensor("ps_tr", [128, 1024], BF16))

        def pairs_v_rg(hh):
            return ps_s[hh // 8], (hh % 8) * 64

        def rest_rg(p):
            return (ps_s[2], 65 * p) if p < 7 else (ps_s[3], 0)

        SEMS = ["sx", "swl", "scv", "ssc", "sfu", "sst",
                "pe_stat", "pe_proj", "pe_v", "pe_tr", "pe_g", "pe_y",
                "pe_zb", "pe_wout", "pe_state", "pe_gate", "pe_gb",
                "pe_ab", "pe_w2", "pe_misc",
                "dv_xb", "dv_row", "dv_xh", "dv_eqk", "dv_bld",
                "dv_evtr", "dv_cast", "dv_f", "dv_z", "dv_evy", "dv_wadd",
                "dv_gate", "dv_hev", "dv_oadd", "dv_init", "dv_gb", "dv_ms",
                "gp_sel", "ac_row", "ac_silu", "ac_f"]
        S = {nm: Sem(st, nc, nm) for nm in SEMS}

        blk = st.enter_context(nc.Block())

        @blk.sync
        def _(sy):
            sy.dma_start(mcol[:, :], mask_d.ap()).then_inc(S["sx"].h, 16)
            sy.dma_start(scol[:, :], scol_d.ap()).then_inc(S["sx"].h, 16)
            S["sx"].n += 32
            sy.dma_start(x_res[:, :], xo_d.ap()).then_inc(S["sx"].h, 16)
            S["sx"].n += 16

        @blk.gpsimd
        def _(g):
            g.memset(ones_rw[:, :], 1.0)
            g.memset(onesmat[:, :], 1.0)
            g.memset(onescol[:, :], 1.0)
            g.memset(onescol_bf[:, :], 1.0)
            g.memset(ident[:, :], 1.0)
            g.affine_select(ident[:, :], ident[:, :], [[1, 128]],
                            ALU.is_equal, 0.0, base=0, channel_multiplier=-1)
            g.memset(tril[:, :], 1.0)
            g.affine_select(tril[:, :], tril[:, :], [[1, 128]],
                            ALU.is_ge, 0.0, base=0, channel_multiplier=-1)
            g.memset(qr_tok[:, :], 0.0)
            g.memset(kr_tok[:, :], 0.0)
            g.memset(bass.AP(qr_tok, 48, [[8 * 128, 128], [64, 16], [1, 1]]), 1.0)
            g.memset(cvals[:, 0:1], LN_EPS)
            g.memset(cvals[:, 1:2], 0.125)
            g.memset(gwt[:, :], 1.0)
            g.affine_select(gwt[0:32, :], gwt[0:32, :], [[0, T]],
                            ALU.is_ge, 0.0, base=-16, channel_multiplier=1)
            g.affine_select(gwt[0:32, :], gwt[0:32, :], [[0, T]],
                            ALU.is_ge, 0.0, base=17, channel_multiplier=-1)
            inst = g.memset(vp[:, :], 0.0)
            S["dv_init"].inc(inst)
            g.dma_start(mcol_bf[:, :], mask_d.ap()).then_inc(S["swl"].h, 16)
            g.dma_start(ebrow[:, :], ebr_d.ap()).then_inc(S["swl"].h, 16)
            S["swl"].n += 32

            def wload(dst_sb, src_d, ncols):
                g.dma_start(dst_sb[:, :], src_d.ap()).then_inc(S["swl"].h, 16)
                S["swl"].n += 16

            wload(wq_sb, wq_d, 256)
            wload(wk_sb, wk_d, 256)
            wload(wv_sb, wv_d, 1024)
            wload(wo_sb, wo_d, 1024)
            wload(gw_sb, gw_d, 16)
            g.dma_start(b1c[:, :], b1c_d.ap()).then_inc(S["swl"].h, 16)
            g.dma_start(b3c[:, :], b3c_d.ap()).then_inc(S["swl"].h, 16)
            S["swl"].n += 32
            g.dma_start(wu2[0][:, 0:DK * 256], wke_d.ap()).then_inc(S["swl"].h, 16)
            src = bass.AP(wve_d, 0, [[DK * 1024, 128], [1, 4096]])
            g.dma_start(wu1[0][:, :], src).then_inc(S["swl"].h, 16)
            src = bass.AP(wve_d, 4096, [[DK * 1024, 128], [1, 4096]])
            g.dma_start(wu3[0][:, :], src).then_inc(S["swl"].h, 16)
            S["swl"].n += 48

        def wke_ap(dk):
            return wu2[0][:, dk * 256:(dk + 1) * 256]

        def wve_ap(dk, half):
            tt = wu1[0] if dk < 4 else wu3[0]
            return tt[:, (dk % 4) * 1024 + half * 512:(dk % 4) * 1024 + half * 512 + 512]

        @blk.vector
        def _(v):
            for i in range(4):
                v.memset(ps_s[i][:, :], 0.0)
            inst = v.memset(ps_t2[:, :], 0.0)
            S["dv_init"].inc(inst)

        @blk.tensor
        def _(t):
            _w(t, S["swl"])
            _w(t, S["dv_init"])
            inst = t.matmul(ps_t0[0:128, 0:16], ones_rw[0:1, 0:128],
                            ebrow[0:1, 0:16], start=True, stop=True)
            S["pe_misc"].inc(inst)

        @blk.vector
        def _(v):
            _w(v, S["pe_misc"])
            inst = v.tensor_copy(eb16[:, :], ps_t0[0:128, 0:16])
            S["dv_init"].inc(inst)

        NCONV = 54
        conv_stage = [wu1[0], wu3[0], wu2[0]]

        def _conv_src(i):
            return bass.AP(wsrc_d, i * 128 * 4096, [[4096, 128], [1, 4096]])

        conv_emitted = [0]

        def emit_conv(k):
            n0 = conv_emitted[0]
            n1 = min(NCONV, n0 + k)
            if n1 == n0:
                return
            conv_emitted[0] = n1

            @blk.gpsimd
            def _(g):
                for i in range(n0, n1):
                    buf = conv_stage[i % 3]
                    if i == 0:
                        g.wait_ge(S["pe_state"].h, 4)
                    if i >= 3:
                        g.wait_ge(S["ssc"].h, (i - 3 + 1) * 16)
                    g.dma_start(buf[:, :], _conv_src(i)).then_inc(
                        S["scv"].h, 16)
                    S["scv"].n += 16

            @blk.sync
            def _(sy):
                for i in range(n0, n1):
                    buf = conv_stage[i % 3]
                    sy.wait_ge(S["scv"].h, (i + 1) * 16)
                    dst = bass.AP(wsc, i * 128 * 4096, [[4096, 128], [1, 4096]])
                    sy.dma_start(dst, buf[:, :]).then_inc(S["ssc"].h, 16)
                    S["ssc"].n += 16

        def ln_group(tok0, from_res, first=False, gidx=0):
            @blk.gpsimd
            def _(g):
                if not from_res:
                    if S["pe_stat"].n > 0:
                        g.wait_ge(S["pe_stat"].h, S["pe_stat"].n)
                    if S["dv_xh"].n > 0:
                        g.wait_ge(S["dv_xh"].h, S["dv_xh"].n)
                    src = bass.AP(xg_d, gidx * 4096,
                                  [[17 * 4096, 128], [1, 4096]])
                    g.dma_start(xb_g[:, :], src).then_inc(S["sx"].h, 16)
                    S["sx"].n += 16

            @blk.vector
            def _(v):
                if from_res:
                    _w(v, S["pe_stat"])
                    src = bass.AP(x_res, tok0, [[DK * T, 128], [T, DK], [1, 512]])
                    v.tensor_copy(xb_g[:, :], src)
                else:
                    _w(v, S["sx"])
                _w(v, S["pe_v"])
                _w(v, S["pe_gate"])
                inst = v.tensor_tensor(xhx[:, :], xb_g[:, :], xb_g[:, :],
                                       ALU.mult)
                S["dv_xb"].inc(inst)

            @blk.tensor
            def _(t):
                _w(t, S["dv_xb"])
                if first:
                    _w(t, S["dv_init"])
                _w(t, S["dv_eqk"])
                _w(t, S["dv_row"])
                _w(t, S["dv_wadd"])
                _w(t, S["dv_bld"])
                _w(t, S["dv_evy"])
                for dk in range(DK):
                    t.matmul(ps_t0[0:128, 0:512], onesmat[:, :],
                             xb_g[:, dk * 512:(dk + 1) * 512],
                             start=(dk == 0), stop=(dk == DK - 1))
                for dk in range(DK):
                    inst = t.matmul(ps_t1[0:128, 0:512], onesmat[:, :],
                                    xhx[:, dk * 512:(dk + 1) * 512],
                                    start=(dk == 0), stop=(dk == DK - 1))
                S["pe_stat"].inc(inst)

            @blk.vector
            def _(v):
                _w(v, S["pe_stat"])
                v.tensor_scalar(mb_s[:, :], ps_t0[0:128, 0:512],
                                1.0 / D, None, ALU.mult)
                v.tensor_tensor(var_s[:, :], mb_s[:, :], mb_s[:, :], ALU.mult)
                inst = v.scalar_tensor_tensor(var_s[:, :], ps_t1[0:128, 0:512],
                                              1.0 / D, var_s[:, :],
                                              ALU.mult, ALU.subtract)
                S["dv_row"].inc(inst)

            @blk.scalar
            def _(s):
                _w(s, S["dv_row"])
                _w(s, S["dv_init"])
                inst = s.activation(var_s[:, :], var_s[:, :], AF.Sqrt,
                                    bias=cvals[:, 0:1], scale=1.0)
                S["ac_row"].inc(inst)

            @blk.vector
            def _(v):
                _w(v, S["ac_row"])
                v.reciprocal(var_s[:, :], var_s[:, :])
                mb_b = bass.AP(mb_s, 0, [[512, 128], [0, DK], [1, 512]])
                rb_b = bass.AP(var_s, 0, [[512, 128], [0, DK], [1, 512]])
                xb3 = bass.AP(xb_g, 0, [[DK * 512, 128], [512, DK], [1, 512]])
                if from_res:
                    _w(v, S["pe_wout"])
                    xh3a = xhat2_ap3(tok0)
                    t3 = bass.AP(xhx, 0, [[DK * 512, 128], [512, DK], [1, 512]])
                    v.tensor_tensor(t3, xb3, mb_b, ALU.subtract)
                    ta = bass.AP(xhx, 0, [[DK * 512, 128], [512, 4], [1, 512]])
                    tb = bass.AP(xhx, 4 * 512, [[DK * 512, 128], [512, 4], [1, 512]])
                    ra = bass.AP(var_s, 0, [[512, 128], [0, 4], [1, 512]])
                    v.tensor_tensor(xh3a[0], ta, ra, ALU.mult)
                    inst = v.tensor_tensor(xh3a[1], tb, ra, ALU.mult)
                else:
                    xh3 = bass.AP(xhx, 0, [[DK * 512, 128], [512, DK], [1, 512]])
                    v.tensor_tensor(xh3, xb3, mb_b, ALU.subtract)
                    inst = v.tensor_tensor(xh3, xh3, rb_b, ALU.mult)
                S["dv_xh"].inc(inst)

        def xhat2_ap(dk, c0, w):
            tt = wv_sb if dk < 4 else wo_sb
            o = (dk % 4) * T + c0
            return tt[:, o:o + w]

        def xhat2_ap3(tok0):
            a = bass.AP(wv_sb, tok0, [[4 * T, 128], [T, 4], [1, 512]])
            b = bass.AP(wo_sb, tok0, [[4 * T, 128], [T, 4], [1, 512]])
            return a, b

        def attn_chunk(kind, tokabs, cidx):
            is_own = kind == "own"
            if kind == "pre":
                jm, mct, mcp = tokabs // 128, mcol, NCH
                mc = mcol[:, jm:jm + 1]
                mcb = mcol_bf[:, jm:jm + 1]
            else:
                jm, mct, mcp = 0, onescol, 1
                mc = onescol[:, 0:1]
                mcb = onescol_bf[:, 0:1]

            def xh(dk):
                c = dk * 512 + cidx * 128
                return xhx[:, c:c + 128]

            def wk_ap(dk):
                if kind == "enc":
                    return wke_ap(dk)
                return wk_sb[:, dk * 256:(dk + 1) * 256]

            def wv_ap(dk, half):
                if kind == "enc":
                    return wve_ap(dk, half)
                return wv_sb[:, dk * 1024 + half * 512:dk * 1024 + half * 512 + 512]

            @blk.tensor
            def _(t):
                _w(t, S["dv_xh"])
                _w(t, S["dv_eqk"])
                _w(t, S["dv_wadd"])
                _w(t, S["dv_row"])
                for dk in range(DK):
                    t.matmul(ps_t0[0:128, 0:256], xh(dk), wk_ap(dk),
                             start=(dk == 0), stop=(dk == DK - 1))
                if is_own:
                    for dk in range(DK):
                        t.matmul(ps_t0[0:128, 256:512], xh(dk),
                                 wq_sb[:, dk * 256:(dk + 1) * 256],
                                 start=(dk == 0), stop=(dk == DK - 1))
                _w(t, S["dv_bld"])
                _w(t, S["dv_evy"])
                for dk in range(DK):
                    inst = t.matmul(ps_t1[0:128, 0:512], xh(dk), wv_ap(dk, 0),
                                    start=(dk == 0), stop=(dk == DK - 1))
                S["pe_proj"].inc(inst)

            @blk.vector
            def _(v):
                _w(v, S["pe_proj"])
                _w(v, S["pe_state"])
                _w(v, S["pe_y"])
                _w(v, S["sx"], 48)
                v.tensor_copy(ktok[:, :], ps_t0[0:128, 0:256])
                if is_own:
                    v.tensor_copy(bass.AP(qtok_t, qtok_o, [[4096, 128], [1, 256]]),
                                  ps_t0[0:128, 256:512])
                ksrc = bass.AP(ktok, 0, [[256, 128], [16, H], [1, 16]])
                v.tensor_copy(bass.AP(kk, 0, [[H * 32, 128], [32, H], [1, 16]]),
                              ksrc)
                v.tensor_copy(bass.AP(kk, 16, [[H * 32, 128], [32, H], [1, 16]]),
                              ksrc)
                if is_own:
                    qsrc = bass.AP(qtok_t, qtok_o, [[4096, 128], [16, H], [1, 16]])
                    v.tensor_copy(
                        bass.AP(qq, 0, [[H * 32, 128], [32, H], [1, 16]]), qsrc)
                    v.tensor_copy(
                        bass.AP(qq, 16, [[H * 32, 128], [32, H], [1, 16]]), qsrc)
                dstA = bass.AP(vp, 0, [[H * 65, 128], [65, 8], [1, 64]])
                srcA = bass.AP(ps_t1, 0, [[512, 128], [64, 8], [1, 64]])
                v.tensor_scalar(dstA, srcA, mc, None, ALU.mult)
                dstO = bass.AP(vp, 64, [[H * 65, 128], [65, 8], [1, 1]])
                srcO = bass.AP(mct, jm, [[mcp, 128], [0, 8], [1, 1]])
                inst = v.tensor_copy(dstO, srcO)
                S["dv_eqk"].inc(inst)

            @blk.tensor
            def _(t):
                _w(t, S["dv_eqk"])
                for dk in range(DK):
                    inst = t.matmul(ps_t1[0:128, 0:512], xh(dk), wv_ap(dk, 1),
                                    start=(dk == 0), stop=(dk == DK - 1))
                S["pe_v"].inc(inst)

            @blk.vector
            def _(v):
                _w(v, S["pe_v"])
                _w(v, S["pe_tr"])
                dstB = bass.AP(vp, 8 * 65, [[H * 65, 128], [65, 8], [1, 64]])
                srcB = bass.AP(ps_t1, 0, [[512, 128], [64, 8], [1, 64]])
                v.tensor_scalar(dstB, srcB, mc, None, ALU.mult)
                dstO = bass.AP(vp, 8 * 65 + 64, [[H * 65, 128], [65, 8], [1, 1]])
                srcO = bass.AP(mct, jm, [[mcp, 128], [0, 8], [1, 1]])
                v.tensor_copy(dstO, srcO)

                def build(pt_tt, pt_oo, rtile, dup, tok_ap):
                    for dd in range(1, 8):
                        i0 = bass.AP(dup, 0, [[H * 32, 128], [32, H], [1, 16]])
                        i1 = bass.AP(dup, dd, [[H * 32, 128], [32, H], [1, 16]])
                        oo = bass.AP(pt_tt, pt_oo + (dd - 1) * 16,
                                     [[4096 if pt_tt in (wu3[1],) else H * NPAIR, 128],
                                      [NPAIR, H], [1, 16]])
                        v.tensor_tensor(oo, i0, i1, ALU.mult)
                    i0 = bass.AP(dup, 0, [[H * 32, 128], [32, H], [1, 8]])
                    i1 = bass.AP(dup, 8, [[H * 32, 128], [32, H], [1, 8]])
                    oo = bass.AP(pt_tt, pt_oo + 112,
                                 [[4096 if pt_tt in (wu3[1],) else H * NPAIR, 128],
                                  [NPAIR, H], [1, 8]])
                    v.tensor_tensor(oo, i0, i1, ALU.mult)
                    src = bass.AP(tok_ap[0], tok_ap[1],
                                  [[tok_ap[2], 128], [32, 8], [16, 2], [1, 16]])
                    dl = bass.AP(rtile, 0,
                                 [[8 * 128, 128], [128, 8], [64, 2], [1, 16]])
                    v.tensor_copy(dl, src)
                    dd2 = bass.AP(rtile, 32,
                                  [[8 * 128, 128], [128, 8], [64, 2], [1, 16]])
                    return v.tensor_tensor(dd2, src, src, ALU.mult)

                inst = build(kp_tok_t, kp_tok_o, kr_tok, kk, (ktok, 0, 256))
                dm = bass.AP(kr_tok, 48, [[8 * 128, 128], [128, 8], [64, 2], [1, 1]])
                sm = bass.AP(mct, jm, [[mcp, 128], [0, 8], [0, 2], [1, 1]])
                inst = v.tensor_copy(dm, sm)
                if is_own:
                    inst = build(qp_tok_t, qp_tok_o, qr_tok, qq,
                                 (qtok_t, qtok_o, 4096))
                S["dv_bld"].inc(inst)

            if is_own and OWNLVL >= 1:
                @blk.vector
                def _(v):
                    _w(v, S["pe_state"])
                    _w(v, S["pe_y"])
                    v.tensor_scalar(sbp[0:120, 0:512], ps_s[0][0:120, 0:512],
                                    1.0 / 16.0, None, ALU.mult)
                    v.tensor_scalar(sbp[0:120, 512:1024], ps_s[1][0:120, 0:512],
                                    1.0 / 16.0, None, ALU.mult)
                    v.tensor_scalar(sbpd[0:120, 0:16], ps_s[2][0:120, 455:471],
                                    1.0 / 16.0, None, ALU.mult)
                    ev_out = bass.AP(sbr, 0, [[H * 65, RW], [130, 7], [1, 65]])
                    ev_in = ps_s[2][0:RW, 0:455].rearrange(
                        "p (r c) -> p r c", r=7)
                    v.tensor_scalar(ev_out, ev_in, scol[0:RW, 0:1], None,
                                    ALU.mult)
                    od_out = bass.AP(sbr, 65, [[H * 65, RW], [130, 7], [1, 65]])
                    od_in = ps_s[2][64:64 + RW, 0:455].rearrange(
                        "p (r c) -> p r c", r=7)
                    v.tensor_scalar(od_out, od_in, scol[0:RW, 0:1], None,
                                    ALU.mult)
                    v.tensor_scalar(sbr[0:RW, 14 * 65:14 * 65 + 65],
                                    ps_s[3][0:RW, 0:65],
                                    scol[0:RW, 0:1], None, ALU.mult)
                    inst = v.tensor_scalar(sbr[0:RW, 15 * 65:15 * 65 + 65],
                                           ps_s[3][64:64 + RW, 0:65],
                                           scol[0:RW, 0:1], None, ALU.mult)
                    S["dv_cast"].inc(inst)

                pass
            if is_own and OWNLVL >= 2:
                def rto(hh):
                    return (hh // 2) * 128 + (hh % 2) * 64

                tr_specs = [
                    ([(qp_tok_t, qp_tok_o + hh * NPAIR, NPAIR) for hh in range(8)],
                     (qp_feat_t, qp_feat_o + 0), NPAIR),
                    ([(qp_tok_t, qp_tok_o + (8 + hh) * NPAIR, NPAIR)
                      for hh in range(8)],
                     (qp_feat_t, qp_feat_o + 1024), NPAIR),
                    ([(qr_tok, rto(hh), RW) for hh in range(8)],
                     (qr_feat, 0), RW),
                    ([(qr_tok, rto(8 + hh), RW) for hh in range(8)],
                     (qr_feat, 1024), RW),
                    ([(kr_tok, rto(hh), RW) for hh in range(8)],
                     (kr_feat, 0), RW),
                    ([(kr_tok, rto(8 + hh), RW) for hh in range(8)],
                     (kr_feat, 1024), RW),
                ]
                for wv_i in range(6):
                    srcs_w, (dt_t, dt_o), rows = tr_specs[wv_i]

                    @blk.tensor
                    def _(t, srcs_w=srcs_w, rows=rows):
                        _w(t, S["dv_bld"])
                        _w(t, S["dv_evtr"])
                        for i8, (tt, o, wdt) in enumerate(srcs_w):
                            src = bass.AP(tt, o, [[tt.shape[1], 128], [1, wdt]])
                            inst = t.transpose(
                                ps_tr[0:rows, i8 * 128:(i8 + 1) * 128],
                                src, ident[:, :])
                        S["pe_tr"].inc(inst)

                    @blk.vector
                    def _(v, dt_t=dt_t, dt_o=dt_o, rows=rows):
                        _w(v, S["pe_tr"])
                        _w(v, S["pe_g"])
                        _w(v, S["pe_y"])
                        dst = bass.AP(dt_t, dt_o, [[dt_t.shape[1], rows], [1, 1024]])
                        inst = v.tensor_copy(dst, ps_tr[0:rows, 0:1024])
                        S["dv_evtr"].inc(inst)

                pass
            if is_own and OWNLVL >= 3:
                for wv_i in range(4):
                    @blk.tensor
                    def _(t, wv_i=wv_i):
                        _w(t, S["dv_evtr"])
                        _w(t, S["dv_f"])
                        for hh in range(4 * wv_i, 4 * wv_i + 4):
                            inst = t.matmul(
                                ps_t0[0:128, (hh % 4) * 128:(hh % 4) * 128 + 128],
                                kr_feat[0:16, hh * 128:(hh + 1) * 128],
                                qr_feat[0:16, hh * 128:(hh + 1) * 128],
                                start=True, stop=True)
                        S["pe_g"].inc(inst)

                    @blk.vector
                    def _(v, wv_i=wv_i):
                        _w(v, S["pe_g"])
                        _w(v, S["pe_y"])
                        fts = mb_s if wv_i % 2 == 0 else var_s
                        par = (wv_i % 2) * 512
                        v.tensor_scalar(fts[:, :], ps_t0[0:128, 0:512],
                                        0.03125, 0.25, ALU.mult, ALU.add)
                        v.scalar_tensor_tensor(ftmp[:, par:par + 512],
                                               ps_t0[0:128, 0:512], 1.0,
                                               fts[:, :], ALU.mult, ALU.mult)
                        trb = bass.AP(tril, 0, [[128, 128], [0, 4], [1, 128]])
                        i0 = bass.AP(ftmp, par, [[1024, 128], [128, 4], [1, 128]])
                        oo = bass.AP(pt_t, pt_o + wv_i * 512,
                                     [[4096, 128], [128, 4], [1, 128]])
                        inst = v.scalar_tensor_tensor(oo, i0, 1.0, trb,
                                                      ALU.add, ALU.mult)
                        S["dv_f"].inc(inst)

                    if OWNLVL < 4:
                        continue

                    @blk.tensor
                    def _(t, wv_i=wv_i):
                        _w(t, S["dv_f"])
                        _w(t, S["dv_cast"])
                        _w(t, S["dv_bld"])
                        _w(t, S["dv_evy"])
                        for hh in range(4 * wv_i, 4 * wv_i + 4):
                            fo = (hh % 4) * 128
                            t.matmul(ps_t1[0:65, fo:fo + 128],
                                     vp[:, hh * 65:hh * 65 + 65],
                                     bass.AP(pt_t, pt_o + hh * 128,
                                             [[4096, 128], [1, 128]]),
                                     start=True, stop=False)
                            t.matmul(ps_t1[0:64, fo:fo + 128],
                                     sbp[0:120, hh * 64:(hh + 1) * 64],
                                     bass.AP(qp_feat_t, qp_feat_o + hh * 128,
                                             [[4096, 120], [1, 128]]),
                                     start=False, stop=False)
                            t.matmul(ps_t1[64:65, fo:fo + 128],
                                     sbpd[0:120, hh:hh + 1],
                                     bass.AP(qp_feat_t, qp_feat_o + hh * 128,
                                             [[4096, 120], [1, 128]]),
                                     start=False, stop=False)
                            inst = t.matmul(
                                ps_t1[0:65, fo:fo + 128],
                                sbr[0:RW, hh * 65:hh * 65 + 65],
                                qr_feat[0:RW, hh * 128:(hh + 1) * 128],
                                start=False, stop=True)
                        S["pe_y"].inc(inst)

                    if OWNLVL < 5:
                        continue

                    @blk.vector
                    def _(v, wv_i=wv_i):
                        _w(v, S["pe_y"])
                        _w(v, S["pe_zb"])
                        v.tensor_scalar(ztmp[0:1, 0:512], ps_t1[64:65, 0:512],
                                        ATT_EPS, None, ALU.add)
                        v.reciprocal(ztmp[0:1, 512:1024], ztmp[0:1, 0:512])
                        inst = v.tensor_copy(zr[0:1, 0:512], ztmp[0:1, 512:1024])
                        S["dv_z"].inc(inst)

                    @blk.tensor
                    def _(t, wv_i=wv_i):
                        _w(t, S["dv_z"])
                        _w(t, S["dv_evy"])
                        _w(t, S["dv_wadd"])
                        inst = t.matmul(ps_t2[0:128, 0:512], ones_rw[0:1, 0:128],
                                        zr[0:1, 0:512], start=True, stop=True)
                        S["pe_zb"].inc(inst)

                    @blk.vector
                    def _(v, wv_i=wv_i):
                        _w(v, S["pe_zb"])
                        _w(v, S["pe_wout"])
                        v.tensor_copy(zb_sb[:, :], ps_t2[0:128, 0:512])
                        inst = None
                        for hh in range(4 * wv_i, 4 * wv_i + 4):
                            p, rowb = hh // 2, (hh % 2) * 64
                            fo = (hh % 4) * 128
                            inst = v.tensor_tensor(
                                yp[rowb:rowb + 64, p * 128:(p + 1) * 128],
                                ps_t1[0:64, fo:fo + 128],
                                zb_sb[rowb:rowb + 64, fo:fo + 128], ALU.mult)
                        S["dv_evy"].inc(inst)

                pass
            if is_own and OWNLVL >= 6:
                @blk.tensor
                def _(t):
                    _w(t, S["dv_evy"])
                    _w(t, S["dv_f"])
                    for m in range(4):
                        for k in range(8):
                            inst = t.matmul(
                                ps_t0[0:128, m * 128:(m + 1) * 128],
                                wo_sb[:, k * 1024 + m * 128:k * 1024 + m * 128 + 128],
                                yp[:, k * 128:(k + 1) * 128],
                                start=(k == 0), stop=(k == 7))
                    for m in range(4, 8):
                        for k in range(8):
                            inst = t.matmul(
                                ps_t2[0:128, (m - 4) * 128:(m - 3) * 128],
                                wo_sb[:, k * 1024 + m * 128:k * 1024 + m * 128 + 128],
                                yp[:, k * 128:(k + 1) * 128],
                                start=(k == 0), stop=(k == 7))
                    S["pe_wout"].inc(inst)

                @blk.vector
                def _(v):
                    _w(v, S["pe_wout"])
                    _w(v, S["sx"], 48)
                    oc = tokabs - PRE
                    o1 = bass.AP(x_res, oc, [[DK * T, 128], [T, 4], [1, 128]])
                    v.tensor_tensor(o1, o1,
                                    bass.AP(ps_t0, 0, [[512, 128], [128, 4], [1, 128]]),
                                    ALU.add)
                    o2 = bass.AP(x_res, 4 * T + oc, [[DK * T, 128], [T, 4], [1, 128]])
                    inst = v.tensor_tensor(
                        o2, o2,
                        bass.AP(ps_t2, 0, [[512, 128], [128, 4], [1, 128]]),
                        ALU.add)
                    S["dv_wadd"].inc(inst)

            @blk.tensor
            def _(t):
                _w(t, S["dv_bld"])
                _w(t, S["dv_init"])
                if is_own:
                    _w(t, S["dv_cast"])
                inst = None
                for hh in range(H):
                    bank, off = pairs_v_rg(hh)
                    t.matmul(bank[0:NPAIR, off:off + 64],
                             bass.AP(kp_tok_t, kp_tok_o + hh * NPAIR,
                                     [[4096, 128], [1, NPAIR]]),
                             vp[:, hh * 65:hh * 65 + 64],
                             start=False, stop=True)
                    t.matmul(ps_s[2][0:NPAIR, 455 + hh:456 + hh],
                             bass.AP(kp_tok_t, kp_tok_o + hh * NPAIR,
                                     [[4096, 128], [1, NPAIR]]),
                             mcb, start=False, stop=True)
                for p in range(8):
                    bank, off = rest_rg(p)
                    for half in range(2):
                        hh = 2 * p + half
                        base = half * 64
                        inst = t.matmul(
                            bank[base:base + RW, off:off + 65],
                            kr_tok[:, p * 128 + base:p * 128 + base + RW],
                            vp[:, hh * 65:hh * 65 + 65],
                            start=False, stop=True)
                S["pe_state"].inc(inst)

        OWNLVL = int(os.environ.get("KMK_OWNLVL", "6"))
        PHASE = os.environ.get("KMK_PHASE", "full")
        NGRP = int(os.environ.get("KMK_GROUPS", "17"))
        DOCONV = os.environ.get("KMK_CONV", "1") == "1" and PHASE != "attn_noconv"
        groups = [("enc", 0)]
        groups += [("pre", 512 * g) for g in range(12)]
        groups += [("own", PRE + 512 * g) for g in range(4)]
        groups = groups[:NGRP]
        first = True
        for gi, (kind, tok0) in enumerate(groups):
            ln_group(tok0, from_res=False, first=first, gidx=gi)
            first = False
            for cidx in range(4):
                attn_chunk(kind, tok0 + cidx * 128, cidx)
            if kind != "enc" and DOCONV:
                emit_conv(4)
        if DOCONV:
            emit_conv(NCONV)
        if PHASE in ("attn", "attn_noconv"):
            @blk.sync
            def _(sy):
                _w(sy, S["dv_wadd"])
                if S["ssc"].n > 0:
                    _w(sy, S["ssc"])
                src = bass.AP(x_res, 0, [[DK * T, 128], [T, DK], [1, T]])
                dst = bass.AP(out_d, 0, [[T, 128], [128 * T, DK], [1, T]])
                sy.dma_start(dst, src).then_inc(S["sst"].h, 16)
                S["sst"].n += 16
                sy.wait_ge(S["sst"].h, S["sst"].n)

            @blk.tensor
            def _(t):
                _w(t, S["pe_state"])

            @blk.vector
            def _(v):
                _w(v, S["dv_bld"])
            return nc

        @blk.gpsimd
        def _(g):
            g.wait_ge(S["pe_state"].h, 68)
            g.wait_ge(S["pe_wout"].h, 16)
            g.dma_start(yp[0:17, 0:D], b2_d.ap()).then_inc(S["swl"].h, 16)
            S["swl"].n += 16
            g.memset(qr_feat[0:32, 0:2048], 1.0)
            g.affine_select(
                bass.AP(qr_feat, 0, [[H * 128, 32], [128, 16], [1, 128]]),
                bass.AP(qr_feat, 0, [[H * 128, 32], [128, 16], [1, 128]]),
                [[128, 16], [0, 128]], ALU.is_equal, 0.0,
                base=0, channel_multiplier=-128)
            g.memset(sbr[0:32, 0:256], 1.0)
            inst = g.affine_select(
                bass.AP(sbr, 0, [[H * 65, 32], [128, 2], [1, 128]]),
                bass.AP(sbr, 0, [[H * 65, 32], [128, 2], [1, 128]]),
                [[128, 2], [0, 128]], ALU.is_equal, 0.0,
                base=16 * 128, channel_multiplier=-128)
            S["gp_sel"].inc(inst)

        def gate_logits(ch):
            par = (ch % 2) * 16

            @blk.tensor
            def _(t):
                _w(t, S["dv_xh"])
                _w(t, S["dv_gate"])
                for dk in range(DK):
                    inst = t.matmul(ps_t0[0:128, par:par + 16],
                                    xhat2_ap(dk, ch * 128, 128),
                                    gw_sb[:, dk * 16:(dk + 1) * 16],
                                    start=(dk == 0), stop=(dk == DK - 1))
                S["pe_gate"].inc(inst)

            @blk.vector
            def _(v):
                _w(v, S["pe_gate"])
                inst = v.tensor_copy(glog[:, ch * 16:(ch + 1) * 16],
                                     ps_t0[0:128, par:par + 16])
                S["dv_gate"].inc(inst)

        for g4 in range(4):
            ln_group(g4 * 512, from_res=True)
            for ci in range(4):
                gate_logits(g4 * 4 + ci)

        @blk.scalar
        def _(s_):
            _w(s_, S["dv_gate"])
            inst = s_.activation(gt1[:, :], glog[:, :], AF.Sigmoid)
            S["ac_silu"].inc(inst)

        @blk.vector
        def _(v):
            _w(v, S["ac_silu"])
            ebb = bass.AP(eb16, 0, [[16, 128], [0, 16], [1, 16]])
            g3 = lambda tt: bass.AP(tt, 0, [[256, 128], [16, 16], [1, 16]])
            v.tensor_tensor(g3(gt0), g3(glog), ebb, ALU.add)
            v.drain()
            v.reduce_max(bass.AP(gm, 0, [[64, 128], [1, 16], [0, 1]]),
                         g3(gt0), axis=AXX)
            v.drain()
            m1b = bass.AP(gm, 0, [[64, 128], [1, 16], [0, 16]])
            v.tensor_tensor(g3(gt2), g3(gt0), m1b, ALU.is_ge)
            v.drain()
            v.scalar_tensor_tensor(gt2[:, :], gt2[:, :], NEG_BIG, gt0[:, :],
                                   ALU.mult, ALU.add)
            v.drain()
            v.reduce_max(bass.AP(gm, 16, [[64, 128], [1, 16], [0, 1]]),
                         g3(gt2), axis=AXX)
            v.drain()
            m2b = bass.AP(gm, 16, [[64, 128], [1, 16], [0, 16]])
            v.tensor_tensor(g3(gt2), g3(gt0), m2b, ALU.is_ge)
            v.drain()
            v.tensor_tensor(gt1[:, :], gt1[:, :], gt2[:, :], ALU.mult)
            v.drain()
            v.reduce_sum(bass.AP(gm, 32, [[64, 128], [1, 16], [0, 1]]),
                         g3(gt1), axis=AXX)
            v.drain()
            v.tensor_scalar(gm[:, 48:64], gm[:, 32:48], GATE_EPS, None, ALU.add)
            v.drain()
            v.reciprocal(gm[:, 32:48], gm[:, 48:64])
            v.drain()
            rb2 = bass.AP(gm, 32, [[64, 128], [1, 16], [0, 16]])
            inst = v.tensor_tensor(
                bass.AP(gchb_t, gchb_o, [[gchb_t.shape[1], 128], [16, 16], [1, 16]]),
                g3(gt1), rb2, ALU.mult)
            v.drain()
            S["dv_gate"].inc(inst)

        for ch in range(16):
            @blk.tensor
            def _(t, ch=ch):
                _w(t, S["dv_gate"])
                _w(t, S["dv_evtr"])
                inst = t.transpose(
                    ps_tr[0:16, (ch % 8) * 128:(ch % 8) * 128 + 128],
                    bass.AP(gchb_t, gchb_o + ch * 16,
                            [[gchb_t.shape[1], 128], [1, 16]]),
                    ident[:, :])
                S["pe_tr"].inc(inst)

            @blk.vector
            def _(v, ch=ch):
                _w(v, S["pe_tr"])
                inst = v.tensor_copy(gwt[0:16, ch * 128:(ch + 1) * 128],
                                     ps_tr[0:16, (ch % 8) * 128:(ch % 8) * 128 + 128])
                S["dv_evtr"].inc(inst)

        if PHASE == "ln2":
            @blk.vector
            def _(v):
                _w(v, S["dv_gate"])
                _w(v, S["dv_xh"])
                _w(v, S["gp_sel"])
                _w(v, S["dv_evtr"])
                inst = None
                if os.environ.get("KMK_DUMP", "xhat2") == "gwt":
                    inst = v.tensor_copy(x_res[0:32, 0:T], gwt[0:32, 0:T])
                    inst = v.tensor_copy(x_res[0:32, 3 * T:3 * T + 2048],
                                         qr_feat[0:32, 0:2048])
                    inst = v.tensor_copy(
                        x_res[0:128, 2 * T:2 * T + 256],
                        bass.AP(gchb_t, gchb_o, [[gchb_t.shape[1], 128], [1, 256]]))
                else:
                    for dk in range(DK):
                        inst = v.tensor_copy(x_res[:, dk * T:(dk + 1) * T],
                                             xhat2_ap(dk, 0, T))
                S["dv_oadd"].inc(inst)

            @blk.sync
            def _(sy):
                _w(sy, S["dv_oadd"])
                _w(sy, S["ssc"])
                src = bass.AP(x_res, 0, [[DK * T, 128], [T, DK], [1, T]])
                dst = bass.AP(out_d, 0, [[T, 128], [128 * T, DK], [1, T]])
                sy.dma_start(dst, src).then_inc(S["sst"].h, 16)
                S["sst"].n += 16
                sy.wait_ge(S["sst"].h, S["sst"].n)
            return nc

        for fblk in range(8):
            @blk.vector
            def _(v):
                _w(v, S["pe_w2"])
                _w(v, S["pe_state"])
                _w(v, S["pe_y"])
                inst = None
                for i in range(4):
                    inst = v.memset(ps_s[i][:, :], 0.0)
                S["dv_ms"].inc(inst)

            @blk.tensor
            def _(t):
                _w(t, S["dv_ms"])
                _w(t, S["dv_gate"])
                _w(t, S["swl"])
                inst = None
                for dk in range(DK):
                    bank, colo = ps_s[dk // 2], (dk % 2) * 256
                    inst = t.matmul(bank[0:128, colo:colo + 256],
                                     b2_sb[0:17, dk * 128:(dk + 1) * 128],
                                     gwt[0:17, fblk * 256:(fblk + 1) * 256],
                                     start=False, stop=True)
                S["pe_misc"].inc(inst)

            for u in range(E_UNITS):
                ui = fblk * E_UNITS + u
                par = ui % 2

                @blk.sync
                def _(sy):
                    if ui >= 2:
                        sy.wait_ge(S["pe_w2"].h, ui - 1)
                    else:
                        sy.wait_ge(S["ssc"].h, NCONV * 16)
                        _w(sy, S["dv_gate"])
                        _w(sy, S["dv_xb"])
                    if fblk == 0:
                        sy.wait_ge(S["ssc"].h, min(NCONV, 3 * (u + 1)) * 16)
                    for j, buf in enumerate([wu1[par], wu3[par], wu2[par]]):
                        src = bass.AP(wsc, (u * 3 + j) * 128 * 4096,
                                      [[4096, 128], [1, 4096]])
                        sy.dma_start(buf[:, :], src).then_inc(S["sfu"].h, 16)
                        S["sfu"].n += 16

                @blk.tensor
                def _(t):
                    _w(t, S["sfu"])
                    _w(t, S["dv_gb"])
                    _w(t, S["dv_xh"])
                    _w(t, S["dv_ms"])
                    _w(t, S["gp_sel"])
                    inst = t.matmul(ps_t0[0:128, 0:256], sel_ap(u),
                                    gwt[0:32, fblk * 256:(fblk + 1) * 256],
                                    start=True, stop=True)
                    S["pe_gb"].inc(inst)

                @blk.vector
                def _(v):
                    _w(v, S["pe_gb"])
                    _w(v, S["dv_hev"])
                    inst = v.tensor_copy(bass.AP(gb_t, gb_o, [[256, 128], [1, 256]]),
                                         ps_t0[0:128, 0:256])
                    S["dv_gb"].inc(inst)

                rot = [ps_t1, ps_t2, ps_t1, ps_t2]

                def emit_ab(ci):
                    @blk.tensor
                    def _(t):
                        _w(t, S["dv_hev"], max(0, S["dv_hev"].n - 1))
                        bank = rot[ci]
                        for dk in range(DK):
                            t.matmul(
                                bank[0:128, 0:256],
                                wu1[par][:, dk * 512 + ci * 128:dk * 512 + ci * 128 + 128],
                                xhat2_ap(dk, fblk * 256, 256),
                                start=(dk == 0), stop=(dk == DK - 1))
                        for dk in range(DK):
                            inst = t.matmul(
                                bank[0:128, 256:512],
                                wu3[par][:, dk * 512 + ci * 128:dk * 512 + ci * 128 + 128],
                                xhat2_ap(dk, fblk * 256, 256),
                                start=(dk == 0), stop=(dk == DK - 1))
                        S["pe_ab"].inc(inst)

                    @blk.vector
                    def _(v):
                        v.wait_ge(S["ac_silu"].h, ac_base + ci + 1)
                        _w(v, S["pe_w2"])
                        v.scalar_tensor_tensor(
                            tb_t[:, ci * 256:(ci + 1) * 256],
                            rot[ci][0:128, 256:512],
                            b3c[:, u * 4 + ci:u * 4 + ci + 1],
                            bass.AP(gb_t, gb_o, [[256, 128], [1, 256]]),
                            ALU.add, ALU.mult)
                        inst = v.tensor_tensor(
                            h_t[:, ci * 256:(ci + 1) * 256],
                            tb_t[:, ci * 256:(ci + 1) * 256],
                            sa_t[:, ci * 256:(ci + 1) * 256], ALU.mult)
                        S["dv_hev"].inc(inst)

                def emit_w2(ci):
                    @blk.tensor
                    def _(t):
                        _w(t, S["dv_hev"],
                           S["dv_hev"].n - (1 if ci < 3 else 0))
                        inst = None
                        for dk in range(DK):
                            bank, colo = ps_s[dk // 2], (dk % 2) * 256
                            inst = t.matmul(
                                bank[0:128, colo:colo + 256],
                                wu2[par][:, ci * 1024 + dk * 128:ci * 1024 + dk * 128 + 128],
                                h_t[:, ci * 256:(ci + 1) * 256],
                                start=False, stop=True)
                        if ci == 3:
                            S["pe_w2"].inc(inst)
                        else:
                            S["pe_misc"].inc(inst)

                ab_base = S["pe_ab"].n
                hev_base = S["dv_hev"].n
                ac_base = S["ac_silu"].n

                @blk.scalar
                def _(s_):
                    for ci in range(4):
                        s_.wait_ge(S["pe_ab"].h, ab_base + ci + 1)
                        if hev_base + ci - 1 > 0:
                            s_.wait_ge(S["dv_hev"].h, hev_base + ci - 1)
                        inst = s_.activation(
                            sa_t[:, ci * 256:(ci + 1) * 256],
                            rot[ci][0:128, 0:256], AF.Silu,
                            bias=b1c[:, u * 4 + ci:u * 4 + ci + 1], scale=1.0)
                        S["ac_silu"].inc(inst)

                emit_ab(0)
                emit_ab(1)
                emit_w2(0)
                emit_ab(2)
                emit_w2(1)
                emit_ab(3)
                emit_w2(2)
                emit_w2(3)

            @blk.vector
            def _(v):
                _w(v, S["pe_w2"])
                _w(v, S["pe_misc"])
                inst = None
                for i in range(4):
                    oo = bass.AP(x_res, 2 * i * T + fblk * 256,
                                 [[DK * T, 128], [T, 2], [1, 256]])
                    inst = v.tensor_tensor(
                        oo, oo,
                        bass.AP(ps_s[i], 0, [[512, 128], [256, 2], [1, 256]]),
                        ALU.add)
                S["dv_oadd"].inc(inst)

        @blk.sync
        def _(sy):
            _w(sy, S["dv_oadd"])
            src = bass.AP(x_res, 0, [[DK * T, 128], [T, DK], [1, T]])
            dst = bass.AP(out_d, 0, [[T, 128], [128 * T, DK], [1, T]])
            sy.dma_start(dst, src).then_inc(S["sst"].h, 16)
            S["sst"].n += 16
            sy.wait_ge(S["sst"].h, S["sst"].n)

    return nc


_NC_CACHE = {}


def _get_nc():
    key = (os.environ.get("KMK_PHASE", "full"), os.environ.get("KMK_GROUPS", "17"),
           os.environ.get("KMK_CONV", "1"))
    if key not in _NC_CACHE:
        _NC_CACHE[key] = build_program()
    return _NC_CACHE[key]


def kernel(**inp):
    f32 = np.float32
    x = np.asarray(inp["L0_attn_x"], dtype=f32)
    wq = np.asarray(inp["L0_attn_Wq"], f32)
    wk = np.asarray(inp["L0_attn_Wk"], f32)
    wv = np.asarray(inp["L0_attn_Wv"], f32)
    wke = np.asarray(inp["L0_attn_Wk_enc"], f32)
    wve = np.asarray(inp["L0_attn_Wv_enc"], f32)
    wo = np.asarray(inp["L0_attn_Wout"], f32)
    gw = np.asarray(inp["L0_ffn_gate_w"], f32)
    eb = np.asarray(inp["L0_ffn_expert_bias"], f32)
    w1 = np.asarray(inp["L0_ffn_w1"], f32)
    b1 = np.asarray(inp["L0_ffn_b1"], f32)
    w2 = np.asarray(inp["L0_ffn_w2"], f32)
    b2 = np.asarray(inp["L0_ffn_b2"], f32)
    w3 = np.asarray(inp["L0_ffn_w3"], f32)
    b3 = np.asarray(inp["L0_ffn_b3"], f32)
    sw1 = np.asarray(inp["L0_ffn_sw1"], f32)
    sb1 = np.asarray(inp["L0_ffn_sb1"], f32)
    sw2 = np.asarray(inp["L0_ffn_sw2"], f32)
    sb2 = np.asarray(inp["L0_ffn_sb2"], f32)
    sw3 = np.asarray(inp["L0_ffn_sw3"], f32)
    sb3 = np.asarray(inp["L0_ffn_sb3"], f32)

    C = np.ascontiguousarray

    def tile_t(wt):
        ncols = wt.shape[1]
        return C(wt.reshape(DK, 128, ncols).transpose(1, 0, 2)
                 .reshape(128, DK * ncols))

    def tile_k(wt, kch):
        ncols = wt.shape[1]
        return wt.reshape(kch, 128, ncols).transpose(1, 0, 2).reshape(
            128, kch * ncols)

    wsrc = np.empty((54, 128, 4096), f32)
    i = 0
    for e in range(16):
        wsrc[i] = tile_k(w1[e].T, 8); i += 1
        wsrc[i] = tile_k(w3[e].T, 8); i += 1
        wsrc[i] = tile_k(w2[e].T, 4); i += 1
    for u_ in range(2):
        wsrc[i] = tile_k(sw1.T[:, u_ * 512:(u_ + 1) * 512], 8); i += 1
        wsrc[i] = tile_k(sw3.T[:, u_ * 512:(u_ + 1) * 512], 8); i += 1
        wsrc[i] = tile_k(sw2.T[u_ * 512:(u_ + 1) * 512], 4); i += 1
    shared = {
        "wqT": tile_t(C(wq.T)), "wkT": tile_t(C(wk.T)), "wvT": tile_t(C(wv.T)),
        "wkeT": tile_t(C(wke.T)), "wveT": tile_t(C(wve.T)),
        "woT": tile_t(C(wo.T)),
        "gwTm": tile_t(C(gw.T)), "ebr": C(eb[None, :]),
        "wsrc": C(wsrc.reshape(54 * 128, 4096)),
        "b2m": C(np.concatenate([b2, sb2[None, :]], axis=0)),
    }
    b1all = np.concatenate([b1, sb1.reshape(2, 512)], axis=0)
    b3all = np.concatenate([b3, sb3.reshape(2, 512)], axis=0)
    shared["b1c"] = C(b1all.reshape(E_UNITS, 4, 128).transpose(2, 0, 1)
                      .reshape(128, E_UNITS * 4))
    shared["b3c"] = C(b3all.reshape(E_UNITS, 4, 128).transpose(2, 0, 1)
                      .reshape(128, E_UNITS * 4))
    sc = np.zeros((128, 1), f32)
    sc[0:16] = 0.25
    sc[32:48] = 1.0 / 32.0
    sc[48] = 1.0
    sc[64:80] = 0.25
    sc[96:112] = 1.0 / 32.0
    sc[112] = 1.0
    shared["scol"] = sc

    g_tok0 = [0] + [512 * g for g in range(12)] + [PRE + 512 * g for g in range(4)]
    in_maps = []
    for core in range(8):
        b, c = core // 4, core % 4
        xb = x[b].T
        xTc = np.concatenate([xb[:, :PRE], xb[:, c * T:(c + 1) * T]], axis=1)
        xtt = tile_k(xTc, DK).reshape(128, DK, PRE + T)
        xgt = np.empty((128, 17, 4096), f32)
        for gi, t0 in enumerate(g_tok0):
            xgt[:, gi] = xtt[:, :, t0:t0 + 512].reshape(128, DK * 512)
        xot = C(xtt[:, :, PRE:].reshape(128, DK * T))
        m = np.zeros((128, NCH), f32)
        lim = c * T
        for j in range(NCH):
            base = j * 128
            if base + 128 <= lim:
                m[:, j] = 1.0
            elif base < lim:
                m[0:lim - base, j] = 1.0
        d = dict(shared)
        d["xgt"] = C(xgt.reshape(128, 17 * 4096))
        d["xot"] = xot
        d["mask"] = m
        in_maps.append(d)

    nc = _get_nc()
    res = run_bass_kernel_spmd(nc, in_maps, core_ids=list(range(8)))
    y = np.empty((B_, N_, D), f32)
    for core in range(8):
        b, c = core // 4, core % 4
        y[b, c * T:(c + 1) * T, :] = res.results[core]["out"].T
    return y


# version 15
# speedup vs baseline: 1.1542x; 1.0081x over previous
import contextlib
import os
import numpy as np
import ml_dtypes
import concourse.bass as bass
import concourse.mybir as mybir
from concourse.bass_utils import run_bass_kernel_spmd

F32 = mybir.dt.float32
BF16 = mybir.dt.bfloat16
AF = mybir.ActivationFunctionType
ALU = mybir.AluOpType
AXX = mybir.AxisListType.X

B_, N_, D = 2, 8192, 1024
H, FD, HD = 16, 16, 64
T = 2048
PRE = 6144
NCH = PRE // 128
DK = D // 128
NPAIR = 120
RW = 49
E_UNITS = 18
DFF = 512
LN_EPS = 1e-5
ATT_EPS = 1e-12
GATE_EPS = 1e-10
NEG_BIG = -1e30


class Sem:
    def __init__(self, stack, nc, name):
        self.h = stack.enter_context(nc.semaphore(name))
        self.n = 0

    def inc(self, inst, k=1):
        assert inst is not None
        inst.then_inc(self.h, k)
        self.n += k


def build_program():
    nc = bass.Bass("TRN2", target_bir_lowering=False)

    def _w(eng, sem, n=None):
        v = sem.n if n is None else n
        assert v <= sem.n, f"future wait {v} > {sem.n}"
        if v > 0:
            eng.wait_ge(sem.h, v)

    xg_d = nc.dram_tensor("xgt", [128, 17 * 4096], F32, kind="ExternalInput")
    xo_d = nc.dram_tensor("xot", [128, DK * T], F32, kind="ExternalInput")
    mask_d = nc.dram_tensor("mask", [128, NCH], F32, kind="ExternalInput")
    scol_d = nc.dram_tensor("scol", [128, 1], F32, kind="ExternalInput")
    wq_d = nc.dram_tensor("wqT", [128, DK * 256], F32, kind="ExternalInput")
    wk_d = nc.dram_tensor("wkT", [128, DK * 256], F32, kind="ExternalInput")
    wv_d = nc.dram_tensor("wvT", [128, DK * 1024], F32, kind="ExternalInput")
    wke_d = nc.dram_tensor("wkeT", [128, DK * 256], F32, kind="ExternalInput")
    wve_d = nc.dram_tensor("wveT", [128, DK * 1024], F32, kind="ExternalInput")
    wo_d = nc.dram_tensor("woT", [128, DK * 1024], F32, kind="ExternalInput")
    gw_d = nc.dram_tensor("gwTm", [128, DK * 16], F32, kind="ExternalInput")
    ebr_d = nc.dram_tensor("ebr", [1, 16], F32, kind="ExternalInput")
    wsrc_d = nc.dram_tensor("wsrc", [54 * 128, 4096], F32, kind="ExternalInput")
    b1c_d = nc.dram_tensor("b1c", [128, 72], F32, kind="ExternalInput")
    b3c_d = nc.dram_tensor("b3c", [128, 72], F32, kind="ExternalInput")
    b2_d = nc.dram_tensor("b2m", [17, D], F32, kind="ExternalInput")
    out_d = nc.dram_tensor("out", [D, T], F32, kind="ExternalOutput")
    wsc = nc.dram_tensor("wsc", [E_UNITS * 3 * 128, 4096], BF16, kind="Internal")

    st = contextlib.ExitStack()
    with st:
        def sb(name, shape, dt):
            return st.enter_context(nc.sbuf_tensor(name, shape, dt))

        x_res = sb("x_res", [128, DK * T], F32)
        ones_rw = sb("ones_rw", [1, 256], BF16)
        onesmat = sb("onesmat", [128, 128], BF16)
        ident = sb("ident", [128, 128], BF16)
        tril = sb("tril", [128, 128], BF16)
        onescol = sb("onescol", [128, 1], F32)
        onescol_bf = sb("onescol_bf", [128, 1], BF16)
        eb16 = sb("eb16", [128, 16], F32)
        scol = sb("scol_s", [128, 1], F32)
        mcol = sb("mcol", [128, NCH], F32)
        mcol_bf = sb("mcol_bf", [128, NCH], BF16)
        ebrow = sb("ebrow", [1, 16], BF16)
        cvals = sb("cvals", [128, 2], F32)
        wq_sb = sb("wq_sb", [128, DK * 256], BF16)
        wk_sb = sb("wk_sb", [128, DK * 256], BF16)
        wv_sb = sb("wv_sb", [128, DK * 1024], BF16)
        wo_sb = sb("wo_sb", [128, DK * 1024], BF16)
        gw_sb = sb("gw_sb", [128, DK * 16], BF16)
        xhx = sb("xhx", [128, DK * 512], BF16)
        mb_s = sb("mb_s", [128, 512], F32)
        var_s = sb("var_s", [128, 512], F32)
        wu1 = [sb(f"wu1_{i}", [128, 4096], BF16) for i in range(2)]
        wu3 = [sb(f"wu3_{i}", [128, 4096], BF16) for i in range(2)]
        wu2 = [sb(f"wu2_{i}", [128, 4096], BF16) for i in range(2)]
        ktok = sb("ktok", [128, 256], BF16)
        qq = sb("qq", [128, H * 32], BF16)
        kk = sb("kk", [128, H * 32], BF16)
        qr_tok = sb("qr_tok", [128, 8 * 128], BF16)
        kr_tok = sb("kr_tok", [128, 8 * 128], BF16)
        vp = sb("vp", [128, H * 65], BF16)
        qr_feat = sb("qr_feat", [128, H * 128], BF16)
        kr_feat = sb("kr_feat", [128, H * 128], BF16)
        sbp = sb("sbp", [128, H * 64], BF16)
        sbpd = sb("sbpd", [128, 16], BF16)
        sbr = sb("sbr", [128, H * 65], BF16)
        yp = sb("yp", [128, 8 * 128], BF16)
        ztmp = sb("ztmp", [1, 1024], F32)
        zr = sb("zr", [1, 512], BF16)
        ftmp = sb("ftmp", [128, 2 * 512], BF16)
        zb_sb = sb("zb_sb", [128, 512], BF16)
        xb_g = wu1[1]
        qp_tok_t, qp_tok_o = wu3[1], 0
        kp_tok_t, kp_tok_o = wu3[1], 1920
        qtok_t, qtok_o = wu3[1], 3840
        pt_t, pt_o = wu2[1], 0
        qp_feat_t, qp_feat_o = wu2[1], 2048
        gwt = sb("gwt", [32, T], BF16)
        glog = sb("glog", [128, 256], F32)
        gt0 = sb("gt0", [128, 256], F32)
        gt1 = sb("gt1", [128, 256], F32)
        gt2 = sb("gt2", [128, 256], F32)
        gm = sb("gm", [128, 64], F32)
        b1c = sb("b1c_s", [128, 72], F32)
        b3c = sb("b3c_s", [128, 72], F32)

        h_t = kr_feat
        sa_t = qr_tok
        tb_t = kr_tok
        gb_t, gb_o = ktok, 0
        b2_sb = yp
        sb2_sb_row = 32
        gchb_t, gchb_o = qq, 0

        def sel_ap(u):
            if u < 16:
                return qr_feat[0:32, u * 128:(u + 1) * 128]
            return sbr[0:32, (u - 16) * 128:(u - 15) * 128]

        ps_s = [st.enter_context(nc.psum_tensor(f"ps_s{i}", [128, 512], F32))
                for i in range(4)]
        ps_t0 = st.enter_context(nc.psum_tensor("ps_t0", [128, 512], F32))
        ps_t1 = st.enter_context(nc.psum_tensor("ps_t1", [128, 512], F32))
        ps_t2 = st.enter_context(nc.psum_tensor("ps_t2", [128, 512], F32))
        ps_tr = st.enter_context(nc.psum_tensor("ps_tr", [128, 1024], BF16))

        def pairs_v_rg(hh):
            return ps_s[hh // 8], (hh % 8) * 64

        def rest_rg(p):
            return (ps_s[2], 65 * p) if p < 7 else (ps_s[3], 0)

        SEMS = ["sx", "swl", "scv", "ssc", "sfu", "sst",
                "pe_stat", "pe_proj", "pe_v", "pe_tr", "pe_g", "pe_y",
                "pe_zb", "pe_wout", "pe_state", "pe_gate", "pe_gb",
                "pe_ab", "pe_w2", "pe_misc",
                "dv_xb", "dv_row", "dv_xh", "dv_eqk", "dv_bld",
                "dv_evtr", "dv_cast", "dv_f", "dv_z", "dv_evy", "dv_wadd",
                "dv_gate", "dv_hev", "dv_oadd", "dv_init", "dv_gb", "dv_ms",
                "gp_sel", "ac_row", "ac_silu", "ac_f", "ac_z"]
        S = {nm: Sem(st, nc, nm) for nm in SEMS}

        blk = st.enter_context(nc.Block())

        @blk.sync
        def _(sy):
            sy.dma_start(mcol[:, :], mask_d.ap()).then_inc(S["sx"].h, 16)
            sy.dma_start(scol[:, :], scol_d.ap()).then_inc(S["sx"].h, 16)
            S["sx"].n += 32
            sy.dma_start(x_res[:, :], xo_d.ap()).then_inc(S["sx"].h, 16)
            S["sx"].n += 16

        @blk.gpsimd
        def _(g):
            g.memset(ones_rw[:, :], 1.0)
            g.memset(onesmat[:, :], 1.0)
            g.memset(onescol[:, :], 1.0)
            g.memset(onescol_bf[:, :], 1.0)
            g.memset(ident[:, :], 1.0)
            g.affine_select(ident[:, :], ident[:, :], [[1, 128]],
                            ALU.is_equal, 0.0, base=0, channel_multiplier=-1)
            g.memset(tril[:, :], 1.0)
            g.affine_select(tril[:, :], tril[:, :], [[1, 128]],
                            ALU.is_ge, 0.0, base=0, channel_multiplier=-1)
            g.memset(qr_tok[:, :], 0.0)
            g.memset(kr_tok[:, :], 0.0)
            g.memset(bass.AP(qr_tok, 48, [[8 * 128, 128], [64, 16], [1, 1]]), 1.0)
            g.memset(cvals[:, 0:1], LN_EPS)
            g.memset(cvals[:, 1:2], ATT_EPS)
            g.memset(gwt[:, :], 1.0)
            g.affine_select(gwt[0:32, :], gwt[0:32, :], [[0, T]],
                            ALU.is_ge, 0.0, base=-16, channel_multiplier=1)
            g.affine_select(gwt[0:32, :], gwt[0:32, :], [[0, T]],
                            ALU.is_ge, 0.0, base=17, channel_multiplier=-1)
            inst = g.memset(vp[:, :], 0.0)
            S["dv_init"].inc(inst)
            g.dma_start(mcol_bf[:, :], mask_d.ap()).then_inc(S["swl"].h, 16)
            g.dma_start(ebrow[:, :], ebr_d.ap()).then_inc(S["swl"].h, 16)
            S["swl"].n += 32

            def wload(dst_sb, src_d, ncols):
                g.dma_start(dst_sb[:, :], src_d.ap()).then_inc(S["swl"].h, 16)
                S["swl"].n += 16

            wload(wq_sb, wq_d, 256)
            wload(wk_sb, wk_d, 256)
            wload(wv_sb, wv_d, 1024)
            wload(wo_sb, wo_d, 1024)
            wload(gw_sb, gw_d, 16)
            g.dma_start(b1c[:, :], b1c_d.ap()).then_inc(S["swl"].h, 16)
            g.dma_start(b3c[:, :], b3c_d.ap()).then_inc(S["swl"].h, 16)
            S["swl"].n += 32
            g.dma_start(wu2[0][:, 0:DK * 256], wke_d.ap()).then_inc(S["swl"].h, 16)
            src = bass.AP(wve_d, 0, [[DK * 1024, 128], [1, 4096]])
            g.dma_start(wu1[0][:, :], src).then_inc(S["swl"].h, 16)
            src = bass.AP(wve_d, 4096, [[DK * 1024, 128], [1, 4096]])
            g.dma_start(wu3[0][:, :], src).then_inc(S["swl"].h, 16)
            S["swl"].n += 48

        def wke_ap(dk):
            return wu2[0][:, dk * 256:(dk + 1) * 256]

        def wve_ap(dk, half):
            tt = wu1[0] if dk < 4 else wu3[0]
            return tt[:, (dk % 4) * 1024 + half * 512:(dk % 4) * 1024 + half * 512 + 512]

        @blk.vector
        def _(v):
            for i in range(4):
                v.memset(ps_s[i][:, :], 0.0)
            inst = v.memset(ps_t2[:, :], 0.0)
            S["dv_init"].inc(inst)

        @blk.tensor
        def _(t):
            _w(t, S["swl"])
            _w(t, S["dv_init"])
            inst = t.matmul(ps_t0[0:128, 0:16], ones_rw[0:1, 0:128],
                            ebrow[0:1, 0:16], start=True, stop=True)
            S["pe_misc"].inc(inst)

        @blk.vector
        def _(v):
            _w(v, S["pe_misc"])
            inst = v.tensor_copy(eb16[:, :], ps_t0[0:128, 0:16])
            S["dv_init"].inc(inst)

        NCONV = 54
        conv_stage = [wu1[0], wu3[0], wu2[0]]

        def _conv_src(i):
            return bass.AP(wsrc_d, i * 128 * 4096, [[4096, 128], [1, 4096]])

        conv_emitted = [0]

        def emit_conv(k):
            n0 = conv_emitted[0]
            n1 = min(NCONV, n0 + k)
            if n1 == n0:
                return
            conv_emitted[0] = n1

            @blk.gpsimd
            def _(g):
                for i in range(n0, n1):
                    buf = conv_stage[i % 3]
                    if i == 0:
                        g.wait_ge(S["pe_state"].h, 4)
                    if i >= 3:
                        g.wait_ge(S["ssc"].h, (i - 3 + 1) * 16)
                    g.dma_start(buf[:, :], _conv_src(i)).then_inc(
                        S["scv"].h, 16)
                    S["scv"].n += 16

            @blk.sync
            def _(sy):
                for i in range(n0, n1):
                    buf = conv_stage[i % 3]
                    sy.wait_ge(S["scv"].h, (i + 1) * 16)
                    dst = bass.AP(wsc, i * 128 * 4096, [[4096, 128], [1, 4096]])
                    sy.dma_start(dst, buf[:, :]).then_inc(S["ssc"].h, 16)
                    S["ssc"].n += 16

        def ln_group(tok0, from_res, first=False, gidx=0):
            @blk.gpsimd
            def _(g):
                if not from_res:
                    if S["pe_stat"].n > 0:
                        g.wait_ge(S["pe_stat"].h, S["pe_stat"].n)
                    if S["dv_xh"].n > 0:
                        g.wait_ge(S["dv_xh"].h, S["dv_xh"].n)
                    src = bass.AP(xg_d, gidx * 4096,
                                  [[17 * 4096, 128], [1, 4096]])
                    g.dma_start(xb_g[:, :], src).then_inc(S["sx"].h, 16)
                    S["sx"].n += 16

            @blk.vector
            def _(v):
                if from_res:
                    _w(v, S["pe_stat"])
                    src = bass.AP(x_res, tok0, [[DK * T, 128], [T, DK], [1, 512]])
                    v.tensor_copy(xb_g[:, :], src)
                else:
                    _w(v, S["sx"])
                _w(v, S["pe_v"])
                _w(v, S["pe_gate"])
                inst = v.tensor_tensor(xhx[:, :], xb_g[:, :], xb_g[:, :],
                                       ALU.mult)
                S["dv_xb"].inc(inst)

            @blk.tensor
            def _(t):
                _w(t, S["dv_xb"])
                if first:
                    _w(t, S["dv_init"])
                _w(t, S["dv_eqk"])
                _w(t, S["dv_row"])
                _w(t, S["dv_wadd"])
                _w(t, S["dv_bld"])
                _w(t, S["dv_evy"])
                for dk in range(DK):
                    t.matmul(ps_t0[0:128, 0:512], onesmat[:, :],
                             xb_g[:, dk * 512:(dk + 1) * 512],
                             start=(dk == 0), stop=(dk == DK - 1))
                for dk in range(DK):
                    inst = t.matmul(ps_t1[0:128, 0:512], onesmat[:, :],
                                    xhx[:, dk * 512:(dk + 1) * 512],
                                    start=(dk == 0), stop=(dk == DK - 1))
                S["pe_stat"].inc(inst)

            @blk.vector
            def _(v):
                _w(v, S["pe_stat"])
                v.tensor_scalar(mb_s[:, :], ps_t0[0:128, 0:512],
                                1.0 / D, None, ALU.mult)
                v.tensor_tensor(var_s[:, :], mb_s[:, :], mb_s[:, :], ALU.mult)
                inst = v.scalar_tensor_tensor(var_s[:, :], ps_t1[0:128, 0:512],
                                              1.0 / D, var_s[:, :],
                                              ALU.mult, ALU.subtract)
                S["dv_row"].inc(inst)

            @blk.scalar
            def _(s):
                _w(s, S["dv_row"])
                _w(s, S["dv_init"])
                inst = s.activation(var_s[:, :], var_s[:, :], AF.Sqrt,
                                    bias=cvals[:, 0:1], scale=1.0)
                S["ac_row"].inc(inst)

            @blk.vector
            def _(v):
                _w(v, S["ac_row"])
                v.reciprocal(var_s[:, :], var_s[:, :])
                mb_b = bass.AP(mb_s, 0, [[512, 128], [0, DK], [1, 512]])
                rb_b = bass.AP(var_s, 0, [[512, 128], [0, DK], [1, 512]])
                xb3 = bass.AP(xb_g, 0, [[DK * 512, 128], [512, DK], [1, 512]])
                if from_res:
                    _w(v, S["pe_wout"])
                    xh3a = xhat2_ap3(tok0)
                    t3 = bass.AP(xhx, 0, [[DK * 512, 128], [512, DK], [1, 512]])
                    v.tensor_tensor(t3, xb3, mb_b, ALU.subtract)
                    ta = bass.AP(xhx, 0, [[DK * 512, 128], [512, 4], [1, 512]])
                    tb = bass.AP(xhx, 4 * 512, [[DK * 512, 128], [512, 4], [1, 512]])
                    ra = bass.AP(var_s, 0, [[512, 128], [0, 4], [1, 512]])
                    v.tensor_tensor(xh3a[0], ta, ra, ALU.mult)
                    inst = v.tensor_tensor(xh3a[1], tb, ra, ALU.mult)
                else:
                    xh3 = bass.AP(xhx, 0, [[DK * 512, 128], [512, DK], [1, 512]])
                    v.tensor_tensor(xh3, xb3, mb_b, ALU.subtract)
                    inst = v.tensor_tensor(xh3, xh3, rb_b, ALU.mult)
                S["dv_xh"].inc(inst)

        def xhat2_ap(dk, c0, w):
            tt = wv_sb if dk < 4 else wo_sb
            o = (dk % 4) * T + c0
            return tt[:, o:o + w]

        def xhat2_ap3(tok0):
            a = bass.AP(wv_sb, tok0, [[4 * T, 128], [T, 4], [1, 512]])
            b = bass.AP(wo_sb, tok0, [[4 * T, 128], [T, 4], [1, 512]])
            return a, b

        def attn_chunk(kind, tokabs, cidx):
            is_own = kind == "own"
            if kind == "pre":
                jm, mct, mcp = tokabs // 128, mcol, NCH
                mc = mcol[:, jm:jm + 1]
                mcb = mcol_bf[:, jm:jm + 1]
            else:
                jm, mct, mcp = 0, onescol, 1
                mc = onescol[:, 0:1]
                mcb = onescol_bf[:, 0:1]

            def xh(dk):
                c = dk * 512 + cidx * 128
                return xhx[:, c:c + 128]

            def wk_ap(dk):
                if kind == "enc":
                    return wke_ap(dk)
                return wk_sb[:, dk * 256:(dk + 1) * 256]

            def wv_ap(dk, half):
                if kind == "enc":
                    return wve_ap(dk, half)
                return wv_sb[:, dk * 1024 + half * 512:dk * 1024 + half * 512 + 512]

            @blk.tensor
            def _(t):
                _w(t, S["dv_xh"])
                _w(t, S["dv_eqk"])
                _w(t, S["dv_wadd"])
                _w(t, S["dv_row"])
                for dk in range(DK):
                    t.matmul(ps_t0[0:128, 0:256], xh(dk), wk_ap(dk),
                             start=(dk == 0), stop=(dk == DK - 1))
                if is_own:
                    for dk in range(DK):
                        t.matmul(ps_t0[0:128, 256:512], xh(dk),
                                 wq_sb[:, dk * 256:(dk + 1) * 256],
                                 start=(dk == 0), stop=(dk == DK - 1))
                _w(t, S["dv_bld"])
                _w(t, S["dv_evy"])
                for dk in range(DK):
                    inst = t.matmul(ps_t1[0:128, 0:512], xh(dk), wv_ap(dk, 0),
                                    start=(dk == 0), stop=(dk == DK - 1))
                S["pe_proj"].inc(inst)

            @blk.vector
            def _(v):
                _w(v, S["pe_proj"])
                _w(v, S["pe_state"])
                _w(v, S["pe_y"])
                _w(v, S["sx"], 48)
                v.tensor_copy(ktok[:, :], ps_t0[0:128, 0:256])
                if is_own:
                    v.tensor_copy(bass.AP(qtok_t, qtok_o, [[4096, 128], [1, 256]]),
                                  ps_t0[0:128, 256:512])
                ksrc = bass.AP(ktok, 0, [[256, 128], [16, H], [1, 16]])
                v.tensor_copy(bass.AP(kk, 0, [[H * 32, 128], [32, H], [1, 16]]),
                              ksrc)
                v.tensor_copy(bass.AP(kk, 16, [[H * 32, 128], [32, H], [1, 16]]),
                              ksrc)
                if is_own:
                    qsrc = bass.AP(qtok_t, qtok_o, [[4096, 128], [16, H], [1, 16]])
                    v.tensor_copy(
                        bass.AP(qq, 0, [[H * 32, 128], [32, H], [1, 16]]), qsrc)
                    v.tensor_copy(
                        bass.AP(qq, 16, [[H * 32, 128], [32, H], [1, 16]]), qsrc)
                dstA = bass.AP(vp, 0, [[H * 65, 128], [65, 8], [1, 64]])
                srcA = bass.AP(ps_t1, 0, [[512, 128], [64, 8], [1, 64]])
                v.tensor_scalar(dstA, srcA, mc, None, ALU.mult)
                dstO = bass.AP(vp, 64, [[H * 65, 128], [65, 8], [1, 1]])
                srcO = bass.AP(mct, jm, [[mcp, 128], [0, 8], [1, 1]])
                inst = v.tensor_copy(dstO, srcO)
                S["dv_eqk"].inc(inst)

            @blk.tensor
            def _(t):
                _w(t, S["dv_eqk"])
                for dk in range(DK):
                    inst = t.matmul(ps_t1[0:128, 0:512], xh(dk), wv_ap(dk, 1),
                                    start=(dk == 0), stop=(dk == DK - 1))
                S["pe_v"].inc(inst)

            @blk.vector
            def _(v):
                _w(v, S["pe_v"])
                _w(v, S["pe_tr"])
                dstB = bass.AP(vp, 8 * 65, [[H * 65, 128], [65, 8], [1, 64]])
                srcB = bass.AP(ps_t1, 0, [[512, 128], [64, 8], [1, 64]])
                v.tensor_scalar(dstB, srcB, mc, None, ALU.mult)
                dstO = bass.AP(vp, 8 * 65 + 64, [[H * 65, 128], [65, 8], [1, 1]])
                srcO = bass.AP(mct, jm, [[mcp, 128], [0, 8], [1, 1]])
                v.tensor_copy(dstO, srcO)

                def build(pt_tt, pt_oo, rtile, dup, tok_ap):
                    for dd in range(1, 8):
                        i0 = bass.AP(dup, 0, [[H * 32, 128], [32, H], [1, 16]])
                        i1 = bass.AP(dup, dd, [[H * 32, 128], [32, H], [1, 16]])
                        oo = bass.AP(pt_tt, pt_oo + (dd - 1) * 16,
                                     [[4096 if pt_tt in (wu3[1],) else H * NPAIR, 128],
                                      [NPAIR, H], [1, 16]])
                        v.tensor_tensor(oo, i0, i1, ALU.mult)
                    i0 = bass.AP(dup, 0, [[H * 32, 128], [32, H], [1, 8]])
                    i1 = bass.AP(dup, 8, [[H * 32, 128], [32, H], [1, 8]])
                    oo = bass.AP(pt_tt, pt_oo + 112,
                                 [[4096 if pt_tt in (wu3[1],) else H * NPAIR, 128],
                                  [NPAIR, H], [1, 8]])
                    v.tensor_tensor(oo, i0, i1, ALU.mult)
                    src = bass.AP(tok_ap[0], tok_ap[1],
                                  [[tok_ap[2], 128], [32, 8], [16, 2], [1, 16]])
                    dl = bass.AP(rtile, 0,
                                 [[8 * 128, 128], [128, 8], [64, 2], [1, 16]])
                    v.tensor_copy(dl, src)
                    dd2 = bass.AP(rtile, 32,
                                  [[8 * 128, 128], [128, 8], [64, 2], [1, 16]])
                    return v.tensor_tensor(dd2, src, src, ALU.mult)

                inst = build(kp_tok_t, kp_tok_o, kr_tok, kk, (ktok, 0, 256))
                dm = bass.AP(kr_tok, 48, [[8 * 128, 128], [128, 8], [64, 2], [1, 1]])
                sm = bass.AP(mct, jm, [[mcp, 128], [0, 8], [0, 2], [1, 1]])
                inst = v.tensor_copy(dm, sm)
                if is_own:
                    inst = build(qp_tok_t, qp_tok_o, qr_tok, qq,
                                 (qtok_t, qtok_o, 4096))
                S["dv_bld"].inc(inst)

            if is_own and OWNLVL >= 1:
                @blk.vector
                def _(v):
                    _w(v, S["pe_state"])
                    _w(v, S["pe_y"])
                    v.tensor_scalar(sbp[0:120, 0:512], ps_s[0][0:120, 0:512],
                                    1.0 / 16.0, None, ALU.mult)
                    v.tensor_scalar(sbp[0:120, 512:1024], ps_s[1][0:120, 0:512],
                                    1.0 / 16.0, None, ALU.mult)
                    v.tensor_scalar(sbpd[0:120, 0:16], ps_s[2][0:120, 455:471],
                                    1.0 / 16.0, None, ALU.mult)
                    ev_out = bass.AP(sbr, 0, [[H * 65, RW], [130, 7], [1, 65]])
                    ev_in = ps_s[2][0:RW, 0:455].rearrange(
                        "p (r c) -> p r c", r=7)
                    v.tensor_scalar(ev_out, ev_in, scol[0:RW, 0:1], None,
                                    ALU.mult)
                    od_out = bass.AP(sbr, 65, [[H * 65, RW], [130, 7], [1, 65]])
                    od_in = ps_s[2][64:64 + RW, 0:455].rearrange(
                        "p (r c) -> p r c", r=7)
                    v.tensor_scalar(od_out, od_in, scol[0:RW, 0:1], None,
                                    ALU.mult)
                    v.tensor_scalar(sbr[0:RW, 14 * 65:14 * 65 + 65],
                                    ps_s[3][0:RW, 0:65],
                                    scol[0:RW, 0:1], None, ALU.mult)
                    inst = v.tensor_scalar(sbr[0:RW, 15 * 65:15 * 65 + 65],
                                           ps_s[3][64:64 + RW, 0:65],
                                           scol[0:RW, 0:1], None, ALU.mult)
                    S["dv_cast"].inc(inst)

                pass
            if is_own and OWNLVL >= 2:
                def rto(hh):
                    return (hh // 2) * 128 + (hh % 2) * 64

                tr_specs = [
                    ([(qp_tok_t, qp_tok_o + hh * NPAIR, NPAIR) for hh in range(8)],
                     (qp_feat_t, qp_feat_o + 0), NPAIR),
                    ([(qp_tok_t, qp_tok_o + (8 + hh) * NPAIR, NPAIR)
                      for hh in range(8)],
                     (qp_feat_t, qp_feat_o + 1024), NPAIR),
                    ([(qr_tok, rto(hh), RW) for hh in range(8)],
                     (qr_feat, 0), RW),
                    ([(qr_tok, rto(8 + hh), RW) for hh in range(8)],
                     (qr_feat, 1024), RW),
                    ([(kr_tok, rto(hh), RW) for hh in range(8)],
                     (kr_feat, 0), RW),
                    ([(kr_tok, rto(8 + hh), RW) for hh in range(8)],
                     (kr_feat, 1024), RW),
                ]
                for wv_i in range(6):
                    srcs_w, (dt_t, dt_o), rows = tr_specs[wv_i]

                    @blk.tensor
                    def _(t, srcs_w=srcs_w, rows=rows):
                        _w(t, S["dv_bld"])
                        _w(t, S["dv_evtr"])
                        for i8, (tt, o, wdt) in enumerate(srcs_w):
                            src = bass.AP(tt, o, [[tt.shape[1], 128], [1, wdt]])
                            inst = t.transpose(
                                ps_tr[0:rows, i8 * 128:(i8 + 1) * 128],
                                src, ident[:, :])
                        S["pe_tr"].inc(inst)

                    @blk.vector
                    def _(v, dt_t=dt_t, dt_o=dt_o, rows=rows):
                        _w(v, S["pe_tr"])
                        _w(v, S["pe_g"])
                        _w(v, S["pe_y"])
                        dst = bass.AP(dt_t, dt_o, [[dt_t.shape[1], rows], [1, 1024]])
                        inst = v.tensor_copy(dst, ps_tr[0:rows, 0:1024])
                        S["dv_evtr"].inc(inst)

                pass
            if is_own and OWNLVL >= 3:
                for wv_i in range(4):
                    @blk.tensor
                    def _(t, wv_i=wv_i):
                        _w(t, S["dv_evtr"])
                        _w(t, S["dv_f"])
                        for hh in range(4 * wv_i, 4 * wv_i + 4):
                            inst = t.matmul(
                                ps_t0[0:128, (hh % 4) * 128:(hh % 4) * 128 + 128],
                                kr_feat[0:16, hh * 128:(hh + 1) * 128],
                                qr_feat[0:16, hh * 128:(hh + 1) * 128],
                                start=True, stop=True)
                        S["pe_g"].inc(inst)

                    @blk.vector
                    def _(v, wv_i=wv_i):
                        _w(v, S["pe_g"])
                        _w(v, S["pe_y"])
                        fts = mb_s if wv_i % 2 == 0 else var_s
                        par = (wv_i % 2) * 512
                        v.tensor_scalar(fts[:, :], ps_t0[0:128, 0:512],
                                        0.03125, 0.25, ALU.mult, ALU.add)
                        v.scalar_tensor_tensor(ftmp[:, par:par + 512],
                                               ps_t0[0:128, 0:512], 1.0,
                                               fts[:, :], ALU.mult, ALU.mult)
                        trb = bass.AP(tril, 0, [[128, 128], [0, 4], [1, 128]])
                        i0 = bass.AP(ftmp, par, [[1024, 128], [128, 4], [1, 128]])
                        oo = bass.AP(pt_t, pt_o + wv_i * 512,
                                     [[4096, 128], [128, 4], [1, 128]])
                        inst = v.scalar_tensor_tensor(oo, i0, 1.0, trb,
                                                      ALU.add, ALU.mult)
                        S["dv_f"].inc(inst)

                    if OWNLVL < 4:
                        continue

                    @blk.tensor
                    def _(t, wv_i=wv_i):
                        _w(t, S["dv_f"])
                        _w(t, S["dv_cast"])
                        _w(t, S["dv_bld"])
                        _w(t, S["dv_evy"])
                        for hh in range(4 * wv_i, 4 * wv_i + 4):
                            fo = (hh % 4) * 128
                            t.matmul(ps_t1[0:65, fo:fo + 128],
                                     vp[:, hh * 65:hh * 65 + 65],
                                     bass.AP(pt_t, pt_o + hh * 128,
                                             [[4096, 128], [1, 128]]),
                                     start=True, stop=False)
                            t.matmul(ps_t1[0:64, fo:fo + 128],
                                     sbp[0:120, hh * 64:(hh + 1) * 64],
                                     bass.AP(qp_feat_t, qp_feat_o + hh * 128,
                                             [[4096, 120], [1, 128]]),
                                     start=False, stop=False)
                            t.matmul(ps_t1[64:65, fo:fo + 128],
                                     sbpd[0:120, hh:hh + 1],
                                     bass.AP(qp_feat_t, qp_feat_o + hh * 128,
                                             [[4096, 120], [1, 128]]),
                                     start=False, stop=False)
                            inst = t.matmul(
                                ps_t1[0:65, fo:fo + 128],
                                sbr[0:RW, hh * 65:hh * 65 + 65],
                                qr_feat[0:RW, hh * 128:(hh + 1) * 128],
                                start=False, stop=True)
                        S["pe_y"].inc(inst)

                    if OWNLVL < 5:
                        continue

                    @blk.vector
                    def _(v, wv_i=wv_i):
                        _w(v, S["pe_y"])
                        _w(v, S["pe_zb"])
                        v.reciprocal(ztmp[0:1, 0:512], ps_t1[64:65, 0:512])
                        inst = v.tensor_copy(zr[0:1, 0:512], ztmp[0:1, 0:512])
                        S["dv_z"].inc(inst)

                    @blk.tensor
                    def _(t, wv_i=wv_i):
                        _w(t, S["dv_z"])
                        _w(t, S["dv_evy"])
                        _w(t, S["dv_wadd"])
                        inst = t.matmul(ps_t2[0:128, 0:512], ones_rw[0:1, 0:128],
                                        zr[0:1, 0:512], start=True, stop=True)
                        S["pe_zb"].inc(inst)

                    @blk.vector
                    def _(v, wv_i=wv_i):
                        _w(v, S["pe_zb"])
                        _w(v, S["pe_wout"])
                        v.tensor_copy(zb_sb[:, :], ps_t2[0:128, 0:512])
                        inst = None
                        for hh in range(4 * wv_i, 4 * wv_i + 4):
                            p, rowb = hh // 2, (hh % 2) * 64
                            fo = (hh % 4) * 128
                            inst = v.tensor_tensor(
                                yp[rowb:rowb + 64, p * 128:(p + 1) * 128],
                                ps_t1[0:64, fo:fo + 128],
                                zb_sb[rowb:rowb + 64, fo:fo + 128], ALU.mult)
                        S["dv_evy"].inc(inst)

                pass
            if is_own and OWNLVL >= 6:
                @blk.tensor
                def _(t):
                    _w(t, S["dv_evy"])
                    _w(t, S["dv_f"])
                    for m in range(4):
                        for k in range(8):
                            inst = t.matmul(
                                ps_t0[0:128, m * 128:(m + 1) * 128],
                                wo_sb[:, k * 1024 + m * 128:k * 1024 + m * 128 + 128],
                                yp[:, k * 128:(k + 1) * 128],
                                start=(k == 0), stop=(k == 7))
                    for m in range(4, 8):
                        for k in range(8):
                            inst = t.matmul(
                                ps_t2[0:128, (m - 4) * 128:(m - 3) * 128],
                                wo_sb[:, k * 1024 + m * 128:k * 1024 + m * 128 + 128],
                                yp[:, k * 128:(k + 1) * 128],
                                start=(k == 0), stop=(k == 7))
                    S["pe_wout"].inc(inst)

                @blk.vector
                def _(v):
                    _w(v, S["pe_wout"])
                    _w(v, S["sx"], 48)
                    oc = tokabs - PRE
                    o1 = bass.AP(x_res, oc, [[DK * T, 128], [T, 4], [1, 128]])
                    v.tensor_tensor(o1, o1,
                                    bass.AP(ps_t0, 0, [[512, 128], [128, 4], [1, 128]]),
                                    ALU.add)
                    o2 = bass.AP(x_res, 4 * T + oc, [[DK * T, 128], [T, 4], [1, 128]])
                    inst = v.tensor_tensor(
                        o2, o2,
                        bass.AP(ps_t2, 0, [[512, 128], [128, 4], [1, 128]]),
                        ALU.add)
                    S["dv_wadd"].inc(inst)

            @blk.tensor
            def _(t):
                _w(t, S["dv_bld"])
                _w(t, S["dv_init"])
                if is_own:
                    _w(t, S["dv_cast"])
                inst = None
                for hh in range(H):
                    bank, off = pairs_v_rg(hh)
                    t.matmul(bank[0:NPAIR, off:off + 64],
                             bass.AP(kp_tok_t, kp_tok_o + hh * NPAIR,
                                     [[4096, 128], [1, NPAIR]]),
                             vp[:, hh * 65:hh * 65 + 64],
                             start=False, stop=True)
                    t.matmul(ps_s[2][0:NPAIR, 455 + hh:456 + hh],
                             bass.AP(kp_tok_t, kp_tok_o + hh * NPAIR,
                                     [[4096, 128], [1, NPAIR]]),
                             mcb, start=False, stop=True)
                for p in range(8):
                    bank, off = rest_rg(p)
                    for half in range(2):
                        hh = 2 * p + half
                        base = half * 64
                        inst = t.matmul(
                            bank[base:base + RW, off:off + 65],
                            kr_tok[:, p * 128 + base:p * 128 + base + RW],
                            vp[:, hh * 65:hh * 65 + 65],
                            start=False, stop=True)
                S["pe_state"].inc(inst)

        OWNLVL = int(os.environ.get("KMK_OWNLVL", "6"))
        PHASE = os.environ.get("KMK_PHASE", "full")
        NGRP = int(os.environ.get("KMK_GROUPS", "17"))
        DOCONV = os.environ.get("KMK_CONV", "1") == "1" and PHASE != "attn_noconv"
        groups = [("enc", 0)]
        groups += [("pre", 512 * g) for g in range(12)]
        groups += [("own", PRE + 512 * g) for g in range(4)]
        groups = groups[:NGRP]
        first = True
        for gi, (kind, tok0) in enumerate(groups):
            ln_group(tok0, from_res=False, first=first, gidx=gi)
            first = False
            for cidx in range(4):
                attn_chunk(kind, tok0 + cidx * 128, cidx)
            if kind != "enc" and DOCONV:
                emit_conv(4)
        if DOCONV:
            emit_conv(NCONV)
        if PHASE in ("attn", "attn_noconv"):
            @blk.sync
            def _(sy):
                _w(sy, S["dv_wadd"])
                if S["ssc"].n > 0:
                    _w(sy, S["ssc"])
                src = bass.AP(x_res, 0, [[DK * T, 128], [T, DK], [1, T]])
                dst = bass.AP(out_d, 0, [[T, 128], [128 * T, DK], [1, T]])
                sy.dma_start(dst, src).then_inc(S["sst"].h, 16)
                S["sst"].n += 16
                sy.wait_ge(S["sst"].h, S["sst"].n)

            @blk.tensor
            def _(t):
                _w(t, S["pe_state"])

            @blk.vector
            def _(v):
                _w(v, S["dv_bld"])
            return nc

        @blk.gpsimd
        def _(g):
            g.wait_ge(S["pe_state"].h, 68)
            g.wait_ge(S["pe_wout"].h, 16)
            g.dma_start(yp[0:17, 0:D], b2_d.ap()).then_inc(S["swl"].h, 16)
            S["swl"].n += 16
            g.memset(qr_feat[0:32, 0:2048], 1.0)
            g.affine_select(
                bass.AP(qr_feat, 0, [[H * 128, 32], [128, 16], [1, 128]]),
                bass.AP(qr_feat, 0, [[H * 128, 32], [128, 16], [1, 128]]),
                [[128, 16], [0, 128]], ALU.is_equal, 0.0,
                base=0, channel_multiplier=-128)
            g.memset(sbr[0:32, 0:256], 1.0)
            inst = g.affine_select(
                bass.AP(sbr, 0, [[H * 65, 32], [128, 2], [1, 128]]),
                bass.AP(sbr, 0, [[H * 65, 32], [128, 2], [1, 128]]),
                [[128, 2], [0, 128]], ALU.is_equal, 0.0,
                base=16 * 128, channel_multiplier=-128)
            S["gp_sel"].inc(inst)

        def gate_logits(ch):
            par = (ch % 2) * 16

            @blk.tensor
            def _(t):
                _w(t, S["dv_xh"])
                _w(t, S["dv_gate"])
                for dk in range(DK):
                    inst = t.matmul(ps_t0[0:128, par:par + 16],
                                    xhat2_ap(dk, ch * 128, 128),
                                    gw_sb[:, dk * 16:(dk + 1) * 16],
                                    start=(dk == 0), stop=(dk == DK - 1))
                S["pe_gate"].inc(inst)

            @blk.vector
            def _(v):
                _w(v, S["pe_gate"])
                inst = v.tensor_copy(glog[:, ch * 16:(ch + 1) * 16],
                                     ps_t0[0:128, par:par + 16])
                S["dv_gate"].inc(inst)

        for g4 in range(4):
            ln_group(g4 * 512, from_res=True)
            for ci in range(4):
                gate_logits(g4 * 4 + ci)

        @blk.scalar
        def _(s_):
            _w(s_, S["dv_gate"])
            inst = s_.activation(gt1[:, :], glog[:, :], AF.Sigmoid)
            S["ac_silu"].inc(inst)

        @blk.vector
        def _(v):
            _w(v, S["ac_silu"])
            ebb = bass.AP(eb16, 0, [[16, 128], [0, 16], [1, 16]])
            g3 = lambda tt: bass.AP(tt, 0, [[256, 128], [16, 16], [1, 16]])
            v.tensor_tensor(g3(gt0), g3(glog), ebb, ALU.add)
            v.drain()
            v.reduce_max(bass.AP(gm, 0, [[64, 128], [1, 16], [0, 1]]),
                         g3(gt0), axis=AXX)
            v.drain()
            m1b = bass.AP(gm, 0, [[64, 128], [1, 16], [0, 16]])
            v.tensor_tensor(g3(gt2), g3(gt0), m1b, ALU.is_ge)
            v.drain()
            v.scalar_tensor_tensor(gt2[:, :], gt2[:, :], NEG_BIG, gt0[:, :],
                                   ALU.mult, ALU.add)
            v.drain()
            v.reduce_max(bass.AP(gm, 16, [[64, 128], [1, 16], [0, 1]]),
                         g3(gt2), axis=AXX)
            v.drain()
            m2b = bass.AP(gm, 16, [[64, 128], [1, 16], [0, 16]])
            v.tensor_tensor(g3(gt2), g3(gt0), m2b, ALU.is_ge)
            v.drain()
            v.tensor_tensor(gt1[:, :], gt1[:, :], gt2[:, :], ALU.mult)
            v.drain()
            v.reduce_sum(bass.AP(gm, 32, [[64, 128], [1, 16], [0, 1]]),
                         g3(gt1), axis=AXX)
            v.drain()
            v.tensor_scalar(gm[:, 48:64], gm[:, 32:48], GATE_EPS, None, ALU.add)
            v.drain()
            v.reciprocal(gm[:, 32:48], gm[:, 48:64])
            v.drain()
            rb2 = bass.AP(gm, 32, [[64, 128], [1, 16], [0, 16]])
            inst = v.tensor_tensor(
                bass.AP(gchb_t, gchb_o, [[gchb_t.shape[1], 128], [16, 16], [1, 16]]),
                g3(gt1), rb2, ALU.mult)
            v.drain()
            S["dv_gate"].inc(inst)

        for ch in range(16):
            @blk.tensor
            def _(t, ch=ch):
                _w(t, S["dv_gate"])
                _w(t, S["dv_evtr"])
                inst = t.transpose(
                    ps_tr[0:16, (ch % 8) * 128:(ch % 8) * 128 + 128],
                    bass.AP(gchb_t, gchb_o + ch * 16,
                            [[gchb_t.shape[1], 128], [1, 16]]),
                    ident[:, :])
                S["pe_tr"].inc(inst)

            @blk.vector
            def _(v, ch=ch):
                _w(v, S["pe_tr"])
                inst = v.tensor_copy(gwt[0:16, ch * 128:(ch + 1) * 128],
                                     ps_tr[0:16, (ch % 8) * 128:(ch % 8) * 128 + 128])
                S["dv_evtr"].inc(inst)

        if PHASE == "ln2":
            @blk.vector
            def _(v):
                _w(v, S["dv_gate"])
                _w(v, S["dv_xh"])
                _w(v, S["gp_sel"])
                _w(v, S["dv_evtr"])
                inst = None
                if os.environ.get("KMK_DUMP", "xhat2") == "gwt":
                    inst = v.tensor_copy(x_res[0:32, 0:T], gwt[0:32, 0:T])
                    inst = v.tensor_copy(x_res[0:32, 3 * T:3 * T + 2048],
                                         qr_feat[0:32, 0:2048])
                    inst = v.tensor_copy(
                        x_res[0:128, 2 * T:2 * T + 256],
                        bass.AP(gchb_t, gchb_o, [[gchb_t.shape[1], 128], [1, 256]]))
                else:
                    for dk in range(DK):
                        inst = v.tensor_copy(x_res[:, dk * T:(dk + 1) * T],
                                             xhat2_ap(dk, 0, T))
                S["dv_oadd"].inc(inst)

            @blk.sync
            def _(sy):
                _w(sy, S["dv_oadd"])
                _w(sy, S["ssc"])
                src = bass.AP(x_res, 0, [[DK * T, 128], [T, DK], [1, T]])
                dst = bass.AP(out_d, 0, [[T, 128], [128 * T, DK], [1, T]])
                sy.dma_start(dst, src).then_inc(S["sst"].h, 16)
                S["sst"].n += 16
                sy.wait_ge(S["sst"].h, S["sst"].n)
            return nc

        for fblk in range(8):
            @blk.vector
            def _(v):
                _w(v, S["pe_w2"])
                _w(v, S["pe_state"])
                _w(v, S["pe_y"])
                inst = None
                for i in range(4):
                    inst = v.memset(ps_s[i][:, :], 0.0)
                S["dv_ms"].inc(inst)

            @blk.tensor
            def _(t):
                _w(t, S["dv_ms"])
                _w(t, S["dv_gate"])
                _w(t, S["swl"])
                inst = None
                for dk in range(DK):
                    bank, colo = ps_s[dk // 2], (dk % 2) * 256
                    inst = t.matmul(bank[0:128, colo:colo + 256],
                                     b2_sb[0:17, dk * 128:(dk + 1) * 128],
                                     gwt[0:17, fblk * 256:(fblk + 1) * 256],
                                     start=False, stop=True)
                S["pe_misc"].inc(inst)

            for u in range(E_UNITS):
                ui = fblk * E_UNITS + u
                par = ui % 2

                @blk.sync
                def _(sy):
                    if ui >= 2:
                        sy.wait_ge(S["pe_w2"].h, ui - 1)
                    else:
                        sy.wait_ge(S["ssc"].h, NCONV * 16)
                        _w(sy, S["dv_gate"])
                        _w(sy, S["dv_xb"])
                    if fblk == 0:
                        sy.wait_ge(S["ssc"].h, min(NCONV, 3 * (u + 1)) * 16)
                    for j, buf in enumerate([wu1[par], wu3[par], wu2[par]]):
                        src = bass.AP(wsc, (u * 3 + j) * 128 * 4096,
                                      [[4096, 128], [1, 4096]])
                        sy.dma_start(buf[:, :], src).then_inc(S["sfu"].h, 16)
                        S["sfu"].n += 16

                @blk.tensor
                def _(t):
                    _w(t, S["sfu"])
                    _w(t, S["dv_gb"])
                    _w(t, S["dv_xh"])
                    _w(t, S["dv_ms"])
                    _w(t, S["gp_sel"])
                    inst = t.matmul(ps_t0[0:128, 0:256], sel_ap(u),
                                    gwt[0:32, fblk * 256:(fblk + 1) * 256],
                                    start=True, stop=True)
                    S["pe_gb"].inc(inst)

                @blk.vector
                def _(v):
                    _w(v, S["pe_gb"])
                    _w(v, S["dv_hev"])
                    inst = v.tensor_copy(bass.AP(gb_t, gb_o, [[256, 128], [1, 256]]),
                                         ps_t0[0:128, 0:256])
                    S["dv_gb"].inc(inst)

                rot = [ps_t1, ps_t2, ps_t1, ps_t2]

                def emit_ab(ci):
                    @blk.tensor
                    def _(t):
                        _w(t, S["dv_hev"], max(0, S["dv_hev"].n - 1))
                        bank = rot[ci]
                        for dk in range(DK):
                            t.matmul(
                                bank[0:128, 0:256],
                                wu1[par][:, dk * 512 + ci * 128:dk * 512 + ci * 128 + 128],
                                xhat2_ap(dk, fblk * 256, 256),
                                start=(dk == 0), stop=(dk == DK - 1))
                        for dk in range(DK):
                            inst = t.matmul(
                                bank[0:128, 256:512],
                                wu3[par][:, dk * 512 + ci * 128:dk * 512 + ci * 128 + 128],
                                xhat2_ap(dk, fblk * 256, 256),
                                start=(dk == 0), stop=(dk == DK - 1))
                        S["pe_ab"].inc(inst)

                    @blk.vector
                    def _(v):
                        v.wait_ge(S["ac_silu"].h, ac_base + ci + 1)
                        _w(v, S["pe_w2"])
                        v.scalar_tensor_tensor(
                            tb_t[:, ci * 256:(ci + 1) * 256],
                            rot[ci][0:128, 256:512],
                            b3c[:, u * 4 + ci:u * 4 + ci + 1],
                            bass.AP(gb_t, gb_o, [[256, 128], [1, 256]]),
                            ALU.add, ALU.mult)
                        inst = v.tensor_tensor(
                            h_t[:, ci * 256:(ci + 1) * 256],
                            tb_t[:, ci * 256:(ci + 1) * 256],
                            sa_t[:, ci * 256:(ci + 1) * 256], ALU.mult)
                        S["dv_hev"].inc(inst)

                def emit_w2(ci):
                    @blk.tensor
                    def _(t):
                        _w(t, S["dv_hev"],
                           S["dv_hev"].n - (1 if ci < 3 else 0))
                        inst = None
                        for dk in range(DK):
                            bank, colo = ps_s[dk // 2], (dk % 2) * 256
                            inst = t.matmul(
                                bank[0:128, colo:colo + 256],
                                wu2[par][:, ci * 1024 + dk * 128:ci * 1024 + dk * 128 + 128],
                                h_t[:, ci * 256:(ci + 1) * 256],
                                start=False, stop=True)
                        if ci == 3:
                            S["pe_w2"].inc(inst)
                        else:
                            S["pe_misc"].inc(inst)

                ab_base = S["pe_ab"].n
                hev_base = S["dv_hev"].n
                ac_base = S["ac_silu"].n

                @blk.scalar
                def _(s_):
                    for ci in range(4):
                        s_.wait_ge(S["pe_ab"].h, ab_base + ci + 1)
                        if hev_base + ci - 1 > 0:
                            s_.wait_ge(S["dv_hev"].h, hev_base + ci - 1)
                        inst = s_.activation(
                            sa_t[:, ci * 256:(ci + 1) * 256],
                            rot[ci][0:128, 0:256], AF.Silu,
                            bias=b1c[:, u * 4 + ci:u * 4 + ci + 1], scale=1.0)
                        S["ac_silu"].inc(inst)

                emit_ab(0)
                emit_ab(1)
                emit_w2(0)
                emit_ab(2)
                emit_w2(1)
                emit_ab(3)
                emit_w2(2)
                emit_w2(3)

            @blk.vector
            def _(v):
                _w(v, S["pe_w2"])
                _w(v, S["pe_misc"])
                inst = None
                for i in range(4):
                    oo = bass.AP(x_res, 2 * i * T + fblk * 256,
                                 [[DK * T, 128], [T, 2], [1, 256]])
                    inst = v.tensor_tensor(
                        oo, oo,
                        bass.AP(ps_s[i], 0, [[512, 128], [256, 2], [1, 256]]),
                        ALU.add)
                S["dv_oadd"].inc(inst)

        @blk.sync
        def _(sy):
            _w(sy, S["dv_oadd"])
            src = bass.AP(x_res, 0, [[DK * T, 128], [T, DK], [1, T]])
            dst = bass.AP(out_d, 0, [[T, 128], [128 * T, DK], [1, T]])
            sy.dma_start(dst, src).then_inc(S["sst"].h, 16)
            S["sst"].n += 16
            sy.wait_ge(S["sst"].h, S["sst"].n)

    return nc


_NC_CACHE = {}


def _get_nc():
    key = (os.environ.get("KMK_PHASE", "full"), os.environ.get("KMK_GROUPS", "17"),
           os.environ.get("KMK_CONV", "1"))
    if key not in _NC_CACHE:
        _NC_CACHE[key] = build_program()
    return _NC_CACHE[key]


def kernel(**inp):
    f32 = np.float32
    x = np.asarray(inp["L0_attn_x"], dtype=f32)
    wq = np.asarray(inp["L0_attn_Wq"], f32)
    wk = np.asarray(inp["L0_attn_Wk"], f32)
    wv = np.asarray(inp["L0_attn_Wv"], f32)
    wke = np.asarray(inp["L0_attn_Wk_enc"], f32)
    wve = np.asarray(inp["L0_attn_Wv_enc"], f32)
    wo = np.asarray(inp["L0_attn_Wout"], f32)
    gw = np.asarray(inp["L0_ffn_gate_w"], f32)
    eb = np.asarray(inp["L0_ffn_expert_bias"], f32)
    w1 = np.asarray(inp["L0_ffn_w1"], f32)
    b1 = np.asarray(inp["L0_ffn_b1"], f32)
    w2 = np.asarray(inp["L0_ffn_w2"], f32)
    b2 = np.asarray(inp["L0_ffn_b2"], f32)
    w3 = np.asarray(inp["L0_ffn_w3"], f32)
    b3 = np.asarray(inp["L0_ffn_b3"], f32)
    sw1 = np.asarray(inp["L0_ffn_sw1"], f32)
    sb1 = np.asarray(inp["L0_ffn_sb1"], f32)
    sw2 = np.asarray(inp["L0_ffn_sw2"], f32)
    sb2 = np.asarray(inp["L0_ffn_sb2"], f32)
    sw3 = np.asarray(inp["L0_ffn_sw3"], f32)
    sb3 = np.asarray(inp["L0_ffn_sb3"], f32)

    C = np.ascontiguousarray

    def tile_t(wt):
        ncols = wt.shape[1]
        return C(wt.reshape(DK, 128, ncols).transpose(1, 0, 2)
                 .reshape(128, DK * ncols))

    def tile_k(wt, kch):
        ncols = wt.shape[1]
        return wt.reshape(kch, 128, ncols).transpose(1, 0, 2).reshape(
            128, kch * ncols)

    wsrc = np.empty((54, 128, 4096), f32)
    i = 0
    for e in range(16):
        wsrc[i] = tile_k(w1[e].T, 8); i += 1
        wsrc[i] = tile_k(w3[e].T, 8); i += 1
        wsrc[i] = tile_k(w2[e].T, 4); i += 1
    for u_ in range(2):
        wsrc[i] = tile_k(sw1.T[:, u_ * 512:(u_ + 1) * 512], 8); i += 1
        wsrc[i] = tile_k(sw3.T[:, u_ * 512:(u_ + 1) * 512], 8); i += 1
        wsrc[i] = tile_k(sw2.T[u_ * 512:(u_ + 1) * 512], 4); i += 1
    shared = {
        "wqT": tile_t(C(wq.T)), "wkT": tile_t(C(wk.T)), "wvT": tile_t(C(wv.T)),
        "wkeT": tile_t(C(wke.T)), "wveT": tile_t(C(wve.T)),
        "woT": tile_t(C(wo.T)),
        "gwTm": tile_t(C(gw.T)), "ebr": C(eb[None, :]),
        "wsrc": C(wsrc.reshape(54 * 128, 4096)),
        "b2m": C(np.concatenate([b2, sb2[None, :]], axis=0)),
    }
    b1all = np.concatenate([b1, sb1.reshape(2, 512)], axis=0)
    b3all = np.concatenate([b3, sb3.reshape(2, 512)], axis=0)
    shared["b1c"] = C(b1all.reshape(E_UNITS, 4, 128).transpose(2, 0, 1)
                      .reshape(128, E_UNITS * 4))
    shared["b3c"] = C(b3all.reshape(E_UNITS, 4, 128).transpose(2, 0, 1)
                      .reshape(128, E_UNITS * 4))
    sc = np.zeros((128, 1), f32)
    sc[0:16] = 0.25
    sc[32:48] = 1.0 / 32.0
    sc[48] = 1.0
    sc[64:80] = 0.25
    sc[96:112] = 1.0 / 32.0
    sc[112] = 1.0
    shared["scol"] = sc

    g_tok0 = [0] + [512 * g for g in range(12)] + [PRE + 512 * g for g in range(4)]
    in_maps = []
    for core in range(8):
        b, c = core // 4, core % 4
        xb = x[b].T
        xTc = np.concatenate([xb[:, :PRE], xb[:, c * T:(c + 1) * T]], axis=1)
        xtt = tile_k(xTc, DK).reshape(128, DK, PRE + T)
        xgt = np.empty((128, 17, 4096), f32)
        for gi, t0 in enumerate(g_tok0):
            xgt[:, gi] = xtt[:, :, t0:t0 + 512].reshape(128, DK * 512)
        xot = C(xtt[:, :, PRE:].reshape(128, DK * T))
        m = np.zeros((128, NCH), f32)
        lim = c * T
        for j in range(NCH):
            base = j * 128
            if base + 128 <= lim:
                m[:, j] = 1.0
            elif base < lim:
                m[0:lim - base, j] = 1.0
        d = dict(shared)
        d["xgt"] = C(xgt.reshape(128, 17 * 4096))
        d["xot"] = xot
        d["mask"] = m
        in_maps.append(d)

    nc = _get_nc()
    res = run_bass_kernel_spmd(nc, in_maps, core_ids=list(range(8)))
    y = np.empty((B_, N_, D), f32)
    for core in range(8):
        b, c = core // 4, core % 4
        y[b, c * T:(c + 1) * T, :] = res.results[core]["out"].T
    return y
